# version 06
# baseline (speedup 1.0000x reference)
import contextlib
import numpy as np

import concourse.bass as bass
import concourse.mybir as mybir
from concourse.bass_utils import run_bass_kernel_spmd
from concourse.aot_env import allow_ici_all_to_all

F32 = mybir.dt.float32
BF16 = mybir.dt.bfloat16
AF = mybir.ActivationFunctionType
ALU = mybir.AluOpType
AX = mybir.AxisListType

D, N, BT, HL, DH, M, ITERS, KC = 1024, 4096, 4, 8, 64, 256, 6, 13
T, NE, NU, MI = 2048, 16, 20, 512
SCALE = 0.125

ENGS = ("tensor", "vector", "scalar", "gpsimd", "sync")
SEMS = ("pe", "vec", "act", "pool", "dsync", "dpool", "cc",
        "dx0", "dx1", "dw", "d3", "d4", "du0", "du1")


def ts(i, n):
    return slice(i * n, (i + 1) * n)


class MK:

    def __init__(self, nc, es, blk):
        self.nc, self.es, self.blk = nc, es, blk
        self.led = {s: 0 for s in SEMS}
        self.sem = {s: es.enter_context(nc.semaphore(s)) for s in SEMS}
        self.lw = {s: {} for s in SEMS}

    def go(self, eng, fn):
        getattr(self.blk, eng)(fn)

    def inc(self, instr, s, n=1):
        instr.then_inc(self.sem[s], n)
        self.led[s] += n
        return instr

    def wait(self, e, ename, s, val=None):
        v = self.led[s] if val is None else val
        if v <= 0 or self.lw[s].get(ename, -1) >= v:
            return
        e.wait_ge(self.sem[s], v)
        self.lw[s][ename] = v

    def join(self):
        snap = dict(self.led)
        for eng in ENGS:
            def f(e, eng=eng, snap=snap):
                for s in SEMS:
                    self.wait(e, eng, s, snap[s])
            self.go(eng, f)

    def sb(self, name, shape, dt):
        return self.es.enter_context(self.nc.sbuf_tensor(name, shape, dt))


def mm3(e, out_ap, lhs, rhs, ks, start=True, stop=True):
    seq = (("hi", "hi"), ("hi", "lo"), ("lo", "hi"))
    tot = len(seq) * len(ks)
    i = 0
    ii = None
    for lw, rw in seq:
        for k in ks:
            ii = e.matmul(out_ap, lhs(k, lw), rhs(k, rw),
                          start=(start and i == 0), stop=(stop and i == tot - 1))
            i += 1
    return ii


def mmf(e, outs, lhsT, rhs, start=True, stop=True):
    ii = None
    for ot in range(2):
        for k in range(2):
            ii = e.matmul(outs[ot], lhsT[:, k, ts(ot, 128)], rhs[:, k, :],
                          start=(start and k == 0), stop=(stop and k == 1))
    return ii


def build_program(dbg=(), stop=99):
    nc = bass.Bass("TRN2", target_bir_lowering=False, debug=False)
    es = contextlib.ExitStack()

    xT = nc.dram_tensor("xT", [D, N], F32, kind="ExternalInput").ap()
    xtm = nc.dram_tensor("xtm", [N, D], F32, kind="ExternalInput").ap()
    xres = nc.dram_tensor("xres", [T, D], F32, kind="ExternalInput").ap()
    wqkv = nc.dram_tensor("wqkv", [HL, D, 3 * DH], F32, kind="ExternalInput").ap()
    wcol = nc.dram_tensor("wcol", [8 * DH, KC], F32, kind="ExternalInput").ap()
    wfc = nc.dram_tensor("wfc", [D, D], F32, kind="ExternalInput").ap()
    bfc_rep = nc.dram_tensor("bfc_rep", [128, D], F32, kind="ExternalInput").ap()
    gate_wT = nc.dram_tensor("gate_wT", [D, NE], F32, kind="ExternalInput").ap()
    be_rep = nc.dram_tensor("be_rep", [128, NE], F32, kind="ExternalInput").ap()
    up_wT = nc.dram_tensor("up_wT", [NU, D, MI], F32, kind="ExternalInput").ap()
    down_wT = nc.dram_tensor("down_wT", [NU, MI, D], F32, kind="ExternalInput").ap()
    identf_d = nc.dram_tensor("identf", [128, 128], F32, kind="ExternalInput").ap()
    identb_d = nc.dram_tensor("identb", [128, 128], BF16, kind="ExternalInput").ap()
    yout = nc.dram_tensor("y", [T, D], F32, kind="ExternalOutput").ap()

    wq_hi_d = nc.dram_tensor("wq_hi_d", [HL, D, 3 * DH], BF16).ap()
    wq_lo_d = nc.dram_tensor("wq_lo_d", [HL, D, 3 * DH], BF16).ap()
    up_hi_d = nc.dram_tensor("up_hi_d", [NU, D, MI], BF16).ap()
    up_lo_d = nc.dram_tensor("up_lo_d", [NU, D, MI], BF16).ap()
    dn_hi_d = nc.dram_tensor("dn_hi_d", [NU, MI, D], BF16).ap()
    dn_lo_d = nc.dram_tensor("dn_lo_d", [NU, MI, D], BF16).ap()
    fc_hi_d = nc.dram_tensor("fc_hi_d", [D, D], BF16).ap()
    fc_lo_d = nc.dram_tensor("fc_lo_d", [D, D], BF16).ap()
    a2a_in = nc.dram_tensor("a2a_in", [8, 512, 512], F32)
    a2a_out = nc.dram_tensor("a2a_out", [8, 512, 512], F32)

    blk = es.enter_context(nc.Block())
    mk = MK(nc, es, blk)
    go, inc, wait, join, sb = mk.go, mk.inc, mk.wait, mk.join, mk.sb

    def tap(name, ap_fn):
        if name not in dbg:
            return
        ap = ap_fn() if callable(ap_fn) else ap_fn
        od = nc.dram_tensor("dbg_" + name, list(ap.shape),
                            ap.dtype, kind="ExternalOutput").ap()
        snap = dict(mk.led)

        def f(e):
            for s in SEMS:
                wait(e, "sync", s, snap[s])
            inc(e.dma_start(out=od, in_=ap), "dsync", 16)
        go("sync", f)

    identf = sb("identf_t", [128, 128], F32)
    identb = sb("identb_t", [128, 128], BF16)
    ones_bf = sb("ones_bf", [1, 128], BF16)
    ones_f = sb("ones_f", [1, 128], F32)
    onescol_f = sb("onescol_f", [128, 1], F32)
    onescol_bf = sb("onescol_bf", [128, 1], BF16)
    wcol_t = sb("wcol_t", [128, 4, KC], F32)
    sc2 = contextlib.ExitStack()
    sc = lambda n, sh, dt: sc2.enter_context(nc.sbuf_tensor(n, sh, dt))
    xnT_hi = sc("xnT_hi", [128, 8, N], BF16)
    xnT_lo = sc("xnT_lo", [128, 8, N], BF16)
    xm_hi = sc("xm_hi", [128, 8, M], BF16)
    xm_lo = sc("xm_lo", [128, 8, M], BF16)
    P = [es.enter_context(nc.psum_tensor(f"P{i}", [128, 512], F32)) for i in range(6)]
    Pb = [es.enter_context(nc.psum_tensor(f"Pb{i}", [128, 1024], BF16)) for i in range(2)]

    WSP = 1024

    def flat(ap_, nelem):
        names = "abcd"[:len(ap_.shape)]
        pat = f"{' '.join(names)} -> ({' '.join(names)})"
        return ap_.rearrange(pat).rearrange("(p f) -> p f", p=128), nelem // 128

    wsplit_jobs = []
    for (src, hid, lod, nel) in ((wfc, fc_hi_d, fc_lo_d, D * D),
                                 (up_wT, up_hi_d, up_lo_d, NU * D * MI),
                                 (down_wT, dn_hi_d, dn_lo_d, NU * MI * D)):
        fs, per = flat(src, nel)
        fh, _ = flat(hid, nel)
        fl, _ = flat(lod, nel)
        for c0 in range(0, per, WSP):
            c1 = min(c0 + WSP, per)
            wsplit_jobs.append((fs[:, c0:c1], fh[:, c0:c1], fl[:, c0:c1]))
    wsp_f = sc("wsp_f", [128, WSP], F32)
    wsp_h = sc("wsp_h", [128, WSP], BF16)
    wsp_l = sc("wsp_l", [128, WSP], BF16)
    wsp_i = [0]

    def emit_wsplit(kn):
        for _ in range(kn):
            if wsp_i[0] >= len(wsplit_jobs):
                return
            src, hid, lod = wsplit_jobs[wsp_i[0]]
            wsp_i[0] += 1

            def f_load(e, src=src):
                wait(e, "sync", "dpool")
                inc(e.dma_start(out=wsp_f[:], in_=src), "d3", 16)
            go("sync", f_load)

            def f_cast(e):
                wait(e, "gpsimd", "d3")
                wait(e, "gpsimd", "dpool")
                inc(e.dma_start(out=wsp_h[:], in_=wsp_f[:]), "dpool", 16)
            go("gpsimd", f_cast)

            def f_lo(e):
                wait(e, "vector", "dpool")
                inc(e.tensor_tensor(wsp_l[:], wsp_f[:], wsp_h[:], ALU.subtract),
                    "vec")
            go("vector", f_lo)

            def f_store(e, hid=hid, lod=lod):
                wait(e, "gpsimd", "vec")
                inc(e.dma_start(out=hid, in_=wsp_h[:]), "dpool", 16)
                inc(e.dma_start(out=lod, in_=wsp_l[:]), "dpool", 16)
            go("gpsimd", f_store)

    def p0_sync(e):
        inc(e.dma_start(out=identf[:], in_=identf_d[:]), "dsync", 16)
        inc(e.dma_start(out=identb[:], in_=identb_d[:]), "dsync", 16)
        inc(e.dma_start(out=wcol_t[:],
                        in_=wcol.rearrange("(c p) k -> p c k", p=128)),
            "dsync", 16)
    go("sync", p0_sync)

    def p0_pool(e):
        e.memset(ones_bf[:], 1.0)
        e.memset(ones_f[:], 1.0)
        e.memset(onescol_bf[:], 1.0)
        inc(e.memset(onescol_f[:], 1.0), "pool")
    go("gpsimd", p0_pool)

    wq_flat, _ = flat(wqkv, HL * D * 3 * DH)
    wqh_flat, _ = flat(wq_hi_d, HL * D * 3 * DH)
    wql_flat, _ = flat(wq_lo_d, HL * D * 3 * DH)
    with contextlib.ExitStack() as s0:
        wq_f = s0.enter_context(nc.sbuf_tensor("wq_f", [128, 6144], F32))
        wq_h = s0.enter_context(nc.sbuf_tensor("wq_h", [128, 6144], BF16))
        wq_l = s0.enter_context(nc.sbuf_tensor("wq_l", [128, 6144], BF16))
        for ih in range(2):
            sl = ts(ih, 6144)

            def p0_ld(e, sl=sl):
                wait(e, "sync", "dpool")
                inc(e.dma_start(out=wq_f[:], in_=wq_flat[:, sl]), "dsync", 16)
            go("sync", p0_ld)

            def p0_act(e):
                wait(e, "scalar", "dsync")
                inc(e.copy(wq_h[:], wq_f[:]), "act")
            go("scalar", p0_act)

            def p0_vec(e):
                wait(e, "vector", "act")
                inc(e.tensor_tensor(wq_l[:], wq_f[:], wq_h[:], ALU.subtract),
                    "vec")
            go("vector", p0_vec)

            def p0_pool2(e, sl=sl):
                wait(e, "gpsimd", "vec")
                inc(e.dma_start(out=wqh_flat[:, sl], in_=wq_h[:]), "dpool", 16)
                inc(e.dma_start(out=wql_flat[:, sl], in_=wq_l[:]), "dpool", 16)
            go("gpsimd", p0_pool2)

        def p0_vec2(e):
            wait(e, "vector", "dsync")
            inc(e.tensor_scalar_add(wcol_t[:, :, 6:7], wcol_t[:, :, 6:7], 1.0),
                "vec")
        go("vector", p0_vec2)
        join()

    with contextlib.ExitStack() as s1:
        sbl = lambda n, sh, dt: s1.enter_context(nc.sbuf_tensor(n, sh, dt))
        nst = sbl("nst", [128, 32], F32)
        sqst = sbl("sqst", [128, 32], F32)
        istd = sbl("istd", [128, 32], F32)
        bist = sbl("bist", [128, 32], F32)
        tst = sbl("tst", [128, 32], F32)
        sq_scr = sbl("sq_scr", [128, 1024], F32)
        xtm_t = [sbl(f"xtm_t{i}", [128, 1024], F32) for i in range(2)]
        x2f = sbl("x2f", [128, 1024], F32)
        x2h = sbl("x2h", [128, 1024], BF16)
        x2l = sbl("x2l", [128, 1024], BF16)
        hist = {}
        for i in range(32):
            dxs = "dx0" if i % 2 == 0 else "dx1"

            def f_load(e, i=i, dxs=dxs):
                if i >= 2:
                    wait(e, "sync", "act", hist[i - 2][0])
                    wait(e, "sync", "vec", hist[i - 2][1])
                inc(e.dma_start(out=xtm_t[i % 2][:], in_=xtm[ts(i, 128), :]),
                    dxs, 16)
            go("sync", f_load)
            dv = mk.led[dxs]

            def f_vec(e, i=i, dv=dv, dxs=dxs):
                wait(e, "vector", dxs, dv)
                inc(e.reduce_sum(nst[:, i:i + 1], xtm_t[i % 2][:], axis=AX.X,
                                 negate=True), "vec")
            go("vector", f_vec)

            def f_act(e, i=i, dv=dv, dxs=dxs):
                wait(e, "scalar", dxs, dv)
                wait(e, "scalar", "act")
                inc(e.activation(sq_scr[:], xtm_t[i % 2][:], AF.Square,
                                 bias=0.0, scale=1.0,
                                 accum_out=sqst[:, i:i + 1]), "act")
            go("scalar", f_act)
            hist[i] = (mk.led["act"], mk.led["vec"])

        varr = sbl("varr", [128, 32], F32)
        sdev = sbl("sdev", [128, 32], F32)

        def f_stats(e):
            wait(e, "vector", "act")
            e.drain()
            e.tensor_scalar_mul(nst[:], nst[:], 1.0 / D)
            e.drain()
            e.tensor_tensor(tst[:], nst[:], nst[:], ALU.mult)
            e.tensor_scalar_mul(sdev[:], sqst[:], 1.0 / D)
            e.drain()
            e.tensor_tensor(varr[:], sdev[:], tst[:], ALU.subtract)
            e.drain()
            inc(e.tensor_scalar_add(varr[:], varr[:], 1e-5), "vec")
        go("vector", f_stats)

        def f_sqrt(e):
            wait(e, "scalar", "vec")
            inc(e.activation(sdev[:], varr[:], AF.Sqrt, bias=0.0, scale=1.0),
                "act")
        go("scalar", f_sqrt)

        def f_recip(e):
            wait(e, "vector", "act")
            e.reciprocal(istd[:], sdev[:])
            e.drain()
            e.tensor_tensor(bist[:], nst[:], istd[:], ALU.mult)
            inc(e.drain(), "vec")
        go("vector", f_recip)
        tap("istd", lambda: istd[:])
        tap("bist", lambda: bist[:])

        hist2 = {}
        for i in range(32):
            dxs = "dx0" if i % 2 == 0 else "dx1"

            def f_load2(e, i=i, dxs=dxs):
                if i >= 2:
                    wait(e, "sync", "vec", hist2[i - 2])
                else:
                    wait(e, "sync", "vec")
                    wait(e, "sync", "act")
                inc(e.dma_start(out=xtm_t[i % 2][:], in_=xtm[ts(i, 128), :]),
                    dxs, 16)
            go("sync", f_load2)
            dv = mk.led[dxs]

            def f_ap(e, i=i, dv=dv, dxs=dxs):
                wait(e, "vector", dxs, dv)
                wait(e, "vector", "act")
                inc(e.tensor_scalar(x2f[:], xtm_t[i % 2][:], istd[:, i:i + 1],
                                    bist[:, i:i + 1], ALU.mult, ALU.add), "vec")
            go("vector", f_ap)
            vv = mk.led["vec"]
            hist2[i] = vv

            def f_hi(e, vv=vv):
                wait(e, "scalar", "vec", vv)
                wait(e, "scalar", "pe")
                inc(e.copy(x2h[:], x2f[:]), "act")
            go("scalar", f_hi)
            av = mk.led["act"]

            def f_lo(e, av=av):
                wait(e, "vector", "act", av)
                wait(e, "vector", "pe")
                e.drain()
                inc(e.tensor_tensor(x2l[:], x2f[:], x2h[:], ALU.subtract), "vec")
            go("vector", f_lo)

            for k in range(8):
                def f_tp(e, k=k):
                    wait(e, "tensor", "vec")
                    wait(e, "tensor", "act")
                    e.transpose(Pb[k % 2][:, 0:128], x2h[:, ts(k, 128)],
                                identb[:])
                    inc(e.transpose(Pb[k % 2][:, 128:256], x2l[:, ts(k, 128)],
                                    identb[:]), "pe")
                go("tensor", f_tp)
                pv = mk.led["pe"]

                def f_tpc(e, k=k, i=i, pv=pv):
                    wait(e, "vector", "pe", pv)
                    e.tensor_copy(xnT_hi[:, k, ts(i, 128)], Pb[k % 2][:, 0:128])
                    inc(e.tensor_copy(xnT_lo[:, k, ts(i, 128)],
                                      Pb[k % 2][:, 128:256]), "vec")
                go("vector", f_tpc)

        xm_f = sbl("xm_f", [128, 8, M], F32)
        xm_t = sbl("xm_t", [128, M], F32)
        xm_t2 = sbl("xm_t2", [128, M], F32)
        for c in range(8):
            def f_xm(e, c=c):
                wait(e, "vector", "vec")
                e.tensor_reduce(xm_t[:],
                                xnT_hi[:, c, :].rearrange("p (m l) -> p m l", l=16),
                                axis=AX.X, op=ALU.add)
                e.tensor_reduce(xm_t2[:],
                                xnT_lo[:, c, :].rearrange("p (m l) -> p m l", l=16),
                                axis=AX.X, op=ALU.add)
                e.drain()
                inc(e.tensor_tensor(xm_f[:, c, :], xm_t[:], xm_t2[:],
                                    ALU.add), "vec")
            go("vector", f_xm)

        def f_xm_a(e):
            wait(e, "scalar", "vec")
            inc(e.activation(xm_hi[:], xm_f[:], AF.Copy, bias=0.0,
                             scale=1.0 / 16), "act")
        go("scalar", f_xm_a)

        def f_xm_v(e):
            wait(e, "vector", "act")
            inc(e.scalar_tensor_tensor(xm_lo[:], xm_f[:], 1.0 / 16, xm_hi[:],
                                       ALU.mult, ALU.subtract), "vec")
        go("vector", f_xm_v)
        join()
        tap("xnT_hi", lambda: xnT_hi[:])
        tap("xnT_lo", lambda: xnT_lo[:])
        tap("xm_hi", lambda: xm_hi[:])
        tap("xm_lo", lambda: xm_lo[:])
    if stop <= 1:
        def f_fin1(e):
            for s in SEMS:
                wait(e, "sync", s)
        go("sync", f_fin1)
        sc2.close()
        es.close()
        return nc

    kqT_hi = sc("kqT_hi", [64, N], BF16)
    kqT_lo = sc("kqT_lo", [64, N], BF16)
    qlT_hi = sc("qlT_hi", [64, M], BF16)
    qlT_lo = sc("qlT_lo", [64, M], BF16)
    klT_hi = sc("klT_hi", [64, M], BF16)
    klT_lo = sc("klT_lo", [64, M], BF16)
    zw_hi = sc("zw_hi", [128, 2, DH], BF16)
    zw_lo = sc("zw_lo", [128, 2, DH], BF16)
    whe_hi = sc("whe_hi", [128, 8, 192], BF16)
    whe_lo = sc("whe_lo", [128, 8, 192], BF16)
    wcol_h = sc("wcol_h", [DH, KC], F32)
    NSPL = (len(wsplit_jobs) + HL - 1) // HL

    for hl in range(HL):
        def f_wl(e, hl=hl):
            wait(e, "sync", "pe")
            wait(e, "sync", "dpool")
            inc(e.dma_start(out=whe_hi[:],
                            in_=wq_hi_d[hl].rearrange("(c p) k -> p c k", p=128)),
                "dw", 16)
            inc(e.dma_start(out=whe_lo[:],
                            in_=wq_lo_d[hl].rearrange("(c p) k -> p c k", p=128)),
                "dw", 16)
        go("sync", f_wl)
        whe_dv = mk.led["dw"]

        def f_wc(e, hl=hl):
            wait(e, "vector", "vec")
            inc(e.tensor_copy(wcol_h[:], wcol_t[ts(hl % 2, 64)][:, hl // 2, :]),
                "vec")
        go("vector", f_wc)

        with contextlib.ExitStack() as sA:
            sba = lambda n, sh, dt, hl=hl: sA.enter_context(
                nc.sbuf_tensor(f"{n}_{hl}", sh, dt))
            vsc_hi = sba("vsc_hi", [64, 512], BF16)
            vsc_lo = sba("vsc_lo", [64, 512], BF16)
            v_tm = sba("v_tm", [128, 32, 132], BF16)
            e3f = sba("e3f", [128, M], F32)
            e3h = sba("e3h", [128, M], BF16)
            e3l = sba("e3l", [128, M], BF16)
            aT = sba("aT", [128, 2, M], F32)
            zA = sba("zA", [128, 2, M], F32)
            zTA = sba("zTA", [128, 2, M], F32)
            zB = sba("zB", [128, 2, M], F32)
            zTB = sba("zTB", [128, 2, M], F32)
            uu = sba("uu", [128, 2, M], F32)
            uuT = sba("uuT", [128, 2, M], F32)
            xw = sba("xw", [128, 2, M], F32)
            e2s = sba("e2s", [128, 2, M], F32)
            srow = sba("srow", [1, 520], F32)
            srow_b = sba("srow_b", [1, 512], BF16)
            stat2 = sba("stat2", [128, 8], F32)
            mnb2 = sba("mnb2", [128, 2], BF16)
            mn3 = sba("mn3", [128, 2], F32)
            mr3t = sba("mr3t", [128, 2, 8], F32)
            mrow3 = sba("mrow3", [1, M], BF16)
            w3s = sba("w3s", [128, 2, 132], F32)
            w3v = sba("w3v", [128, 2, DH], F32)

            for tc in range(8):
                def f_mm(e, tc=tc):
                    wait(e, "tensor", "dw", whe_dv)
                    wait(e, "tensor", "vec")
                    wait(e, "tensor", "act")
                    ii = mm3(e, P[tc % 2][:, 0:512],
                             lambda k, w: (whe_hi if w == "hi" else whe_lo)[:, k, 64:192],
                             lambda k, w: (xnT_hi if w == "hi" else xnT_lo)[:, k, ts(tc, 512)],
                             range(8))
                    inc(ii, "pe")
                go("tensor", f_mm)
                pv = mk.led["pe"]

                def f_ev_a(e, tc=tc, pv=pv):
                    wait(e, "scalar", "pe", pv)
                    e.copy(kqT_hi[:, ts(tc, 512)], P[tc % 2][0:64, 0:512])
                    inc(e.copy(vsc_hi[:], P[tc % 2][64:128, 0:512]), "act")
                go("scalar", f_ev_a)
                av = mk.led["act"]

                def f_ev_v(e, tc=tc, pv=pv, av=av):
                    wait(e, "vector", "pe", pv)
                    wait(e, "vector", "act", av)
                    e.tensor_tensor(kqT_lo[:, ts(tc, 512)], P[tc % 2][0:64, 0:512],
                                    kqT_hi[:, ts(tc, 512)], ALU.subtract)
                    inc(e.tensor_tensor(vsc_lo[:], P[tc % 2][64:128, 0:512],
                                        vsc_hi[:], ALU.subtract), "vec")
                go("vector", f_ev_v)
                vv = mk.led["vec"]

                def f_tp(e, tc=tc, vv=vv):
                    wait(e, "tensor", "vec", vv)
                    for sub in range(4):
                        e.transpose(Pb[0][:, ts(sub, 64)], vsc_hi[:, ts(sub, 128)],
                                    identb[0:64, 0:64])
                    for sub in range(4):
                        ii = e.transpose(Pb[1][:, ts(sub, 64)],
                                         vsc_lo[:, ts(sub, 128)],
                                         identb[0:64, 0:64])
                    inc(ii, "pe")
                go("tensor", f_tp)
                pv2 = mk.led["pe"]

                def f_tpc(e, tc=tc, pv2=pv2):
                    wait(e, "vector", "pe", pv2)
                    for sub in range(4):
                        e.tensor_copy(v_tm[:, tc * 4 + sub, 0:64],
                                      Pb[0][:, ts(sub, 64)])
                    for sub in range(4):
                        ii = e.tensor_copy(v_tm[:, tc * 4 + sub, 64:128],
                                           Pb[1][:, ts(sub, 64)])
                    inc(ii, "vec")
                go("vector", f_tpc)
            go("gpsimd", lambda e: inc(e.memset(v_tm[:, :, 128:129], 1.0), "pool"))
            onev = mk.led["pool"]
            if hl == 0:
                tap("kT_hi", lambda: kqT_hi[:])
                tap("kT_lo", lambda: kqT_lo[:])
                tap("v_tm", lambda: v_tm[:])

            def f_lm(e):
                wait(e, "tensor", "vec")
                mm3(e, P[4][0:64, 0:256],
                    lambda k, w: (whe_hi if w == "hi" else whe_lo)[:, k, 0:64],
                    lambda k, w: (xm_hi if w == "hi" else xm_lo)[:, k, :],
                    range(8))
                ii = mm3(e, P[4][0:64, 256:512],
                         lambda k, w: (whe_hi if w == "hi" else whe_lo)[:, k, 64:128],
                         lambda k, w: (xm_hi if w == "hi" else xm_lo)[:, k, :],
                         range(8))
                inc(ii, "pe")
            go("tensor", f_lm)
            pv = mk.led["pe"]

            def f_lm_a(e, pv=pv):
                wait(e, "scalar", "pe", pv)
                e.activation(qlT_hi[:], P[4][0:64, 0:256], AF.Copy, bias=0.0,
                             scale=SCALE)
                inc(e.copy(klT_hi[:], P[4][0:64, 256:512]), "act")
            go("scalar", f_lm_a)
            av = mk.led["act"]

            def f_lm_v(e, pv=pv, av=av):
                wait(e, "vector", "pe", pv)
                wait(e, "vector", "act", av)
                e.scalar_tensor_tensor(qlT_lo[:], P[4][0:64, 0:256], SCALE,
                                       qlT_hi[:], ALU.mult, ALU.subtract)
                inc(e.tensor_tensor(klT_lo[:], P[4][0:64, 256:512], klT_hi[:],
                                    ALU.subtract), "vec")
            go("vector", f_lm_v)
            if hl == 0:
                tap("qlT_hi", lambda: qlT_hi[:])
                tap("qlT_lo", lambda: qlT_lo[:])
                tap("klT_hi", lambda: klT_hi[:])

            def f_s2(e):
                wait(e, "tensor", "vec")
                for h2 in range(2):
                    ii = mm3(e, P[4][:, ts(h2, 256)],
                             lambda k, w, h2=h2: (qlT_hi if w == "hi" else qlT_lo)[:, ts(h2, 128)],
                             lambda k, w: (klT_hi if w == "hi" else klT_lo)[:],
                             (0,))
                inc(ii, "pe")
            go("tensor", f_s2)
            pv = mk.led["pe"]

            def f_s2v(e, pv=pv):
                wait(e, "vector", "pe", pv)
                e.reduce_max(stat2[:, 0:1], P[4][:, 0:256], axis=AX.X, negate=True)
                inc(e.reduce_max(stat2[:, 1:2], P[4][:, 256:512], axis=AX.X,
                                 negate=True), "vec")
            go("vector", f_s2v)

            def f_s2r(e):
                wait(e, "scalar", "vec")
                inc(e.copy(mnb2[:], stat2[:, 0:2]), "act")
            go("scalar", f_s2r)

            def f_s2r2(e):
                wait(e, "vector", "act")
                inc(e.tensor_copy(stat2[:, 0:2], mnb2[:]), "vec")
            go("vector", f_s2r2)

            def f_s2e(e):
                wait(e, "scalar", "vec")
                e.activation(e2s[:, 0, :], P[4][:, 0:256], AF.Exp,
                             bias=stat2[:, 0:1], scale=1.0,
                             accum_out=stat2[:, 2:3])
                inc(e.activation(e2s[:, 1, :], P[4][:, 256:512], AF.Exp,
                                 bias=stat2[:, 1:2], scale=1.0,
                                 accum_out=stat2[:, 3:4]), "act")
            go("scalar", f_s2e)

            def f_s2i(e):
                wait(e, "vector", "act")
                inc(e.reciprocal(stat2[:, 4:6], stat2[:, 2:4]), "vec")
            go("vector", f_s2i)

            def f_s2t(e):
                wait(e, "tensor", "vec")
                e.transpose(P[2][0:1, 0:128], stat2[:, 0:1], identf[:])
                e.transpose(P[2][0:1, 128:256], stat2[:, 1:2], identf[:])
                e.transpose(P[2][0:1, 256:384], stat2[:, 4:5], identf[:])
                inc(e.transpose(P[2][0:1, 384:512], stat2[:, 5:6], identf[:]),
                    "pe")
            go("tensor", f_s2t)
            pv = mk.led["pe"]

            def f_s2tc(e, pv=pv):
                wait(e, "vector", "pe", pv)
                inc(e.tensor_copy(srow[0:1, 0:512], P[2][0:1, 0:512]), "vec")
            go("vector", f_s2tc)

            def f_s2tb(e):
                wait(e, "scalar", "vec")
                inc(e.copy(srow_b[0:1, 0:256], srow[0:1, 0:256]), "act")
            go("scalar", f_s2tb)

            def f_s2t2(e):
                wait(e, "tensor", "act")
                for h2 in range(2):
                    mm3(e, P[5][:, ts(h2, 256)],
                        lambda k, w, h2=h2: (klT_hi if w == "hi" else klT_lo)[:, ts(h2, 128)],
                        lambda k, w: (qlT_hi if w == "hi" else qlT_lo)[:],
                        (0,), stop=False)
                    e.matmul(P[5][:, ts(h2, 256)], ones_bf[:],
                             srow_b[0:1, 0:256], start=False, stop=True)
                inc(e.matmul(P[3][:, 0:256], ones_f[:], srow[0:1, 256:512],
                             start=True, stop=True), "pe")
            go("tensor", f_s2t2)
            pv = mk.led["pe"]

            def f_s2t3(e, pv=pv):
                wait(e, "scalar", "pe", pv)
                e.activation(e2s[:, 0, :], P[5][:, 0:256], AF.Exp, bias=0.0,
                             scale=1.0)
                inc(e.activation(e2s[:, 1, :], P[5][:, 256:512], AF.Exp,
                                 bias=0.0, scale=1.0), "act")
            go("scalar", f_s2t3)

            def f_s2t4(e):
                wait(e, "vector", "act")
                e.tensor_copy(xw[:, 0, :], P[3][:, 0:256])
                e.drain()
                e.tensor_tensor(aT[:, 0, :], e2s[:, 0, :], xw[:, 0, :], ALU.mult)
                inc(e.tensor_tensor(aT[:, 1, :], e2s[:, 1, :], xw[:, 0, :],
                                    ALU.mult), "vec")
            go("vector", f_s2t4)

            def f_z0a(e):
                wait(e, "vector", "vec")
                e.reduce_sum(stat2[:, 0:1], aT[:, 0, :], axis=AX.X)
                inc(e.reduce_sum(stat2[:, 1:2], aT[:, 1, :], axis=AX.X), "vec")
            go("vector", f_z0a)

            def f_z0b(e):
                wait(e, "tensor", "vec")
                e.transpose(P[2][0:1, 0:128], stat2[:, 0:1], identf[:])
                e.transpose(P[2][0:1, 128:256], stat2[:, 1:2], identf[:])
                for k in range(2):
                    ii = e.matmul(P[2][0:1, 256:512], onescol_f[:], aT[:, k, :],
                                  start=(k == 0), stop=(k == 1))
                inc(ii, "pe")
            go("tensor", f_z0b)
            pv = mk.led["pe"]

            def f_z0c(e, pv=pv):
                wait(e, "vector", "pe", pv)
                e.tensor_copy(srow[0:1, 0:512], P[2][0:1, 0:512])
                e.drain()
                e.reduce_max(srow[0:1, 512:513], srow[0:1, 0:256], axis=AX.X)
                e.reduce_max(srow[0:1, 513:514], srow[0:1, 256:512], axis=AX.X)
                e.drain()
                e.tensor_tensor(srow[0:1, 514:515], srow[0:1, 512:513],
                                srow[0:1, 513:514], ALU.mult)
                e.drain()
                inc(e.reciprocal(srow[0:1, 515:516], srow[0:1, 514:515]), "vec")
            go("vector", f_z0c)

            def f_z0d(e):
                wait(e, "tensor", "vec")
                inc(e.matmul(P[2][:, 0:1], ones_f[:], srow[0:1, 515:516],
                             start=True, stop=True), "pe")
            go("tensor", f_z0d)
            pv = mk.led["pe"]

            def f_z0e(e, pv=pv):
                wait(e, "vector", "pe", pv)
                e.tensor_copy(stat2[:, 6:7], P[2][:, 0:1])
                e.drain()
                e.tensor_scalar(zA[:, 0, :], aT[:, 0, :], stat2[:, 6:7], None,
                                ALU.mult)
                inc(e.tensor_scalar(zA[:, 1, :], aT[:, 1, :], stat2[:, 6:7],
                                    None, ALU.mult), "vec")
            go("vector", f_z0e)

            def f_z0f(e):
                wait(e, "tensor", "vec")
                e.transpose(P[2][:, 0:128], zA[:, 0, 0:128], identf[:])
                e.transpose(P[2][:, 128:256], zA[:, 1, 0:128], identf[:])
                e.transpose(P[3][:, 0:128], zA[:, 0, 128:256], identf[:])
                inc(e.transpose(P[3][:, 128:256], zA[:, 1, 128:256], identf[:]),
                    "pe")
            go("tensor", f_z0f)
            pv = mk.led["pe"]

            def f_z0g(e, pv=pv):
                wait(e, "vector", "pe", pv)
                e.tensor_copy(zTA[:, 0, 0:128], P[2][:, 0:128])
                e.tensor_copy(zTA[:, 0, 128:256], P[2][:, 128:256])
                e.tensor_copy(zTA[:, 1, 0:128], P[3][:, 0:128])
                inc(e.tensor_copy(zTA[:, 1, 128:256], P[3][:, 128:256]), "vec")
            go("vector", f_z0g)
            if hl == 0:
                tap("z0", lambda: zA[:])
                tap("zT0", lambda: zTA[:])

            zc, zTc, zn, zTn = zA, zTA, zB, zTB
            for it in range(ITERS):
                def f_u(e, zc=zc):
                    wait(e, "tensor", "vec")
                    mmf(e, (P[4][:, 0:256], P[4][:, 256:512]), aT, zc)
                    inc(mmf(e, (P[5][:, 0:256], P[5][:, 256:512]), zc, aT), "pe")
                go("tensor", f_u)
                pv = mk.led["pe"]

                def f_ue(e, pv=pv):
                    wait(e, "scalar", "pe", pv)
                    e.copy(uu[:, 0, :], P[4][:, 0:256])
                    e.copy(uu[:, 1, :], P[4][:, 256:512])
                    e.copy(uuT[:, 0, :], P[5][:, 0:256])
                    inc(e.copy(uuT[:, 1, :], P[5][:, 256:512]), "act")
                go("scalar", f_ue)

                def f_p(e):
                    wait(e, "tensor", "act")
                    inc(mmf(e, (P[4][:, 0:256], P[4][:, 256:512]), uuT, uu),
                        "pe")
                go("tensor", f_p)
                pv = mk.led["pe"]

                def f_x1(e, pv=pv):
                    wait(e, "vector", "pe", pv)
                    e.scalar_tensor_tensor(xw[:, 0, :], uu[:, 0, :], 7.0,
                                           P[4][:, 0:256], ALU.mult, ALU.subtract)
                    inc(e.scalar_tensor_tensor(xw[:, 1, :], uu[:, 1, :], 7.0,
                                               P[4][:, 256:512], ALU.mult,
                                               ALU.subtract), "vec")
                go("vector", f_x1)

                def f_r(e):
                    wait(e, "tensor", "vec")
                    inc(mmf(e, (P[4][:, 0:256], P[4][:, 256:512]), uuT, xw),
                        "pe")
                go("tensor", f_r)
                pv = mk.led["pe"]

                def f_x3(e, pv=pv):
                    wait(e, "vector", "pe", pv)
                    e.scalar_tensor_tensor(xw[:, 0, :], uu[:, 0, :], 15.0,
                                           P[4][:, 0:256], ALU.mult, ALU.subtract)
                    inc(e.scalar_tensor_tensor(xw[:, 1, :], uu[:, 1, :], 15.0,
                                               P[4][:, 256:512], ALU.mult,
                                               ALU.subtract), "vec")
                go("vector", f_x3)

                def f_s(e, zc=zc, zTc=zTc):
                    wait(e, "tensor", "vec")
                    mmf(e, (P[4][:, 0:256], P[4][:, 256:512]), zTc, xw)
                    inc(mmf(e, (P[5][:, 0:256], P[5][:, 256:512]), xw, zTc),
                        "pe")
                go("tensor", f_s)
                pv = mk.led["pe"]

                def f_z(e, zc=zc, zTc=zTc, zn=zn, zTn=zTn, pv=pv):
                    wait(e, "vector", "pe", pv)
                    e.scalar_tensor_tensor(zn[:, 0, :], zc[:, 0, :], 13.0,
                                           P[4][:, 0:256], ALU.mult, ALU.subtract)
                    e.scalar_tensor_tensor(zn[:, 1, :], zc[:, 1, :], 13.0,
                                           P[4][:, 256:512], ALU.mult,
                                           ALU.subtract)
                    e.scalar_tensor_tensor(zTn[:, 0, :], zTc[:, 0, :], 13.0,
                                           P[5][:, 0:256], ALU.mult, ALU.subtract)
                    e.scalar_tensor_tensor(zTn[:, 1, :], zTc[:, 1, :], 13.0,
                                           P[5][:, 256:512], ALU.mult,
                                           ALU.subtract)
                    e.drain()
                    e.tensor_scalar_mul(zn[:, 0, :], zn[:, 0, :], 0.25)
                    e.tensor_scalar_mul(zn[:, 1, :], zn[:, 1, :], 0.25)
                    e.tensor_scalar_mul(zTn[:, 0, :], zTn[:, 0, :], 0.25)
                    inc(e.tensor_scalar_mul(zTn[:, 1, :], zTn[:, 1, :], 0.25),
                        "vec")
                go("vector", f_z)
                if hl == 0 and it == 0:
                    tap("u0", lambda: uu[:])
                    tap("uT0", lambda: uuT[:])
                    tap("x3_0", lambda: xw[:])
                    tap("z1", lambda: zn[:])
                    tap("zT1", lambda: zTn[:])
                zc, zn = zn, zc
                zTc, zTn = zTn, zTc
            zT_fin = zTc
            if hl == 0:
                tap("aT", lambda: aT[:])
                tap("zT_fin", lambda: zT_fin[:])
                tap("zA0", lambda: zA[:])
                tap("srow", lambda: srow[:])
                tap("stat2", lambda: stat2[:])
                tap("e2s", lambda: e2s[:])

            for tc in range(8):
                def f_s3(e, tc=tc):
                    wait(e, "tensor", "vec")
                    for h2 in range(2):
                        ii = mm3(e, P[h2][:, 0:512],
                                 lambda k, w, h2=h2: (qlT_hi if w == "hi" else qlT_lo)[:, ts(h2, 128)],
                                 lambda k, w: (kqT_hi if w == "hi" else kqT_lo)[:, ts(tc, 512)],
                                 (0,))
                    inc(ii, "pe")
                go("tensor", f_s3)
                pv = mk.led["pe"]

                def f_s3v(e, tc=tc, pv=pv):
                    wait(e, "vector", "pe", pv)
                    e.reduce_max(mr3t[:, 0, tc:tc + 1], P[0][:, 0:512], axis=AX.X)
                    inc(e.reduce_max(mr3t[:, 1, tc:tc + 1], P[1][:, 0:512],
                                     axis=AX.X), "vec")
                go("vector", f_s3v)

            mnb3 = sba("mnb3", [128, 2], BF16)

            def f_s3m(e):
                wait(e, "vector", "vec")
                e.tensor_reduce(mn3[:], mr3t[:], axis=AX.X, op=ALU.max,
                                negate=True)
                inc(e.drain(), "vec")
            go("vector", f_s3m)

            def f_s3m2(e):
                wait(e, "scalar", "vec")
                inc(e.copy(mnb3[:], mn3[:]), "act")
            go("scalar", f_s3m2)

            def f_s3m3(e):
                wait(e, "vector", "act")
                inc(e.tensor_copy(mn3[:], mnb3[:]), "vec")
            go("vector", f_s3m3)

            def f_s3t(e):
                wait(e, "tensor", "vec")
                wait(e, "tensor", "act")
                e.transpose(Pb[0][0:1, 0:128], mnb3[:, 0:1], identb[:])
                inc(e.transpose(Pb[0][0:1, 128:256], mnb3[:, 1:2], identb[:]),
                    "pe")
            go("tensor", f_s3t)
            pv = mk.led["pe"]

            def f_s3tc(e, pv=pv):
                wait(e, "vector", "pe", pv)
                inc(e.tensor_copy(mrow3[0:1, 0:256], Pb[0][0:1, 0:256]), "vec")
            go("vector", f_s3tc)

            for sc in range(32):
                def f_v1(e, sc=sc):
                    wait(e, "tensor", "vec")
                    wait(e, "tensor", "act")
                    wait(e, "tensor", "pool", onev)
                    mm3(e, P[2][:, 0:256],
                        lambda k, w, sc=sc: (kqT_hi if w == "hi" else kqT_lo)[:, ts(sc, 128)],
                        lambda k, w: (qlT_hi if w == "hi" else qlT_lo)[:],
                        (0,), stop=False)
                    inc(e.matmul(P[2][:, 0:256], ones_bf[:], mrow3[0:1, :],
                                 start=False, stop=True), "pe")
                go("tensor", f_v1)
                pv = mk.led["pe"]

                def f_v2(e, sc=sc, pv=pv):
                    wait(e, "scalar", "pe", pv)
                    e.activation(e3f[:], P[2][:, 0:256], AF.Exp, bias=0.0,
                                 scale=1.0)
                    e.drain()
                    inc(e.copy(e3h[:], e3f[:]), "act")
                go("scalar", f_v2)
                av = mk.led["act"]

                def f_v3(e, sc=sc, av=av):
                    wait(e, "vector", "act", av)
                    inc(e.tensor_tensor(e3l[:], e3f[:], e3h[:], ALU.subtract),
                        "vec")
                go("vector", f_v3)
                vv = mk.led["vec"]

                def f_v4(e, sc=sc, vv=vv):
                    wait(e, "tensor", "vec", vv)
                    for h2 in range(2):
                        e.matmul(P[3 + h2][:, 0:129], e3h[:, ts(h2, 128)],
                                 v_tm[:, sc, 0:129], start=(sc == 0), stop=False)
                        ii = e.matmul(P[3 + h2][:, 0:129],
                                      e3l[:, ts(h2, 128)], v_tm[:, sc, 0:129],
                                      start=False, stop=(sc == 31))
                    inc(ii, "pe")
                go("tensor", f_v4)

            def f_w3(e):
                wait(e, "vector", "pe")
                e.tensor_copy(w3s[:, 0, 0:129], P[3][:, 0:129])
                e.tensor_copy(w3s[:, 1, 0:129], P[4][:, 0:129])
                e.drain()
                e.tensor_copy(stat2[:, 2:4],
                              w3s[:, :, 128:129].rearrange("p c o -> p (c o)"))
                e.drain()
                e.reciprocal(stat2[:, 0:2], stat2[:, 2:4])
                e.tensor_tensor(w3v[:, 0, :], w3s[:, 0, 0:64], w3s[:, 0, 64:128],
                                ALU.add)
                e.tensor_tensor(w3v[:, 1, :], w3s[:, 1, 0:64], w3s[:, 1, 64:128],
                                ALU.add)
                e.drain()
                e.tensor_scalar(w3v[:, 0, :], w3v[:, 0, :], stat2[:, 0:1], None,
                                ALU.mult)
                inc(e.tensor_scalar(w3v[:, 1, :], w3v[:, 1, :], stat2[:, 1:2],
                                    None, ALU.mult), "vec")
            go("vector", f_w3)

            def f_zw(e, zT_fin=zT_fin):
                wait(e, "tensor", "vec")
                for ot in range(2):
                    for k in range(2):
                        ii = e.matmul(P[4][:, ts(ot, 64)],
                                      zT_fin[:, k, ts(ot, 128)], w3v[:, k, :],
                                      start=(k == 0), stop=(k == 1))
                inc(ii, "pe")
            go("tensor", f_zw)
            pv = mk.led["pe"]

            def f_zwe(e, pv=pv):
                wait(e, "scalar", "pe", pv)
                e.copy(zw_hi[:, 0, :], P[4][:, 0:64])
                inc(e.copy(zw_hi[:, 1, :], P[4][:, 64:128]), "act")
            go("scalar", f_zwe)
            av = mk.led["act"]

            def f_zwl(e, av=av):
                wait(e, "vector", "act", av)
                e.tensor_tensor(zw_lo[:, 0, :], P[4][:, 0:64], zw_hi[:, 0, :],
                                ALU.subtract)
                inc(e.tensor_tensor(zw_lo[:, 1, :], P[4][:, 64:128],
                                    zw_hi[:, 1, :], ALU.subtract), "vec")
            go("vector", f_zwl)
            if hl == 0:
                tap("mrow3", lambda: mrow3[:])
                tap("w3v", lambda: w3v[:])
                tap("zw_hi", lambda: zw_hi[:])
                tap("zw_lo", lambda: zw_lo[:])

            emit_wsplit(NSPL // 2)

        with contextlib.ExitStack() as sB:
            sbb = lambda n, sh, dt, hl=hl: sB.enter_context(
                nc.sbuf_tensor(f"{n}_{hl}", sh, dt))
            outc = sbb("outc", [64, N + 12], F32)
            acc = [sbb(f"acc{i}", [64, 512], F32) for i in range(2)]
            e1f = sbb("e1f", [128, 2, 512], F32)
            e1h = sbb("e1h", [128, 2, 512], BF16)
            e1l = sbb("e1l", [128, 2, 512], BF16)
            mn1 = sbb("mn1", [128, 32], F32)
            mn1b = sbb("mn1b", [128, 32], BF16)
            rowm = sbb("rowm", [1, 512], BF16)
            rowi = sbb("rowi", [1, 512], F32)
            rowi2 = sbb("rowi2", [1, 512], F32)
            irs = sbb("irs", [64, 512], F32)

            join()

            for tc in range(8):
                def f_qm(e, tc=tc):
                    wait(e, "tensor", "vec")
                    wait(e, "tensor", "act")
                    ii = mm3(e, P[tc % 2][0:64, 0:512],
                             lambda k, w: (whe_hi if w == "hi" else whe_lo)[:, k, 0:64],
                             lambda k, w: (xnT_hi if w == "hi" else xnT_lo)[:, k, ts(tc, 512)],
                             range(8))
                    inc(ii, "pe")
                go("tensor", f_qm)
                pv = mk.led["pe"]

                def f_qa(e, tc=tc, pv=pv):
                    wait(e, "scalar", "pe", pv)
                    inc(e.activation(kqT_hi[:, ts(tc, 512)], P[tc % 2][0:64, 0:512],
                                     AF.Copy, bias=0.0, scale=SCALE), "act")
                go("scalar", f_qa)
                av = mk.led["act"]

                def f_qv(e, tc=tc, pv=pv, av=av):
                    wait(e, "vector", "pe", pv)
                    wait(e, "vector", "act", av)
                    inc(e.scalar_tensor_tensor(kqT_lo[:, ts(tc, 512)],
                                               P[tc % 2][0:64, 0:512], SCALE,
                                               kqT_hi[:, ts(tc, 512)], ALU.mult,
                                               ALU.subtract), "vec")
                go("vector", f_qv)

            if hl == 0:
                tap("qT_hi", lambda: kqT_hi[:])
                tap("qT_lo", lambda: kqT_lo[:])
            for sc in range(32):
                def f_1s(e, sc=sc):
                    wait(e, "tensor", "vec")
                    wait(e, "tensor", "act")
                    ii = mm3(e, P[2 + sc % 2][:, 0:256],
                             lambda k, w, sc=sc: (kqT_hi if w == "hi" else kqT_lo)[:, ts(sc, 128)],
                             lambda k, w: (klT_hi if w == "hi" else klT_lo)[:],
                             (0,))
                    inc(ii, "pe")
                go("tensor", f_1s)
                pv = mk.led["pe"]

                def f_1sv(e, sc=sc, pv=pv):
                    wait(e, "vector", "pe", pv)
                    inc(e.reduce_max(mn1[:, sc:sc + 1], P[2 + sc % 2][:, 0:256],
                                     axis=AX.X, negate=True), "vec")
                go("vector", f_1sv)

            def f_1sb(e):
                wait(e, "scalar", "vec")
                inc(e.copy(mn1b[:], mn1[:]), "act")
            go("scalar", f_1sb)

            def f_gz(e):
                wait(e, "vector", "vec")
                e.memset(outc[:, 0:6], 0.0)
                inc(e.memset(outc[:, N + 6:N + 12], 0.0), "vec")
            go("vector", f_gz)

            for tc in range(8):
                def f_rowt(e, tc=tc):
                    wait(e, "tensor", "vec")
                    wait(e, "tensor", "act")
                    for r4 in range(4):
                        ii = e.transpose(Pb[0][0:1, ts(r4, 128)],
                                         mn1b[:, 4 * tc + r4:4 * tc + r4 + 1],
                                         identb[:])
                    inc(ii, "pe")
                go("tensor", f_rowt)
                pv = mk.led["pe"]

                def f_rowc(e, pv=pv):
                    wait(e, "vector", "pe", pv)
                    inc(e.tensor_copy(rowm[0:1, 0:512], Pb[0][0:1, 0:512]),
                        "vec")
                go("vector", f_rowc)
                vv = mk.led["vec"]

                def f_o1(e, tc=tc, vv=vv):
                    wait(e, "tensor", "vec", vv)
                    for h2 in range(2):
                        mm3(e, P[h2][:, 0:512],
                            lambda k, w, h2=h2: (klT_hi if w == "hi" else klT_lo)[:, ts(h2, 128)],
                            lambda k, w, tc=tc: (kqT_hi if w == "hi" else kqT_lo)[:, ts(tc, 512)],
                            (0,), stop=False)
                        ii = e.matmul(P[h2][:, 0:512], ones_bf[:],
                                      rowm[0:1, 0:512], start=False, stop=True)
                    inc(ii, "pe")
                go("tensor", f_o1)
                pv = mk.led["pe"]

                def f_o2(e, tc=tc, pv=pv):
                    wait(e, "scalar", "pe", pv)
                    e.activation(e1f[:, 0, :], P[0][:, 0:512], AF.Exp, bias=0.0,
                                 scale=1.0)
                    e.activation(e1f[:, 1, :], P[1][:, 0:512], AF.Exp, bias=0.0,
                                 scale=1.0)
                    e.drain()
                    inc(e.copy(e1h[:], e1f[:]), "act")
                go("scalar", f_o2)
                av = mk.led["act"]

                def f_o3(e, tc=tc, av=av):
                    wait(e, "vector", "act", av)
                    inc(e.tensor_tensor(e1l[:], e1f[:], e1h[:], ALU.subtract),
                        "vec")
                go("vector", f_o3)
                vv = mk.led["vec"]

                def f_o3b(e, vv=vv):
                    wait(e, "tensor", "vec", vv)
                    i = 0
                    for h2 in range(2):
                        for ee in (e1h, e1l):
                            ii = e.matmul(P[3][0:1, 0:512], onescol_bf[:],
                                          ee[:, h2, :], start=(i == 0),
                                          stop=(i == 3))
                            i += 1
                    inc(ii, "pe")
                go("tensor", f_o3b)
                pv = mk.led["pe"]

                def f_o3c(e, pv=pv):
                    wait(e, "vector", "pe", pv)
                    e.tensor_copy(rowi2[0:1, 0:512], P[3][0:1, 0:512])
                    e.drain()
                    inc(e.reciprocal(rowi[0:1, 0:512], rowi2[0:1, 0:512]), "vec")
                go("vector", f_o3c)
                vv = mk.led["vec"]

                def f_o4(e, tc=tc, vv=vv):
                    wait(e, "tensor", "vec", vv)
                    inc(e.matmul(P[3][0:64, 0:512], ones_f[0:1, 0:64],
                                 rowi[0:1, 0:512], start=True, stop=True), "pe")
                    i = 0
                    for h2 in range(2):
                        for (zz, ee) in ((zw_hi, e1h), (zw_hi, e1l), (zw_lo, e1h)):
                            ii = e.matmul(P[2][0:64, 0:512], zz[:, h2, :],
                                          ee[:, h2, :], start=(i == 0),
                                          stop=(i == 5))
                            i += 1
                    inc(ii, "pe")
                go("tensor", f_o4)
                pv = mk.led["pe"]

                def f_o5(e, tc=tc, pv=pv):
                    wait(e, "vector", "pe", pv)
                    e.tensor_copy(irs[:], P[3][0:64, 0:512])
                    e.drain()
                    inc(e.tensor_tensor(outc[:, 6 + tc * 512:6 + (tc + 1) * 512],
                                        P[2][0:64, 0:512], irs[:], ALU.mult),
                        "vec")
                go("vector", f_o5)

            for cc_ in range(8):
                def f_cv(e, cc_=cc_, hl=hl):
                    wait(e, "vector", "vec")
                    wait(e, "vector", "d4")
                    a, b = acc[0], acc[1]
                    e.tensor_scalar(a[:], outc[:, cc_ * 512 + 0:cc_ * 512 + 512],
                                    wcol_h[:, 0:1], None, ALU.mult)
                    for j in range(1, KC):
                        src, dst = (a, b) if j % 2 == 1 else (b, a)
                        e.drain()
                        ii = e.scalar_tensor_tensor(
                            dst[:], outc[:, cc_ * 512 + j:cc_ * 512 + j + 512],
                            wcol_h[:, j:j + 1], src[:], ALU.mult, ALU.add)
                    inc(ii, "vec")
                go("vector", f_cv)
                vv = mk.led["vec"]

                def f_st(e, cc_=cc_, hl=hl, vv=vv):
                    wait(e, "sync", "vec", vv)
                    inc(e.dma_start(out=a2a_in.ap()[cc_, ts(hl, 64), :],
                                    in_=acc[0][:]), "d4", 16)
                go("sync", f_st)

            if hl == 0:
                tap("mn1", lambda: mn1[:])
                tap("outc", lambda: outc[:])
            emit_wsplit(NSPL - NSPL // 2)
            join()

    emit_wsplit(len(wsplit_jobs))
    join()
    if stop <= 2:
        def f_fin2(e):
            for s in SEMS:
                wait(e, "sync", s)
        go("sync", f_fin2)
        sc2.close()
        es.close()
        return nc
    sc2.close()

    def f_cc(e):
        for s in SEMS:
            wait(e, "gpsimd", s)
        with allow_ici_all_to_all():
            inc(e.collective_compute(
                "AllToAll", ALU.bypass, replica_groups=[list(range(8))],
                ins=[a2a_in.ap().opt()], outs=[a2a_out.ap().opt()]), "cc")
    go("gpsimd", f_cc)

    y_tm = sb("y_tm", [128, 16, D], F32)
    s34 = contextlib.ExitStack()
    of_hi = s34.enter_context(nc.sbuf_tensor("of_hi", [128, 8, T], BF16))
    of_lo = s34.enter_context(nc.sbuf_tensor("of_lo", [128, 8, T], BF16))
    with contextlib.ExitStack() as s3:
        of_f = s3.enter_context(nc.sbuf_tensor("of_f", [128, 8, T], F32))

        def f_as(e):
            wait(e, "sync", "cc")
            for r in range(8):
                for q in range(4):
                    ii = e.dma_start(
                        out=of_f[:, 4 * (r % 2) + q, ts(r // 2, 512)],
                        in_=a2a_out.ap()[r, ts(q, 128), :])
                    inc(ii, "dsync", 16)
        go("sync", f_as)
        dv = mk.led["dsync"]

        tap("of_f", lambda: of_f[:])
        for k in range(8):
            def f_sp_a(e, k=k, dv=dv):
                wait(e, "scalar", "dsync", dv)
                inc(e.copy(of_hi[:, k, :], of_f[:, k, :]), "act")
            go("scalar", f_sp_a)
            av = mk.led["act"]

            def f_sp_v(e, k=k, av=av):
                wait(e, "vector", "act", av)
                inc(e.tensor_tensor(of_lo[:, k, :], of_f[:, k, :], of_hi[:, k, :],
                                    ALU.subtract), "vec")
            go("vector", f_sp_v)
        join()

    with contextlib.ExitStack() as s4:
        sbl = lambda n, sh, dt: s4.enter_context(nc.sbuf_tensor(n, sh, dt))
        fch = sbl("fch", [128, 8, D], BF16)
        fcl = sbl("fcl", [128, 8, D], BF16)
        bfc_t = sbl("bfc_t", [128, D], F32)
        brx_t = sbl("brx_t", [128, D], F32)
        xres_t = [sbl(f"xres_t{i}", [128, D], F32) for i in range(2)]

        def f_fl(e):
            wait(e, "sync", "dpool")
            inc(e.dma_start(out=fch[:],
                            in_=fc_hi_d.rearrange("(c p) d -> p c d", p=128)),
                "dsync", 16)
            inc(e.dma_start(out=fcl[:],
                            in_=fc_lo_d.rearrange("(c p) d -> p c d", p=128)),
                "dsync", 16)
            inc(e.dma_start(out=bfc_t[:], in_=bfc_rep[:]), "dsync", 16)
        go("sync", f_fl)
        fdv = mk.led["dsync"]

        hist4 = {}
        for tt in range(16):
            dxs = "dx0" if tt % 2 == 0 else "dx1"

            def f_xl(e, tt=tt, dxs=dxs):
                if tt >= 2:
                    wait(e, "sync", "vec", hist4[tt - 2])
                else:
                    wait(e, "sync", "vec")
                inc(e.dma_start(out=xres_t[tt % 2][:], in_=xres[ts(tt, 128), :]),
                    dxs, 16)
            go("sync", f_xl)
            dv = mk.led[dxs]

            def f_fm(e, tt=tt):
                wait(e, "tensor", "dsync", fdv)
                wait(e, "tensor", "vec")
                for half in range(2):
                    ii = mm3(e, P[2 * (tt % 2) + half][:, 0:512],
                             lambda k, w, tt=tt: (of_hi if w == "hi" else of_lo)[:, k, ts(tt, 128)],
                             lambda k, w, half=half: (fch if w == "hi" else fcl)[:, k, ts(half, 512)],
                             range(8))
                inc(ii, "pe")
            go("tensor", f_fm)
            pv = mk.led["pe"]

            def f_fe(e, tt=tt, pv=pv, dv=dv, dxs=dxs):
                wait(e, "vector", "pe", pv)
                wait(e, "vector", dxs, dv)
                e.tensor_tensor(brx_t[:], bfc_t[:], xres_t[tt % 2][:], ALU.add)
                e.drain()
                for half in range(2):
                    ii = e.tensor_tensor(y_tm[:, tt, ts(half, 512)],
                                         P[2 * (tt % 2) + half][:, 0:512],
                                         brx_t[:, ts(half, 512)], ALU.add)
                inc(ii, "vec")
            go("vector", f_fe)
            hist4[tt] = mk.led["vec"]
        tap("y_fc", lambda: y_tm[:])
        join()
    s34.close()
    join()

    comb = sb("comb", [128, 16, NE], F32)
    gwh = sb("gwh", [128, 8, NE], BF16)
    gwl = sb("gwl", [128, 8, NE], BF16)
    with contextlib.ExitStack() as sg:
        gwf = sg.enter_context(nc.sbuf_tensor("gwf", [128, 8, NE], F32))

        def f_gl(e):
            inc(e.dma_start(out=gwf[:],
                            in_=gate_wT.rearrange("(c p) d -> p c d", p=128)),
                "dsync", 16)
        go("sync", f_gl)

        def f_gc(e):
            wait(e, "scalar", "dsync")
            inc(e.copy(gwh[:], gwf[:]), "act")
        go("scalar", f_gc)

        def f_gv(e):
            wait(e, "vector", "act")
            inc(e.tensor_tensor(gwl[:], gwf[:], gwh[:], ALU.subtract), "vec")
        go("vector", f_gv)
        join()

    be_t = sb("be_t", [128, NE], F32)
    go("sync", lambda e: inc(e.dma_start(out=be_t[:], in_=be_rep[:]), "dsync", 16))

    for half in range(2):
        with contextlib.ExitStack() as s5:
            sbl = lambda n, sh, dt, half=half: s5.enter_context(
                nc.sbuf_tensor(f"{n}_{half}", sh, dt))
            xn2T_hi = sbl("xn2T_hi", [128, 8, 1024], BF16)
            xn2T_lo = sbl("xn2T_lo", [128, 8, 1024], BF16)
            x2f = sbl("x2f", [128, D], F32)
            x2h = sbl("x2h", [128, D], BF16)
            x2l = sbl("x2l", [128, D], BF16)
            st5 = sbl("st5", [128, 8], F32)
            sq5 = sbl("sq5", [128, D], F32)
            sco = sbl("sco", [128, NE], F32)
            r_t2 = sbl("r_t2", [128, 4], F32)
            r_tmp = sbl("r_tmp", [128, NE], F32)
            r_msk = sbl("r_msk", [128, NE], F32)
            r_cnt = sbl("r_cnt", [128, NE], F32)
            r_eq = sbl("r_eq", [128, NE], F32)
            r_a = sbl("r_a", [128, NE], F32)
            r_b = sbl("r_b", [128, NE], F32)
            rbuf = sbl("rbuf", [128, NE, NE], F32)
            rbuf2 = sbl("rbuf2", [128, NE, NE], F32)

            for ttl in range(8):
                tt = half * 8 + ttl

                def f_l2(e, tt=tt):
                    wait(e, "vector", "act")
                    inc(e.reduce_sum(st5[:, 0:1], y_tm[:, tt, :], axis=AX.X,
                                     negate=True), "vec")
                go("vector", f_l2)
                vv = mk.led["vec"]

                def f_l2a(e, tt=tt, vv=vv):
                    wait(e, "scalar", "vec", vv)
                    inc(e.activation(sq5[:], y_tm[:, tt, :], AF.Square, bias=0.0,
                                     scale=1.0, accum_out=st5[:, 1:2]), "act")
                go("scalar", f_l2a)

                def f_l2v(e, tt=tt):
                    wait(e, "vector", "act")
                    e.drain()
                    e.tensor_scalar_mul(st5[:, 2:3], st5[:, 0:1], 1.0 / D)
                    e.drain()
                    e.tensor_tensor(st5[:, 3:4], st5[:, 2:3], st5[:, 2:3],
                                    ALU.mult)
                    e.tensor_scalar_mul(st5[:, 4:5], st5[:, 1:2], 1.0 / D)
                    e.drain()
                    e.tensor_tensor(st5[:, 0:1], st5[:, 4:5], st5[:, 3:4],
                                    ALU.subtract)
                    e.drain()
                    inc(e.tensor_scalar_add(st5[:, 0:1], st5[:, 0:1], 1e-5),
                        "vec")
                go("vector", f_l2v)

                def f_l2r(e, tt=tt):
                    wait(e, "scalar", "vec")
                    inc(e.activation(st5[:, 5:6], st5[:, 0:1], AF.Sqrt,
                                     bias=0.0, scale=1.0), "act")
                go("scalar", f_l2r)

                def f_l2b(e, tt=tt):
                    wait(e, "vector", "act")
                    e.reciprocal(st5[:, 1:2], st5[:, 5:6])
                    e.drain()
                    e.tensor_tensor(st5[:, 6:7], st5[:, 2:3], st5[:, 1:2],
                                    ALU.mult)
                    inc(e.drain(), "vec")
                go("vector", f_l2b)

                def f_l2x(e, tt=tt):
                    wait(e, "vector", "vec")
                    inc(e.tensor_scalar(x2f[:], y_tm[:, tt, :], st5[:, 1:2],
                                        st5[:, 6:7], ALU.mult, ALU.add), "vec")
                go("vector", f_l2x)

                def f_l2h(e, tt=tt):
                    wait(e, "scalar", "vec")
                    wait(e, "scalar", "pe")
                    inc(e.copy(x2h[:], x2f[:]), "act")
                go("scalar", f_l2h)
                av = mk.led["act"]

                def f_l2l(e, tt=tt, av=av):
                    wait(e, "vector", "act", av)
                    wait(e, "vector", "pe")
                    e.drain()
                    inc(e.tensor_tensor(x2l[:], x2f[:], x2h[:], ALU.subtract),
                        "vec")
                go("vector", f_l2l)
                vv = mk.led["vec"]

                for k in range(8):
                    def f_l2t(e, k=k):
                        wait(e, "tensor", "vec")
                        wait(e, "tensor", "act")
                        e.transpose(Pb[k % 2][:, 0:128], x2h[:, ts(k, 128)],
                                    identb[:])
                        inc(e.transpose(Pb[k % 2][:, 128:256],
                                        x2l[:, ts(k, 128)], identb[:]), "pe")
                    go("tensor", f_l2t)
                    pv = mk.led["pe"]

                    def f_l2tc(e, k=k, ttl=ttl, pv=pv):
                        wait(e, "vector", "pe", pv)
                        e.tensor_copy(xn2T_hi[:, k, ts(ttl, 128)],
                                      Pb[k % 2][:, 0:128])
                        inc(e.tensor_copy(xn2T_lo[:, k, ts(ttl, 128)],
                                          Pb[k % 2][:, 128:256]), "vec")
                    go("vector", f_l2tc)
                vv = mk.led["vec"]

                def f_rl(e, ttl=ttl, vv=vv):
                    wait(e, "tensor", "vec", vv)
                    wait(e, "tensor", "act")
                    ii = mm3(e, P[4][:, 0:NE],
                             lambda k, w, ttl=ttl: (xn2T_hi if w == "hi" else xn2T_lo)[:, k, ts(ttl, 128)],
                             lambda k, w: (gwh if w == "hi" else gwl)[:, k, :],
                             range(8))
                    inc(ii, "pe")
                go("tensor", f_rl)
                pv = mk.led["pe"]

                def f_rs(e, tt=tt, pv=pv):
                    wait(e, "scalar", "pe", pv)
                    inc(e.activation(sco[:], P[4][:, 0:NE], AF.Sigmoid,
                                     bias=0.0, scale=1.0), "act")
                go("scalar", f_rs)

                def f_rt(e, tt=tt):
                    wait(e, "vector", "act")
                    wait(e, "vector", "dsync")
                    e.drain()
                    e.tensor_tensor(r_tmp[:], sco[:], be_t[:], ALU.add)
                    e.drain()
                    g = r_tmp[:].rearrange("p (g i) -> p g i", i=4)
                    pairs = [(a_, b_) for a_ in range(4) for b_ in range(a_ + 1, 4)]
                    for pi, (a_, b_) in enumerate(pairs):
                        e.tensor_tensor(rbuf[:, pi, 0:4], g[:, :, a_],
                                        g[:, :, b_], ALU.add)
                    e.drain()
                    e.tensor_reduce(r_a[:, 0:4],
                                    rbuf[:, 0:6, 0:4].rearrange("p j g -> p g j"),
                                    axis=AX.X, op=ALU.max)
                    e.drain()
                    e.memset(rbuf[:], 0.0)
                    e.drain()
                    for j in range(4):
                        e.tensor_scalar(rbuf[:, j, 0:4], r_a[:, 0:4],
                                        r_a[:, j:j + 1], None, ALU.is_ge)
                        if j < 3:
                            e.tensor_scalar(rbuf[:, 4 + j, j + 1:4],
                                            r_a[:, j + 1:4], r_a[:, j:j + 1],
                                            None, ALU.is_equal)
                    e.drain()
                    e.tensor_reduce(r_b[:, 0:4],
                                    rbuf[:, 0:4, 0:4].rearrange("p j g -> p g j"),
                                    axis=AX.X, op=ALU.add)
                    e.tensor_reduce(r_b[:, 4:8],
                                    rbuf[:, 4:7, 0:4].rearrange("p j g -> p g j"),
                                    axis=AX.X, op=ALU.add)
                    e.drain()
                    e.tensor_tensor(r_a[:, 4:8], r_b[:, 0:4], r_b[:, 4:8],
                                    ALU.subtract)
                    e.drain()
                    e.tensor_scalar(r_b[:, 0:4], r_a[:, 4:8], 3.0, None,
                                    ALU.is_ge)
                    e.drain()
                    for i_ in range(4):
                        e.tensor_copy(r_a[:].rearrange("p (g i) -> p g i", i=4)[:, :, i_],
                                      r_b[:, 0:4])
                    e.drain()
                    e.tensor_tensor(r_msk[:], r_tmp[:], r_a[:], ALU.mult)
                    e.drain()
                    e.memset(rbuf[:], 0.0)
                    e.memset(rbuf2[:], 0.0)
                    e.drain()
                    for j in range(NE):
                        e.tensor_scalar(rbuf[:, j, :], r_msk[:],
                                        r_msk[:, j:j + 1], None, ALU.is_ge)
                        if j < NE - 1:
                            e.tensor_scalar(rbuf2[:, j, j + 1:], r_msk[:, j + 1:],
                                            r_msk[:, j:j + 1], None, ALU.is_equal)
                    e.drain()
                    e.tensor_reduce(r_a[:], rbuf[:].rearrange("p j g -> p g j"),
                                    axis=AX.X, op=ALU.add)
                    e.tensor_reduce(r_b[:], rbuf2[:].rearrange("p j g -> p g j"),
                                    axis=AX.X, op=ALU.add)
                    e.drain()
                    e.tensor_tensor(r_cnt[:], r_a[:], r_b[:], ALU.subtract)
                    e.drain()
                    e.tensor_scalar(r_eq[:], r_cnt[:], 13.0, None, ALU.is_ge)
                    e.drain()
                    e.tensor_tensor(r_a[:], sco[:], r_eq[:], ALU.mult)
                    e.drain()
                    e.reduce_sum(st5[:, 7:8], r_a[:], axis=AX.X)
                    e.drain()
                    e.tensor_scalar_add(st5[:, 7:8], st5[:, 7:8], 1e-20)
                    e.drain()
                    e.reciprocal(st5[:, 3:4], st5[:, 7:8])
                    e.drain()
                    inc(e.tensor_scalar(comb[:, tt, :], r_a[:], st5[:, 3:4],
                                        2.5, ALU.mult, ALU.mult), "vec")
                go("vector", f_rt)

            uph = [sbl(f"uph{i}", [128, 8, MI], BF16) for i in range(2)]
            upl = [sbl(f"upl{i}", [128, 8, MI], BF16) for i in range(2)]
            dnh = [sbl(f"dnh{i}", [128, 4, D], BF16) for i in range(2)]
            dnl = [sbl(f"dnl{i}", [128, 4, D], BF16) for i in range(2)]
            ytmp = sbl("ytmp", [128, D], F32)
            hrel = sbl("hrel", [128, 4, 512], F32)
            hpro = sbl("hpro", [128, 4, 512], F32)
            hh = sbl("hh", [128, 4, 512], BF16)
            hlo = sbl("hlo", [128, 4, 512], BF16)
            if half == 0:
                tap("xn2T_hi", lambda: xn2T_hi[:])
                tap("comb", lambda: comb[:])
            uhist = {}
            for u in range(NU):
                bi = u % 2

                dus = "du0" if bi == 0 else "du1"

                def f_uw(e, u=u, bi=bi, dus=dus):
                    if u >= 2:
                        wait(e, "sync", "pe", uhist[u - 2])
                    else:
                        wait(e, "sync", "pe")
                    wait(e, "sync", "dpool")
                    inc(e.dma_start(out=uph[bi][:],
                                    in_=up_hi_d[u].rearrange("(c p) m -> p c m", p=128)),
                        dus, 16)
                    inc(e.dma_start(out=upl[bi][:],
                                    in_=up_lo_d[u].rearrange("(c p) m -> p c m", p=128)),
                        dus, 16)
                    inc(e.dma_start(out=dnh[bi][:],
                                    in_=dn_hi_d[u].rearrange("(c p) d -> p c d", p=128)),
                        dus, 16)
                    inc(e.dma_start(out=dnl[bi][:],
                                    in_=dn_lo_d[u].rearrange("(c p) d -> p c d", p=128)),
                        dus, 16)
                go("sync", f_uw)
                udv = mk.led[dus]

                for tc in range(2):
                    def f_up(e, u=u, bi=bi, tc=tc, udv=udv, dus=dus):
                        wait(e, "tensor", dus, udv)
                        wait(e, "tensor", "act")
                        wait(e, "tensor", "pool")
                        for it in range(4):
                            ii = mm3(e, P[it][:, 0:512],
                                     lambda k, w, bi=bi, it=it: (uph[bi] if w == "hi" else upl[bi])[:, k, ts(it, 128)],
                                     lambda k, w, tc=tc: (xn2T_hi if w == "hi" else xn2T_lo)[:, k, ts(tc, 512)],
                                     range(8))
                        inc(ii, "pe")
                    go("tensor", f_up)
                    pv = mk.led["pe"]

                    def f_hr(e, pv=pv):
                        wait(e, "scalar", "pe", pv)
                        wait(e, "scalar", "vec")
                        for it in range(4):
                            ii = e.activation(hrel[:, it, :], P[it][:, 0:512],
                                              AF.Relu, bias=0.0, scale=1.0)
                        inc(ii, "act")
                    go("scalar", f_hr)
                    av = mk.led["act"]

                    def f_hp(e, av=av):
                        wait(e, "vector", "act", av)
                        wait(e, "vector", "pool")
                        inc(e.scalar_tensor_tensor(hpro[:], hrel[:], 1.0,
                                                   hrel[:], ALU.mult, ALU.mult),
                            "vec")
                    go("vector", f_hp)
                    vv = mk.led["vec"]

                    def f_hh(e, vv=vv):
                        wait(e, "scalar", "vec", vv)
                        wait(e, "scalar", "pe")
                        inc(e.copy(hh[:], hpro[:]), "act")
                    go("scalar", f_hh)
                    av = mk.led["act"]

                    def f_hl(e, av=av):
                        wait(e, "gpsimd", "act", av)
                        wait(e, "gpsimd", "pe")
                        inc(e.tensor_tensor(hlo[:], hpro[:], hh[:], ALU.subtract),
                            "pool")
                    go("gpsimd", f_hl)
                    plv = mk.led["pool"]

                    for tsub in range(4):
                        gt = half * 8 + tc * 4 + tsub

                        def f_dn(e, u=u, bi=bi, tsub=tsub, plv=plv):
                            wait(e, "tensor", "pool", plv)
                            wait(e, "tensor", "vec")
                            for h2 in range(2):
                                ii = mm3(e, P[4 + h2][:, 0:512],
                                         lambda k, w, tsub=tsub: (hh if w == "hi" else hlo)[:, k, ts(tsub, 128)],
                                         lambda k, w, bi=bi, h2=h2: (dnh[bi] if w == "hi" else dnl[bi])[:, k, ts(h2, 512)],
                                         range(4))
                            inc(ii, "pe")
                        go("tensor", f_dn)
                        pv = mk.led["pe"]

                        def f_ya(e, u=u, gt=gt, tsub=tsub, pv=pv):
                            wait(e, "vector", "pe", pv)
                            for h2 in range(2):
                                ps = P[4 + h2][:, 0:512]
                                if u < NE:
                                    e.scalar_tensor_tensor(
                                        ytmp[:, ts(h2, 512)], ps,
                                        comb[:, gt, u:u + 1],
                                        y_tm[:, gt, ts(h2, 512)], ALU.mult,
                                        ALU.add)
                                else:
                                    e.tensor_tensor(
                                        ytmp[:, ts(h2, 512)], ps,
                                        y_tm[:, gt, ts(h2, 512)], ALU.add)
                            e.drain()
                            inc(e.tensor_copy(y_tm[:, gt, :], ytmp[:]), "vec")
                        go("vector", f_ya)
                uhist[u] = mk.led["pe"]
            join()

    def f_out(e):
        for s in SEMS:
            wait(e, "sync", s)
        inc(e.dma_start(out=yout.rearrange("(c p) d -> p c d", p=128),
                        in_=y_tm[:]), "dsync", 16)
        wait(e, "sync", "dsync")
    go("sync", f_out)

    es.close()
    return nc


_NC_CACHE = [None]


def stage_inputs(L0_attn_x, L0_attn_w_qkv, L0_attn_w_conv, L0_attn_w_fc,
                 L0_attn_b_fc, L0_ffn_gate_w, L0_ffn_bias_e, L0_ffn_up_w,
                 L0_ffn_down_w, L0_ffn_shared_up, L0_ffn_shared_down):
    x = np.ascontiguousarray(np.asarray(L0_attn_x, dtype=np.float32))
    w_qkv = np.asarray(L0_attn_w_qkv, dtype=np.float32)
    w_conv = np.asarray(L0_attn_w_conv, dtype=np.float32)[:, 0, :, 0]
    w_fc = np.ascontiguousarray(np.asarray(L0_attn_w_fc, dtype=np.float32))
    b_fc = np.asarray(L0_attn_b_fc, dtype=np.float32)
    gate_w = np.asarray(L0_ffn_gate_w, dtype=np.float32)
    bias_e = np.asarray(L0_ffn_bias_e, dtype=np.float32)
    up_w = np.asarray(L0_ffn_up_w, dtype=np.float32)
    down_w = np.asarray(L0_ffn_down_w, dtype=np.float32)
    shared_up = np.asarray(L0_ffn_shared_up, dtype=np.float32)
    shared_down = np.asarray(L0_ffn_shared_down, dtype=np.float32)

    import ml_dtypes
    identf = np.eye(128, dtype=np.float32)
    identb = np.eye(128, dtype=np.float32).astype(ml_dtypes.bfloat16)
    bfc_rep = np.ascontiguousarray(np.broadcast_to(b_fc, (128, D)))
    be_rep = np.ascontiguousarray(np.broadcast_to(bias_e, (128, NE)))
    gate_wT = np.ascontiguousarray(gate_w.T)
    up_all = np.empty((NU, D, MI), dtype=np.float32)
    dn_all = np.empty((NU, MI, D), dtype=np.float32)
    for e in range(NE):
        up_all[e] = up_w[e].T
        dn_all[e] = down_w[e].T
    for u in range(4):
        up_all[NE + u] = shared_up[ts(u, 512), :].T
        dn_all[NE + u] = shared_down[:, ts(u, 512)].T

    in_maps = []
    for c in range(8):
        b, hg = c // 2, c % 2
        wq = np.empty((HL, D, 3 * DH), dtype=np.float32)
        for hll in range(HL):
            h = 8 * hg + hll
            wq[hll, :, 0:64] = w_qkv[:, ts(h, 64)]
            wq[hll, :, 64:128] = w_qkv[:, 1024 + 64 * h:1024 + 64 * h + 64]
            wq[hll, :, 128:192] = w_qkv[:, 2048 + 64 * h:2048 + 64 * h + 64]
        wcol_c = np.ascontiguousarray(
            np.repeat(w_conv[ts(hg, 8), :], 64, axis=0))
        in_maps.append({
            "xT": np.ascontiguousarray(x[b].T),
            "xtm": np.ascontiguousarray(x[b]),
            "xres": np.ascontiguousarray(
                x[:, ts(c, 512), :].reshape(T, D)),
            "wqkv": wq,
            "wcol": wcol_c,
            "wfc": w_fc,
            "bfc_rep": bfc_rep,
            "gate_wT": gate_wT,
            "be_rep": be_rep,
            "up_wT": up_all,
            "down_wT": dn_all,
            "identf": identf,
            "identb": identb,
        })
    return in_maps


def kernel(**inputs):
    in_maps = stage_inputs(**inputs)
    if _NC_CACHE[0] is None:
        _NC_CACHE[0] = build_program()
    nc = _NC_CACHE[0]
    res = run_bass_kernel_spmd(nc, in_maps, core_ids=list(range(8)))
    out = np.zeros((BT, N, D), dtype=np.float32)
    for c in range(8):
        yc = np.asarray(res.results[c]["y"])
        for b in range(BT):
            out[b, ts(c, 512), :] = yc[ts(b, 512), :]
    return out


# version 09
# speedup vs baseline: 1.0344x; 1.0344x over previous
import contextlib
import numpy as np

import concourse.bass as bass
import concourse.mybir as mybir
from concourse.bass_utils import run_bass_kernel_spmd
from concourse.aot_env import allow_ici_all_to_all

F32 = mybir.dt.float32
BF16 = mybir.dt.bfloat16
AF = mybir.ActivationFunctionType
ALU = mybir.AluOpType
AX = mybir.AxisListType

D, N, BT, HL, DH, M, ITERS, KC = 1024, 4096, 4, 8, 64, 256, 6, 13
T, NE, NU, MI = 2048, 16, 20, 512
SCALE = 0.125

ENGS = ("tensor", "vector", "scalar", "gpsimd", "sync")
SEMS = ("pe", "vec", "act", "pool", "dsync", "dpool", "cc",
        "dx0", "dx1", "dw", "d3", "d4", "du0", "du1")


def ts(i, n):
    return slice(i * n, (i + 1) * n)


class MK:

    def __init__(self, nc, es, blk):
        self.nc, self.es, self.blk = nc, es, blk
        self.led = {s: 0 for s in SEMS}
        self.sem = {s: es.enter_context(nc.semaphore(s)) for s in SEMS}
        self.lw = {s: {} for s in SEMS}

    def go(self, eng, fn):
        getattr(self.blk, eng)(fn)

    def inc(self, instr, s, n=1):
        instr.then_inc(self.sem[s], n)
        self.led[s] += n
        return instr

    def wait(self, e, ename, s, val=None):
        v = self.led[s] if val is None else val
        if v <= 0 or self.lw[s].get(ename, -1) >= v:
            return
        e.wait_ge(self.sem[s], v)
        self.lw[s][ename] = v

    def join(self):
        snap = dict(self.led)
        for eng in ENGS:
            def f(e, eng=eng, snap=snap):
                for s in SEMS:
                    self.wait(e, eng, s, snap[s])
            self.go(eng, f)

    def sb(self, name, shape, dt):
        return self.es.enter_context(self.nc.sbuf_tensor(name, shape, dt))


def mm3(e, out_ap, lhs, rhs, ks, start=True, stop=True):
    seq = (("hi", "hi"), ("hi", "lo"), ("lo", "hi"))
    tot = len(seq) * len(ks)
    i = 0
    ii = None
    for lw, rw in seq:
        for k in ks:
            ii = e.matmul(out_ap, lhs(k, lw), rhs(k, rw),
                          start=(start and i == 0), stop=(stop and i == tot - 1))
            i += 1
    return ii


def mmf(e, outs, lhsT, rhs, start=True, stop=True):
    ii = None
    for ot in range(2):
        for k in range(2):
            ii = e.matmul(outs[ot], lhsT[:, k, ts(ot, 128)], rhs[:, k, :],
                          start=(start and k == 0), stop=(stop and k == 1))
    return ii


def build_program(dbg=(), stop=99):
    nc = bass.Bass("TRN2", target_bir_lowering=False, debug=False)
    es = contextlib.ExitStack()

    xT = nc.dram_tensor("xT", [D, N], F32, kind="ExternalInput").ap()
    xtm = nc.dram_tensor("xtm", [N, D], F32, kind="ExternalInput").ap()
    xres = nc.dram_tensor("xres", [T, D], F32, kind="ExternalInput").ap()
    wqkv = nc.dram_tensor("wqkv", [HL, D, 3 * DH], F32, kind="ExternalInput").ap()
    wcol = nc.dram_tensor("wcol", [8 * DH, KC], F32, kind="ExternalInput").ap()
    wfc = nc.dram_tensor("wfc", [D, D], F32, kind="ExternalInput").ap()
    bfc_rep = nc.dram_tensor("bfc_rep", [128, D], F32, kind="ExternalInput").ap()
    gate_wT = nc.dram_tensor("gate_wT", [D, NE], F32, kind="ExternalInput").ap()
    be_rep = nc.dram_tensor("be_rep", [128, NE], F32, kind="ExternalInput").ap()
    up_wT = nc.dram_tensor("up_wT", [NU, D, MI], F32, kind="ExternalInput").ap()
    down_wT = nc.dram_tensor("down_wT", [NU, MI, D], F32, kind="ExternalInput").ap()
    identf_d = nc.dram_tensor("identf", [128, 128], F32, kind="ExternalInput").ap()
    identb_d = nc.dram_tensor("identb", [128, 128], BF16, kind="ExternalInput").ap()
    yout = nc.dram_tensor("y", [T, D], F32, kind="ExternalOutput").ap()

    wq_hi_d = nc.dram_tensor("wq_hi_d", [HL, D, 3 * DH], BF16).ap()
    wq_lo_d = nc.dram_tensor("wq_lo_d", [HL, D, 3 * DH], BF16).ap()
    up_hi_d = nc.dram_tensor("up_hi_d", [NU, D, MI], BF16).ap()
    up_lo_d = nc.dram_tensor("up_lo_d", [NU, D, MI], BF16).ap()
    dn_hi_d = nc.dram_tensor("dn_hi_d", [NU, MI, D], BF16).ap()
    dn_lo_d = nc.dram_tensor("dn_lo_d", [NU, MI, D], BF16).ap()
    fc_hi_d = nc.dram_tensor("fc_hi_d", [D, D], BF16).ap()
    fc_lo_d = nc.dram_tensor("fc_lo_d", [D, D], BF16).ap()
    a2a_in = nc.dram_tensor("a2a_in", [8, 512, 512], F32)
    a2a_out = nc.dram_tensor("a2a_out", [8, 512, 512], F32)

    blk = es.enter_context(nc.Block())
    mk = MK(nc, es, blk)
    go, inc, wait, join, sb = mk.go, mk.inc, mk.wait, mk.join, mk.sb

    def tap(name, ap_fn):
        if name not in dbg:
            return
        ap = ap_fn() if callable(ap_fn) else ap_fn
        od = nc.dram_tensor("dbg_" + name, list(ap.shape),
                            ap.dtype, kind="ExternalOutput").ap()
        snap = dict(mk.led)

        def f(e):
            for s in SEMS:
                wait(e, "sync", s, snap[s])
            inc(e.dma_start(out=od, in_=ap), "dsync", 16)
        go("sync", f)

    identf = sb("identf_t", [128, 128], F32)
    identb = sb("identb_t", [128, 128], BF16)
    ones_bf = sb("ones_bf", [1, 128], BF16)
    ones_f = sb("ones_f", [1, 128], F32)
    onescol_f = sb("onescol_f", [128, 1], F32)
    onescol_bf = sb("onescol_bf", [128, 1], BF16)
    wcol_t = sb("wcol_t", [128, 4, KC], F32)
    sc2 = contextlib.ExitStack()
    sc = lambda n, sh, dt: sc2.enter_context(nc.sbuf_tensor(n, sh, dt))
    xnT_hi = sc("xnT_hi", [128, 8, N], BF16)
    xnT_lo = sc("xnT_lo", [128, 8, N], BF16)
    xm_hi = sc("xm_hi", [128, 8, M], BF16)
    xm_lo = sc("xm_lo", [128, 8, M], BF16)
    P = [es.enter_context(nc.psum_tensor(f"P{i}", [128, 512], F32)) for i in range(6)]
    Pb = [es.enter_context(nc.psum_tensor(f"Pb{i}", [128, 1024], BF16)) for i in range(2)]

    WSP = 1024

    def flat(ap_, nelem):
        names = "abcd"[:len(ap_.shape)]
        pat = f"{' '.join(names)} -> ({' '.join(names)})"
        return ap_.rearrange(pat).rearrange("(p f) -> p f", p=128), nelem // 128

    wsplit_jobs = []
    for (src, hid, lod, nel) in ((wfc, fc_hi_d, fc_lo_d, D * D),
                                 (up_wT, up_hi_d, up_lo_d, NU * D * MI),
                                 (down_wT, dn_hi_d, dn_lo_d, NU * MI * D)):
        fs, per = flat(src, nel)
        fh, _ = flat(hid, nel)
        fl, _ = flat(lod, nel)
        for c0 in range(0, per, WSP):
            c1 = min(c0 + WSP, per)
            wsplit_jobs.append((fs[:, c0:c1], fh[:, c0:c1], fl[:, c0:c1]))
    wsp_f = sc("wsp_f", [128, WSP], F32)
    wsp_h = sc("wsp_h", [128, WSP], BF16)
    wsp_l = sc("wsp_l", [128, WSP], BF16)
    wsp_i = [0]

    def emit_wsplit(kn):
        for _ in range(kn):
            if wsp_i[0] >= len(wsplit_jobs):
                return
            src, hid, lod = wsplit_jobs[wsp_i[0]]
            wsp_i[0] += 1

            def f_load(e, src=src):
                wait(e, "sync", "dpool")
                inc(e.dma_start(out=wsp_f[:], in_=src), "d3", 16)
            go("sync", f_load)

            def f_cast(e):
                wait(e, "gpsimd", "d3")
                wait(e, "gpsimd", "dpool")
                inc(e.dma_start(out=wsp_h[:], in_=wsp_f[:]), "dpool", 16)
            go("gpsimd", f_cast)

            def f_lo(e):
                wait(e, "vector", "dpool")
                inc(e.tensor_tensor(wsp_l[:], wsp_f[:], wsp_h[:], ALU.subtract),
                    "vec")
            go("vector", f_lo)

            def f_store(e, hid=hid, lod=lod):
                wait(e, "gpsimd", "vec")
                inc(e.dma_start(out=hid, in_=wsp_h[:]), "dpool", 16)
                inc(e.dma_start(out=lod, in_=wsp_l[:]), "dpool", 16)
            go("gpsimd", f_store)

    def p0_sync(e):
        inc(e.dma_start(out=identf[:], in_=identf_d[:]), "dsync", 16)
        inc(e.dma_start(out=identb[:], in_=identb_d[:]), "dsync", 16)
        inc(e.dma_start(out=wcol_t[:],
                        in_=wcol.rearrange("(c p) k -> p c k", p=128)),
            "dsync", 16)
    go("sync", p0_sync)

    def p0_pool(e):
        e.memset(ones_bf[:], 1.0)
        e.memset(ones_f[:], 1.0)
        e.memset(onescol_bf[:], 1.0)
        inc(e.memset(onescol_f[:], 1.0), "pool")
    go("gpsimd", p0_pool)

    wq_flat, _ = flat(wqkv, HL * D * 3 * DH)
    wqh_flat, _ = flat(wq_hi_d, HL * D * 3 * DH)
    wql_flat, _ = flat(wq_lo_d, HL * D * 3 * DH)
    with contextlib.ExitStack() as s0:
        wq_f = s0.enter_context(nc.sbuf_tensor("wq_f", [128, 6144], F32))
        wq_h = s0.enter_context(nc.sbuf_tensor("wq_h", [128, 6144], BF16))
        wq_l = s0.enter_context(nc.sbuf_tensor("wq_l", [128, 6144], BF16))
        for ih in range(2):
            sl = ts(ih, 6144)

            def p0_ld(e, sl=sl):
                wait(e, "sync", "dpool")
                inc(e.dma_start(out=wq_f[:], in_=wq_flat[:, sl]), "dsync", 16)
            go("sync", p0_ld)

            def p0_act(e):
                wait(e, "scalar", "dsync")
                inc(e.copy(wq_h[:], wq_f[:]), "act")
            go("scalar", p0_act)

            def p0_vec(e):
                wait(e, "vector", "act")
                inc(e.tensor_tensor(wq_l[:], wq_f[:], wq_h[:], ALU.subtract),
                    "vec")
            go("vector", p0_vec)

            def p0_pool2(e, sl=sl):
                wait(e, "gpsimd", "vec")
                inc(e.dma_start(out=wqh_flat[:, sl], in_=wq_h[:]), "dpool", 16)
                inc(e.dma_start(out=wql_flat[:, sl], in_=wq_l[:]), "dpool", 16)
            go("gpsimd", p0_pool2)

        def p0_vec2(e):
            wait(e, "vector", "dsync")
            inc(e.tensor_scalar_add(wcol_t[:, :, 6:7], wcol_t[:, :, 6:7], 1.0),
                "vec")
        go("vector", p0_vec2)
        join()

    with contextlib.ExitStack() as s1:
        sbl = lambda n, sh, dt: s1.enter_context(nc.sbuf_tensor(n, sh, dt))
        nst = sbl("nst", [128, 32], F32)
        sqst = sbl("sqst", [128, 32], F32)
        istd = sbl("istd", [128, 32], F32)
        bist = sbl("bist", [128, 32], F32)
        tst = sbl("tst", [128, 32], F32)
        sq_scr = sbl("sq_scr", [128, 1024], F32)
        xtm_t = [sbl(f"xtm_t{i}", [128, 1024], F32) for i in range(2)]
        x2f = sbl("x2f", [128, 1024], F32)
        x2h = sbl("x2h", [128, 1024], BF16)
        x2l = sbl("x2l", [128, 1024], BF16)
        hist = {}
        for i in range(32):
            dxs = "dx0" if i % 2 == 0 else "dx1"

            def f_load(e, i=i, dxs=dxs):
                if i >= 2:
                    wait(e, "sync", "act", hist[i - 2][0])
                    wait(e, "sync", "vec", hist[i - 2][1])
                inc(e.dma_start(out=xtm_t[i % 2][:], in_=xtm[ts(i, 128), :]),
                    dxs, 16)
            go("sync", f_load)
            dv = mk.led[dxs]

            def f_vec(e, i=i, dv=dv, dxs=dxs):
                wait(e, "vector", dxs, dv)
                inc(e.reduce_sum(nst[:, i:i + 1], xtm_t[i % 2][:], axis=AX.X,
                                 negate=True), "vec")
            go("vector", f_vec)

            def f_act(e, i=i, dv=dv, dxs=dxs):
                wait(e, "scalar", dxs, dv)
                wait(e, "scalar", "act")
                inc(e.activation(sq_scr[:], xtm_t[i % 2][:], AF.Square,
                                 bias=0.0, scale=1.0,
                                 accum_out=sqst[:, i:i + 1]), "act")
            go("scalar", f_act)
            hist[i] = (mk.led["act"], mk.led["vec"])

        varr = sbl("varr", [128, 32], F32)
        sdev = sbl("sdev", [128, 32], F32)

        def f_stats(e):
            wait(e, "vector", "act")
            e.drain()
            e.tensor_scalar_mul(nst[:], nst[:], 1.0 / D)
            e.drain()
            e.tensor_tensor(tst[:], nst[:], nst[:], ALU.mult)
            e.tensor_scalar_mul(sdev[:], sqst[:], 1.0 / D)
            e.drain()
            e.tensor_tensor(varr[:], sdev[:], tst[:], ALU.subtract)
            e.drain()
            inc(e.tensor_scalar_add(varr[:], varr[:], 1e-5), "vec")
        go("vector", f_stats)

        def f_sqrt(e):
            wait(e, "scalar", "vec")
            inc(e.activation(sdev[:], varr[:], AF.Sqrt, bias=0.0, scale=1.0),
                "act")
        go("scalar", f_sqrt)

        def f_recip(e):
            wait(e, "vector", "act")
            e.reciprocal(istd[:], sdev[:])
            e.drain()
            e.tensor_tensor(bist[:], nst[:], istd[:], ALU.mult)
            inc(e.drain(), "vec")
        go("vector", f_recip)
        tap("istd", lambda: istd[:])
        tap("bist", lambda: bist[:])

        hist2 = {}
        for i in range(32):
            dxs = "dx0" if i % 2 == 0 else "dx1"

            def f_load2(e, i=i, dxs=dxs):
                if i >= 2:
                    wait(e, "sync", "vec", hist2[i - 2])
                else:
                    wait(e, "sync", "vec")
                    wait(e, "sync", "act")
                inc(e.dma_start(out=xtm_t[i % 2][:], in_=xtm[ts(i, 128), :]),
                    dxs, 16)
            go("sync", f_load2)
            dv = mk.led[dxs]

            def f_ap(e, i=i, dv=dv, dxs=dxs):
                wait(e, "vector", dxs, dv)
                wait(e, "vector", "act")
                inc(e.tensor_scalar(x2f[:], xtm_t[i % 2][:], istd[:, i:i + 1],
                                    bist[:, i:i + 1], ALU.mult, ALU.add), "vec")
            go("vector", f_ap)
            vv = mk.led["vec"]
            hist2[i] = vv

            def f_hi(e, vv=vv):
                wait(e, "scalar", "vec", vv)
                wait(e, "scalar", "pe")
                inc(e.copy(x2h[:], x2f[:]), "act")
            go("scalar", f_hi)
            av = mk.led["act"]

            def f_lo(e, av=av):
                wait(e, "vector", "act", av)
                wait(e, "vector", "pe")
                e.drain()
                inc(e.tensor_tensor(x2l[:], x2f[:], x2h[:], ALU.subtract), "vec")
            go("vector", f_lo)

            for k in range(8):
                def f_tp(e, k=k):
                    wait(e, "tensor", "vec")
                    wait(e, "tensor", "act")
                    e.transpose(Pb[k % 2][:, 0:128], x2h[:, ts(k, 128)],
                                identb[:])
                    inc(e.transpose(Pb[k % 2][:, 128:256], x2l[:, ts(k, 128)],
                                    identb[:]), "pe")
                go("tensor", f_tp)
                pv = mk.led["pe"]

                def f_tpc(e, k=k, i=i, pv=pv):
                    wait(e, "vector", "pe", pv)
                    e.tensor_copy(xnT_hi[:, k, ts(i, 128)], Pb[k % 2][:, 0:128])
                    inc(e.tensor_copy(xnT_lo[:, k, ts(i, 128)],
                                      Pb[k % 2][:, 128:256]), "vec")
                go("vector", f_tpc)

        xm_f = sbl("xm_f", [128, 8, M], F32)
        xm_t = sbl("xm_t", [128, M], F32)
        xm_t2 = sbl("xm_t2", [128, M], F32)
        for c in range(8):
            def f_xm(e, c=c):
                wait(e, "vector", "vec")
                e.tensor_reduce(xm_t[:],
                                xnT_hi[:, c, :].rearrange("p (m l) -> p m l", l=16),
                                axis=AX.X, op=ALU.add)
                e.tensor_reduce(xm_t2[:],
                                xnT_lo[:, c, :].rearrange("p (m l) -> p m l", l=16),
                                axis=AX.X, op=ALU.add)
                e.drain()
                inc(e.tensor_tensor(xm_f[:, c, :], xm_t[:], xm_t2[:],
                                    ALU.add), "vec")
            go("vector", f_xm)

        def f_xm_a(e):
            wait(e, "scalar", "vec")
            inc(e.activation(xm_hi[:], xm_f[:], AF.Copy, bias=0.0,
                             scale=1.0 / 16), "act")
        go("scalar", f_xm_a)

        def f_xm_v(e):
            wait(e, "vector", "act")
            inc(e.scalar_tensor_tensor(xm_lo[:], xm_f[:], 1.0 / 16, xm_hi[:],
                                       ALU.mult, ALU.subtract), "vec")
        go("vector", f_xm_v)
        join()
        tap("xnT_hi", lambda: xnT_hi[:])
        tap("xnT_lo", lambda: xnT_lo[:])
        tap("xm_hi", lambda: xm_hi[:])
        tap("xm_lo", lambda: xm_lo[:])
    if stop <= 1:
        def f_fin1(e):
            for s in SEMS:
                wait(e, "sync", s)
        go("sync", f_fin1)
        sc2.close()
        es.close()
        return nc

    kqT_hi = sc("kqT_hi", [64, N], BF16)
    kqT_lo = sc("kqT_lo", [64, N], BF16)
    qlT_hi = sc("qlT_hi", [64, M], BF16)
    qlT_lo = sc("qlT_lo", [64, M], BF16)
    klT_hi = sc("klT_hi", [64, M], BF16)
    klT_lo = sc("klT_lo", [64, M], BF16)
    zw_hi = sc("zw_hi", [128, 2, DH], BF16)
    zw_lo = sc("zw_lo", [128, 2, DH], BF16)
    whe_hi = sc("whe_hi", [128, 8, 192], BF16)
    whe_lo = sc("whe_lo", [128, 8, 192], BF16)
    wcol_h = sc("wcol_h", [DH, KC], F32)
    NSPL = (len(wsplit_jobs) + HL - 1) // HL

    for hl in range(HL):
        def f_wl(e, hl=hl):
            wait(e, "sync", "pe")
            wait(e, "sync", "dpool")
            inc(e.dma_start(out=whe_hi[:],
                            in_=wq_hi_d[hl].rearrange("(c p) k -> p c k", p=128)),
                "dw", 16)
            inc(e.dma_start(out=whe_lo[:],
                            in_=wq_lo_d[hl].rearrange("(c p) k -> p c k", p=128)),
                "dw", 16)
        go("sync", f_wl)
        whe_dv = mk.led["dw"]

        def f_wc(e, hl=hl):
            wait(e, "vector", "vec")
            inc(e.tensor_copy(wcol_h[:], wcol_t[ts(hl % 2, 64)][:, hl // 2, :]),
                "vec")
        go("vector", f_wc)

        with contextlib.ExitStack() as sA:
            sba = lambda n, sh, dt, hl=hl: sA.enter_context(
                nc.sbuf_tensor(f"{n}_{hl}", sh, dt))
            vsc_hi = sba("vsc_hi", [64, 512], BF16)
            vsc_lo = sba("vsc_lo", [64, 512], BF16)
            v_tm = sba("v_tm", [128, 32, 132], BF16)
            e3f = sba("e3f", [128, M], F32)
            e3h = sba("e3h", [128, M], BF16)
            e3l = sba("e3l", [128, M], BF16)
            aT = sba("aT", [128, 2, M], F32)
            zA = sba("zA", [128, 2, M], F32)
            zTA = sba("zTA", [128, 2, M], F32)
            zB = sba("zB", [128, 2, M], F32)
            zTB = sba("zTB", [128, 2, M], F32)
            uu = sba("uu", [128, 2, M], F32)
            uuT = sba("uuT", [128, 2, M], F32)
            xw = sba("xw", [128, 2, M], F32)
            e2s = sba("e2s", [128, 2, M], F32)
            srow = sba("srow", [1, 520], F32)
            srow_b = sba("srow_b", [1, 512], BF16)
            stat2 = sba("stat2", [128, 8], F32)
            mnb2 = sba("mnb2", [128, 2], BF16)
            mn3 = sba("mn3", [128, 2], F32)
            mr3t = sba("mr3t", [128, 2, 8], F32)
            mrow3 = sba("mrow3", [1, M], BF16)
            w3s = sba("w3s", [128, 2, 132], F32)
            w3v = sba("w3v", [128, 2, DH], F32)

            for tc in range(8):
                def f_mm(e, tc=tc):
                    wait(e, "tensor", "dw", whe_dv)
                    wait(e, "tensor", "vec")
                    wait(e, "tensor", "act")
                    ii = mm3(e, P[tc % 2][:, 0:512],
                             lambda k, w: (whe_hi if w == "hi" else whe_lo)[:, k, 64:192],
                             lambda k, w: (xnT_hi if w == "hi" else xnT_lo)[:, k, ts(tc, 512)],
                             range(8))
                    inc(ii, "pe")
                go("tensor", f_mm)
                pv = mk.led["pe"]

                def f_ev_a(e, tc=tc, pv=pv):
                    wait(e, "scalar", "pe", pv)
                    e.copy(kqT_hi[:, ts(tc, 512)], P[tc % 2][0:64, 0:512])
                    inc(e.copy(vsc_hi[:], P[tc % 2][64:128, 0:512]), "act")
                go("scalar", f_ev_a)
                av = mk.led["act"]

                def f_ev_v(e, tc=tc, pv=pv, av=av):
                    wait(e, "vector", "pe", pv)
                    wait(e, "vector", "act", av)
                    e.tensor_tensor(kqT_lo[:, ts(tc, 512)], P[tc % 2][0:64, 0:512],
                                    kqT_hi[:, ts(tc, 512)], ALU.subtract)
                    inc(e.tensor_tensor(vsc_lo[:], P[tc % 2][64:128, 0:512],
                                        vsc_hi[:], ALU.subtract), "vec")
                go("vector", f_ev_v)
                vv = mk.led["vec"]

                def f_tp(e, tc=tc, vv=vv):
                    wait(e, "tensor", "vec", vv)
                    for sub in range(4):
                        e.transpose(Pb[0][:, ts(sub, 64)], vsc_hi[:, ts(sub, 128)],
                                    identb[0:64, 0:64])
                    for sub in range(4):
                        ii = e.transpose(Pb[1][:, ts(sub, 64)],
                                         vsc_lo[:, ts(sub, 128)],
                                         identb[0:64, 0:64])
                    inc(ii, "pe")
                go("tensor", f_tp)
                pv2 = mk.led["pe"]

                def f_tpc(e, tc=tc, pv2=pv2):
                    wait(e, "vector", "pe", pv2)
                    for sub in range(4):
                        e.tensor_copy(v_tm[:, tc * 4 + sub, 0:64],
                                      Pb[0][:, ts(sub, 64)])
                    for sub in range(4):
                        ii = e.tensor_copy(v_tm[:, tc * 4 + sub, 64:128],
                                           Pb[1][:, ts(sub, 64)])
                    inc(ii, "vec")
                go("vector", f_tpc)
            go("gpsimd", lambda e: inc(e.memset(v_tm[:, :, 128:129], 1.0), "pool"))
            onev = mk.led["pool"]
            if hl == 0:
                tap("kT_hi", lambda: kqT_hi[:])
                tap("kT_lo", lambda: kqT_lo[:])
                tap("v_tm", lambda: v_tm[:])

            def f_lm(e):
                wait(e, "tensor", "vec")
                mm3(e, P[4][0:64, 0:256],
                    lambda k, w: (whe_hi if w == "hi" else whe_lo)[:, k, 0:64],
                    lambda k, w: (xm_hi if w == "hi" else xm_lo)[:, k, :],
                    range(8))
                ii = mm3(e, P[4][0:64, 256:512],
                         lambda k, w: (whe_hi if w == "hi" else whe_lo)[:, k, 64:128],
                         lambda k, w: (xm_hi if w == "hi" else xm_lo)[:, k, :],
                         range(8))
                inc(ii, "pe")
            go("tensor", f_lm)
            pv = mk.led["pe"]

            def f_lm_a(e, pv=pv):
                wait(e, "scalar", "pe", pv)
                e.activation(qlT_hi[:], P[4][0:64, 0:256], AF.Copy, bias=0.0,
                             scale=SCALE)
                inc(e.copy(klT_hi[:], P[4][0:64, 256:512]), "act")
            go("scalar", f_lm_a)
            av = mk.led["act"]

            def f_lm_v(e, pv=pv, av=av):
                wait(e, "vector", "pe", pv)
                wait(e, "vector", "act", av)
                e.scalar_tensor_tensor(qlT_lo[:], P[4][0:64, 0:256], SCALE,
                                       qlT_hi[:], ALU.mult, ALU.subtract)
                inc(e.tensor_tensor(klT_lo[:], P[4][0:64, 256:512], klT_hi[:],
                                    ALU.subtract), "vec")
            go("vector", f_lm_v)
            if hl == 0:
                tap("qlT_hi", lambda: qlT_hi[:])
                tap("qlT_lo", lambda: qlT_lo[:])
                tap("klT_hi", lambda: klT_hi[:])

            def f_s2(e):
                wait(e, "tensor", "vec")
                for h2 in range(2):
                    ii = mm3(e, P[4][:, ts(h2, 256)],
                             lambda k, w, h2=h2: (qlT_hi if w == "hi" else qlT_lo)[:, ts(h2, 128)],
                             lambda k, w: (klT_hi if w == "hi" else klT_lo)[:],
                             (0,))
                inc(ii, "pe")
            go("tensor", f_s2)
            pv = mk.led["pe"]

            def f_s2v(e, pv=pv):
                wait(e, "vector", "pe", pv)
                e.reduce_max(stat2[:, 0:1], P[4][:, 0:256], axis=AX.X, negate=True)
                inc(e.reduce_max(stat2[:, 1:2], P[4][:, 256:512], axis=AX.X,
                                 negate=True), "vec")
            go("vector", f_s2v)

            def f_s2r(e):
                wait(e, "scalar", "vec")
                inc(e.copy(mnb2[:], stat2[:, 0:2]), "act")
            go("scalar", f_s2r)

            def f_s2r2(e):
                wait(e, "vector", "act")
                inc(e.tensor_copy(stat2[:, 0:2], mnb2[:]), "vec")
            go("vector", f_s2r2)

            def f_s2e(e):
                wait(e, "scalar", "vec")
                e.activation(e2s[:, 0, :], P[4][:, 0:256], AF.Exp,
                             bias=stat2[:, 0:1], scale=1.0,
                             accum_out=stat2[:, 2:3])
                inc(e.activation(e2s[:, 1, :], P[4][:, 256:512], AF.Exp,
                                 bias=stat2[:, 1:2], scale=1.0,
                                 accum_out=stat2[:, 3:4]), "act")
            go("scalar", f_s2e)

            def f_s2i(e):
                wait(e, "vector", "act")
                inc(e.reciprocal(stat2[:, 4:6], stat2[:, 2:4]), "vec")
            go("vector", f_s2i)

            def f_s2t(e):
                wait(e, "tensor", "vec")
                e.transpose(P[2][0:1, 0:128], stat2[:, 0:1], identf[:])
                e.transpose(P[2][0:1, 128:256], stat2[:, 1:2], identf[:])
                e.transpose(P[2][0:1, 256:384], stat2[:, 4:5], identf[:])
                inc(e.transpose(P[2][0:1, 384:512], stat2[:, 5:6], identf[:]),
                    "pe")
            go("tensor", f_s2t)
            pv = mk.led["pe"]

            def f_s2tc(e, pv=pv):
                wait(e, "vector", "pe", pv)
                inc(e.tensor_copy(srow[0:1, 0:512], P[2][0:1, 0:512]), "vec")
            go("vector", f_s2tc)

            def f_s2tb(e):
                wait(e, "scalar", "vec")
                inc(e.copy(srow_b[0:1, 0:256], srow[0:1, 0:256]), "act")
            go("scalar", f_s2tb)

            def f_s2t2(e):
                wait(e, "tensor", "act")
                for h2 in range(2):
                    mm3(e, P[5][:, ts(h2, 256)],
                        lambda k, w, h2=h2: (klT_hi if w == "hi" else klT_lo)[:, ts(h2, 128)],
                        lambda k, w: (qlT_hi if w == "hi" else qlT_lo)[:],
                        (0,), stop=False)
                    e.matmul(P[5][:, ts(h2, 256)], ones_bf[:],
                             srow_b[0:1, 0:256], start=False, stop=True)
                inc(e.matmul(P[3][:, 0:256], ones_f[:], srow[0:1, 256:512],
                             start=True, stop=True), "pe")
            go("tensor", f_s2t2)
            pv = mk.led["pe"]

            def f_s2t3(e, pv=pv):
                wait(e, "scalar", "pe", pv)
                e.activation(e2s[:, 0, :], P[5][:, 0:256], AF.Exp, bias=0.0,
                             scale=1.0)
                inc(e.activation(e2s[:, 1, :], P[5][:, 256:512], AF.Exp,
                                 bias=0.0, scale=1.0), "act")
            go("scalar", f_s2t3)

            def f_s2t4(e):
                wait(e, "vector", "act")
                e.tensor_copy(xw[:, 0, :], P[3][:, 0:256])
                e.drain()
                e.tensor_tensor(aT[:, 0, :], e2s[:, 0, :], xw[:, 0, :], ALU.mult)
                inc(e.tensor_tensor(aT[:, 1, :], e2s[:, 1, :], xw[:, 0, :],
                                    ALU.mult), "vec")
            go("vector", f_s2t4)

            def f_z0a(e):
                wait(e, "vector", "vec")
                e.reduce_sum(stat2[:, 0:1], aT[:, 0, :], axis=AX.X)
                inc(e.reduce_sum(stat2[:, 1:2], aT[:, 1, :], axis=AX.X), "vec")
            go("vector", f_z0a)

            def f_z0b(e):
                wait(e, "tensor", "vec")
                e.transpose(P[2][0:1, 0:128], stat2[:, 0:1], identf[:])
                e.transpose(P[2][0:1, 128:256], stat2[:, 1:2], identf[:])
                for k in range(2):
                    ii = e.matmul(P[2][0:1, 256:512], onescol_f[:], aT[:, k, :],
                                  start=(k == 0), stop=(k == 1))
                inc(ii, "pe")
            go("tensor", f_z0b)
            pv = mk.led["pe"]

            def f_z0c(e, pv=pv):
                wait(e, "vector", "pe", pv)
                e.tensor_copy(srow[0:1, 0:512], P[2][0:1, 0:512])
                e.drain()
                e.reduce_max(srow[0:1, 512:513], srow[0:1, 0:256], axis=AX.X)
                e.reduce_max(srow[0:1, 513:514], srow[0:1, 256:512], axis=AX.X)
                e.drain()
                e.tensor_tensor(srow[0:1, 514:515], srow[0:1, 512:513],
                                srow[0:1, 513:514], ALU.mult)
                e.drain()
                inc(e.reciprocal(srow[0:1, 515:516], srow[0:1, 514:515]), "vec")
            go("vector", f_z0c)

            def f_z0d(e):
                wait(e, "tensor", "vec")
                inc(e.matmul(P[2][:, 0:1], ones_f[:], srow[0:1, 515:516],
                             start=True, stop=True), "pe")
            go("tensor", f_z0d)
            pv = mk.led["pe"]

            def f_z0e(e, pv=pv):
                wait(e, "vector", "pe", pv)
                e.tensor_copy(stat2[:, 6:7], P[2][:, 0:1])
                e.drain()
                e.tensor_scalar(zA[:, 0, :], aT[:, 0, :], stat2[:, 6:7], None,
                                ALU.mult)
                inc(e.tensor_scalar(zA[:, 1, :], aT[:, 1, :], stat2[:, 6:7],
                                    None, ALU.mult), "vec")
            go("vector", f_z0e)

            def f_z0f(e):
                wait(e, "tensor", "vec")
                e.transpose(P[2][:, 0:128], zA[:, 0, 0:128], identf[:])
                e.transpose(P[2][:, 128:256], zA[:, 1, 0:128], identf[:])
                e.transpose(P[3][:, 0:128], zA[:, 0, 128:256], identf[:])
                inc(e.transpose(P[3][:, 128:256], zA[:, 1, 128:256], identf[:]),
                    "pe")
            go("tensor", f_z0f)
            pv = mk.led["pe"]

            def f_z0g(e, pv=pv):
                wait(e, "vector", "pe", pv)
                e.tensor_copy(zTA[:, 0, 0:128], P[2][:, 0:128])
                e.tensor_copy(zTA[:, 0, 128:256], P[2][:, 128:256])
                e.tensor_copy(zTA[:, 1, 0:128], P[3][:, 0:128])
                inc(e.tensor_copy(zTA[:, 1, 128:256], P[3][:, 128:256]), "vec")
            go("vector", f_z0g)
            if hl == 0:
                tap("z0", lambda: zA[:])
                tap("zT0", lambda: zTA[:])

            zc, zTc, zn, zTn = zA, zTA, zB, zTB
            for it in range(ITERS):
                def f_u(e, zc=zc):
                    wait(e, "tensor", "vec")
                    mmf(e, (P[4][:, 0:256], P[4][:, 256:512]), aT, zc)
                    inc(mmf(e, (P[5][:, 0:256], P[5][:, 256:512]), zc, aT), "pe")
                go("tensor", f_u)
                pv = mk.led["pe"]

                def f_ue(e, pv=pv):
                    wait(e, "scalar", "pe", pv)
                    e.copy(uu[:, 0, :], P[4][:, 0:256])
                    e.copy(uu[:, 1, :], P[4][:, 256:512])
                    e.copy(uuT[:, 0, :], P[5][:, 0:256])
                    inc(e.copy(uuT[:, 1, :], P[5][:, 256:512]), "act")
                go("scalar", f_ue)

                def f_p(e):
                    wait(e, "tensor", "act")
                    inc(mmf(e, (P[4][:, 0:256], P[4][:, 256:512]), uuT, uu),
                        "pe")
                go("tensor", f_p)
                pv = mk.led["pe"]

                def f_x1(e, pv=pv):
                    wait(e, "vector", "pe", pv)
                    e.scalar_tensor_tensor(xw[:, 0, :], uu[:, 0, :], 7.0,
                                           P[4][:, 0:256], ALU.mult, ALU.subtract)
                    inc(e.scalar_tensor_tensor(xw[:, 1, :], uu[:, 1, :], 7.0,
                                               P[4][:, 256:512], ALU.mult,
                                               ALU.subtract), "vec")
                go("vector", f_x1)

                def f_r(e):
                    wait(e, "tensor", "vec")
                    inc(mmf(e, (P[4][:, 0:256], P[4][:, 256:512]), uuT, xw),
                        "pe")
                go("tensor", f_r)
                pv = mk.led["pe"]

                def f_x3(e, pv=pv):
                    wait(e, "vector", "pe", pv)
                    e.scalar_tensor_tensor(xw[:, 0, :], uu[:, 0, :], 15.0,
                                           P[4][:, 0:256], ALU.mult, ALU.subtract)
                    inc(e.scalar_tensor_tensor(xw[:, 1, :], uu[:, 1, :], 15.0,
                                               P[4][:, 256:512], ALU.mult,
                                               ALU.subtract), "vec")
                go("vector", f_x3)

                def f_s(e, zc=zc, zTc=zTc):
                    wait(e, "tensor", "vec")
                    mmf(e, (P[4][:, 0:256], P[4][:, 256:512]), zTc, xw)
                    inc(mmf(e, (P[5][:, 0:256], P[5][:, 256:512]), xw, zTc),
                        "pe")
                go("tensor", f_s)
                pv = mk.led["pe"]

                def f_z(e, zc=zc, zTc=zTc, zn=zn, zTn=zTn, pv=pv):
                    wait(e, "vector", "pe", pv)
                    e.scalar_tensor_tensor(zn[:, 0, :], zc[:, 0, :], 13.0,
                                           P[4][:, 0:256], ALU.mult, ALU.subtract)
                    e.scalar_tensor_tensor(zn[:, 1, :], zc[:, 1, :], 13.0,
                                           P[4][:, 256:512], ALU.mult,
                                           ALU.subtract)
                    e.scalar_tensor_tensor(zTn[:, 0, :], zTc[:, 0, :], 13.0,
                                           P[5][:, 0:256], ALU.mult, ALU.subtract)
                    e.scalar_tensor_tensor(zTn[:, 1, :], zTc[:, 1, :], 13.0,
                                           P[5][:, 256:512], ALU.mult,
                                           ALU.subtract)
                    e.drain()
                    e.tensor_scalar_mul(zn[:, 0, :], zn[:, 0, :], 0.25)
                    e.tensor_scalar_mul(zn[:, 1, :], zn[:, 1, :], 0.25)
                    e.tensor_scalar_mul(zTn[:, 0, :], zTn[:, 0, :], 0.25)
                    inc(e.tensor_scalar_mul(zTn[:, 1, :], zTn[:, 1, :], 0.25),
                        "vec")
                go("vector", f_z)
                if hl == 0 and it == 0:
                    tap("u0", lambda: uu[:])
                    tap("uT0", lambda: uuT[:])
                    tap("x3_0", lambda: xw[:])
                    tap("z1", lambda: zn[:])
                    tap("zT1", lambda: zTn[:])
                zc, zn = zn, zc
                zTc, zTn = zTn, zTc
            zT_fin = zTc
            if hl == 0:
                tap("aT", lambda: aT[:])
                tap("zT_fin", lambda: zT_fin[:])
                tap("zA0", lambda: zA[:])
                tap("srow", lambda: srow[:])
                tap("stat2", lambda: stat2[:])
                tap("e2s", lambda: e2s[:])

            for tc in range(8):
                def f_s3(e, tc=tc):
                    wait(e, "tensor", "vec")
                    for h2 in range(2):
                        ii = mm3(e, P[h2][:, 0:512],
                                 lambda k, w, h2=h2: (qlT_hi if w == "hi" else qlT_lo)[:, ts(h2, 128)],
                                 lambda k, w: (kqT_hi if w == "hi" else kqT_lo)[:, ts(tc, 512)],
                                 (0,))
                    inc(ii, "pe")
                go("tensor", f_s3)
                pv = mk.led["pe"]

                def f_s3v(e, tc=tc, pv=pv):
                    wait(e, "vector", "pe", pv)
                    e.reduce_max(mr3t[:, 0, tc:tc + 1], P[0][:, 0:512], axis=AX.X)
                    inc(e.reduce_max(mr3t[:, 1, tc:tc + 1], P[1][:, 0:512],
                                     axis=AX.X), "vec")
                go("vector", f_s3v)

            mnb3 = sba("mnb3", [128, 2], BF16)

            def f_s3m(e):
                wait(e, "vector", "vec")
                e.tensor_reduce(mn3[:], mr3t[:], axis=AX.X, op=ALU.max,
                                negate=True)
                inc(e.drain(), "vec")
            go("vector", f_s3m)

            def f_s3m2(e):
                wait(e, "scalar", "vec")
                inc(e.copy(mnb3[:], mn3[:]), "act")
            go("scalar", f_s3m2)

            def f_s3m3(e):
                wait(e, "vector", "act")
                inc(e.tensor_copy(mn3[:], mnb3[:]), "vec")
            go("vector", f_s3m3)

            def f_s3t(e):
                wait(e, "tensor", "vec")
                wait(e, "tensor", "act")
                e.transpose(Pb[0][0:1, 0:128], mnb3[:, 0:1], identb[:])
                inc(e.transpose(Pb[0][0:1, 128:256], mnb3[:, 1:2], identb[:]),
                    "pe")
            go("tensor", f_s3t)
            pv = mk.led["pe"]

            def f_s3tc(e, pv=pv):
                wait(e, "vector", "pe", pv)
                inc(e.tensor_copy(mrow3[0:1, 0:256], Pb[0][0:1, 0:256]), "vec")
            go("vector", f_s3tc)

            for sc in range(32):
                def f_v1(e, sc=sc):
                    wait(e, "tensor", "vec")
                    wait(e, "tensor", "act")
                    wait(e, "tensor", "pool", onev)
                    mm3(e, P[2][:, 0:256],
                        lambda k, w, sc=sc: (kqT_hi if w == "hi" else kqT_lo)[:, ts(sc, 128)],
                        lambda k, w: (qlT_hi if w == "hi" else qlT_lo)[:],
                        (0,), stop=False)
                    inc(e.matmul(P[2][:, 0:256], ones_bf[:], mrow3[0:1, :],
                                 start=False, stop=True), "pe")
                go("tensor", f_v1)
                pv = mk.led["pe"]

                def f_v2(e, sc=sc, pv=pv):
                    wait(e, "scalar", "pe", pv)
                    inc(e.activation(e3f[:], P[2][:, 0:256], AF.Exp, bias=0.0,
                                     scale=1.0), "act")
                go("scalar", f_v2)
                av = mk.led["act"]

                def f_v2b(e, sc=sc, av=av):
                    wait(e, "vector", "act", av)
                    inc(e.tensor_copy(e3h[:], e3f[:]), "vec")
                go("vector", f_v2b)
                vv = mk.led["vec"]

                def f_v3(e, sc=sc, vv=vv):
                    wait(e, "gpsimd", "vec", vv)
                    inc(e.tensor_tensor(e3l[:], e3f[:], e3h[:], ALU.subtract),
                        "pool")
                go("gpsimd", f_v3)
                plv = mk.led["pool"]

                def f_v4(e, sc=sc, vv=vv, plv=plv):
                    wait(e, "tensor", "vec", vv)
                    wait(e, "tensor", "pool", plv)
                    for h2 in range(2):
                        e.matmul(P[3 + h2][:, 0:129], e3h[:, ts(h2, 128)],
                                 v_tm[:, sc, 0:129], start=(sc == 0), stop=False)
                        ii = e.matmul(P[3 + h2][:, 0:129],
                                      e3l[:, ts(h2, 128)], v_tm[:, sc, 0:129],
                                      start=False, stop=(sc == 31))
                    inc(ii, "pe")
                go("tensor", f_v4)

            def f_w3(e):
                wait(e, "vector", "pe")
                e.tensor_copy(w3s[:, 0, 0:129], P[3][:, 0:129])
                e.tensor_copy(w3s[:, 1, 0:129], P[4][:, 0:129])
                e.drain()
                e.tensor_copy(stat2[:, 2:4],
                              w3s[:, :, 128:129].rearrange("p c o -> p (c o)"))
                e.drain()
                e.reciprocal(stat2[:, 0:2], stat2[:, 2:4])
                e.tensor_tensor(w3v[:, 0, :], w3s[:, 0, 0:64], w3s[:, 0, 64:128],
                                ALU.add)
                e.tensor_tensor(w3v[:, 1, :], w3s[:, 1, 0:64], w3s[:, 1, 64:128],
                                ALU.add)
                e.drain()
                e.tensor_scalar(w3v[:, 0, :], w3v[:, 0, :], stat2[:, 0:1], None,
                                ALU.mult)
                inc(e.tensor_scalar(w3v[:, 1, :], w3v[:, 1, :], stat2[:, 1:2],
                                    None, ALU.mult), "vec")
            go("vector", f_w3)

            def f_zw(e, zT_fin=zT_fin):
                wait(e, "tensor", "vec")
                for ot in range(2):
                    for k in range(2):
                        ii = e.matmul(P[4][:, ts(ot, 64)],
                                      zT_fin[:, k, ts(ot, 128)], w3v[:, k, :],
                                      start=(k == 0), stop=(k == 1))
                inc(ii, "pe")
            go("tensor", f_zw)
            pv = mk.led["pe"]

            def f_zwe(e, pv=pv):
                wait(e, "scalar", "pe", pv)
                e.copy(zw_hi[:, 0, :], P[4][:, 0:64])
                inc(e.copy(zw_hi[:, 1, :], P[4][:, 64:128]), "act")
            go("scalar", f_zwe)
            av = mk.led["act"]

            def f_zwl(e, av=av):
                wait(e, "vector", "act", av)
                e.tensor_tensor(zw_lo[:, 0, :], P[4][:, 0:64], zw_hi[:, 0, :],
                                ALU.subtract)
                inc(e.tensor_tensor(zw_lo[:, 1, :], P[4][:, 64:128],
                                    zw_hi[:, 1, :], ALU.subtract), "vec")
            go("vector", f_zwl)
            if hl == 0:
                tap("mrow3", lambda: mrow3[:])
                tap("w3v", lambda: w3v[:])
                tap("zw_hi", lambda: zw_hi[:])
                tap("zw_lo", lambda: zw_lo[:])

            emit_wsplit(NSPL // 2)

        with contextlib.ExitStack() as sB:
            sbb = lambda n, sh, dt, hl=hl: sB.enter_context(
                nc.sbuf_tensor(f"{n}_{hl}", sh, dt))
            outc = sbb("outc", [64, N + 12], F32)
            acc = [sbb(f"acc{i}", [64, 512], F32) for i in range(2)]
            e1f = sbb("e1f", [128, 2, 512], F32)
            e1h = sbb("e1h", [128, 2, 512], BF16)
            e1l = sbb("e1l", [128, 2, 512], BF16)
            mn1 = sbb("mn1", [128, 32], F32)
            mn1b = sbb("mn1b", [128, 32], BF16)
            rowm = sbb("rowm", [1, 512], BF16)
            rowi = sbb("rowi", [1, 512], F32)
            rowi2 = sbb("rowi2", [1, 512], F32)
            irs = sbb("irs", [64, 512], F32)

            join()

            for tc in range(8):
                def f_qm(e, tc=tc):
                    wait(e, "tensor", "vec")
                    wait(e, "tensor", "act")
                    ii = mm3(e, P[tc % 2][0:64, 0:512],
                             lambda k, w: (whe_hi if w == "hi" else whe_lo)[:, k, 0:64],
                             lambda k, w: (xnT_hi if w == "hi" else xnT_lo)[:, k, ts(tc, 512)],
                             range(8))
                    inc(ii, "pe")
                go("tensor", f_qm)
                pv = mk.led["pe"]

                def f_qa(e, tc=tc, pv=pv):
                    wait(e, "scalar", "pe", pv)
                    inc(e.activation(kqT_hi[:, ts(tc, 512)], P[tc % 2][0:64, 0:512],
                                     AF.Copy, bias=0.0, scale=SCALE), "act")
                go("scalar", f_qa)
                av = mk.led["act"]

                def f_qv(e, tc=tc, pv=pv, av=av):
                    wait(e, "vector", "pe", pv)
                    wait(e, "vector", "act", av)
                    inc(e.scalar_tensor_tensor(kqT_lo[:, ts(tc, 512)],
                                               P[tc % 2][0:64, 0:512], SCALE,
                                               kqT_hi[:, ts(tc, 512)], ALU.mult,
                                               ALU.subtract), "vec")
                go("vector", f_qv)

            if hl == 0:
                tap("qT_hi", lambda: kqT_hi[:])
                tap("qT_lo", lambda: kqT_lo[:])
            for sc in range(32):
                def f_1s(e, sc=sc):
                    wait(e, "tensor", "vec")
                    wait(e, "tensor", "act")
                    ii = mm3(e, P[2 + sc % 2][:, 0:256],
                             lambda k, w, sc=sc: (kqT_hi if w == "hi" else kqT_lo)[:, ts(sc, 128)],
                             lambda k, w: (klT_hi if w == "hi" else klT_lo)[:],
                             (0,))
                    inc(ii, "pe")
                go("tensor", f_1s)
                pv = mk.led["pe"]

                def f_1sv(e, sc=sc, pv=pv):
                    wait(e, "vector", "pe", pv)
                    inc(e.reduce_max(mn1[:, sc:sc + 1], P[2 + sc % 2][:, 0:256],
                                     axis=AX.X, negate=True), "vec")
                go("vector", f_1sv)

            def f_1sb(e):
                wait(e, "scalar", "vec")
                inc(e.copy(mn1b[:], mn1[:]), "act")
            go("scalar", f_1sb)

            def f_gz(e):
                wait(e, "vector", "vec")
                e.memset(outc[:, 0:6], 0.0)
                inc(e.memset(outc[:, N + 6:N + 12], 0.0), "vec")
            go("vector", f_gz)

            for tc in range(8):
                def f_rowt(e, tc=tc):
                    wait(e, "tensor", "vec")
                    wait(e, "tensor", "act")
                    for r4 in range(4):
                        ii = e.transpose(Pb[0][0:1, ts(r4, 128)],
                                         mn1b[:, 4 * tc + r4:4 * tc + r4 + 1],
                                         identb[:])
                    inc(ii, "pe")
                go("tensor", f_rowt)
                pv = mk.led["pe"]

                def f_rowc(e, pv=pv):
                    wait(e, "vector", "pe", pv)
                    inc(e.tensor_copy(rowm[0:1, 0:512], Pb[0][0:1, 0:512]),
                        "vec")
                go("vector", f_rowc)
                vv = mk.led["vec"]

                def f_o1(e, tc=tc, vv=vv):
                    wait(e, "tensor", "vec", vv)
                    for h2 in range(2):
                        mm3(e, P[h2][:, 0:512],
                            lambda k, w, h2=h2: (klT_hi if w == "hi" else klT_lo)[:, ts(h2, 128)],
                            lambda k, w, tc=tc: (kqT_hi if w == "hi" else kqT_lo)[:, ts(tc, 512)],
                            (0,), stop=False)
                        ii = e.matmul(P[h2][:, 0:512], ones_bf[:],
                                      rowm[0:1, 0:512], start=False, stop=True)
                    inc(ii, "pe")
                go("tensor", f_o1)
                pv = mk.led["pe"]

                def f_o2(e, tc=tc, pv=pv):
                    wait(e, "scalar", "pe", pv)
                    e.activation(e1f[:, 0, :], P[0][:, 0:512], AF.Exp, bias=0.0,
                                 scale=1.0)
                    inc(e.activation(e1f[:, 1, :], P[1][:, 0:512], AF.Exp,
                                     bias=0.0, scale=1.0), "act")
                go("scalar", f_o2)
                av = mk.led["act"]

                def f_o2b(e, tc=tc, av=av):
                    wait(e, "vector", "act", av)
                    inc(e.tensor_copy(e1h[:], e1f[:]), "vec")
                go("vector", f_o2b)
                vv = mk.led["vec"]

                def f_o3(e, tc=tc, vv=vv):
                    wait(e, "gpsimd", "vec", vv)
                    inc(e.tensor_tensor(e1l[:], e1f[:], e1h[:], ALU.subtract),
                        "pool")
                go("gpsimd", f_o3)
                plv1 = mk.led["pool"]

                def f_o3b(e, vv=vv, plv1=plv1):
                    wait(e, "tensor", "vec", vv)
                    wait(e, "tensor", "pool", plv1)
                    i = 0
                    for h2 in range(2):
                        for ee in (e1h, e1l):
                            ii = e.matmul(P[3][0:1, 0:512], onescol_bf[:],
                                          ee[:, h2, :], start=(i == 0),
                                          stop=(i == 3))
                            i += 1
                    inc(ii, "pe")
                go("tensor", f_o3b)
                pv = mk.led["pe"]

                def f_o3c(e, pv=pv):
                    wait(e, "vector", "pe", pv)
                    e.tensor_copy(rowi2[0:1, 0:512], P[3][0:1, 0:512])
                    e.drain()
                    inc(e.reciprocal(rowi[0:1, 0:512], rowi2[0:1, 0:512]), "vec")
                go("vector", f_o3c)
                vv = mk.led["vec"]

                def f_o4(e, tc=tc, vv=vv):
                    wait(e, "tensor", "vec", vv)
                    inc(e.matmul(P[3][0:64, 0:512], ones_f[0:1, 0:64],
                                 rowi[0:1, 0:512], start=True, stop=True), "pe")
                    i = 0
                    for h2 in range(2):
                        for (zz, ee) in ((zw_hi, e1h), (zw_hi, e1l), (zw_lo, e1h)):
                            ii = e.matmul(P[2][0:64, 0:512], zz[:, h2, :],
                                          ee[:, h2, :], start=(i == 0),
                                          stop=(i == 5))
                            i += 1
                    inc(ii, "pe")
                go("tensor", f_o4)
                pv = mk.led["pe"]

                def f_o5(e, tc=tc, pv=pv):
                    wait(e, "vector", "pe", pv)
                    e.tensor_copy(irs[:], P[3][0:64, 0:512])
                    e.drain()
                    inc(e.tensor_tensor(outc[:, 6 + tc * 512:6 + (tc + 1) * 512],
                                        P[2][0:64, 0:512], irs[:], ALU.mult),
                        "vec")
                go("vector", f_o5)

            for cc_ in range(8):
                def f_cv(e, cc_=cc_, hl=hl):
                    wait(e, "vector", "vec")
                    wait(e, "vector", "d4")
                    a, b = acc[0], acc[1]
                    e.tensor_scalar(a[:], outc[:, cc_ * 512 + 0:cc_ * 512 + 512],
                                    wcol_h[:, 0:1], None, ALU.mult)
                    for j in range(1, KC):
                        src, dst = (a, b) if j % 2 == 1 else (b, a)
                        e.drain()
                        ii = e.scalar_tensor_tensor(
                            dst[:], outc[:, cc_ * 512 + j:cc_ * 512 + j + 512],
                            wcol_h[:, j:j + 1], src[:], ALU.mult, ALU.add)
                    inc(ii, "vec")
                go("vector", f_cv)
                vv = mk.led["vec"]

                def f_st(e, cc_=cc_, hl=hl, vv=vv):
                    wait(e, "sync", "vec", vv)
                    inc(e.dma_start(out=a2a_in.ap()[cc_, ts(hl, 64), :],
                                    in_=acc[0][:]), "d4", 16)
                go("sync", f_st)

            if hl == 0:
                tap("mn1", lambda: mn1[:])
                tap("outc", lambda: outc[:])
            emit_wsplit(NSPL - NSPL // 2)
            join()

    emit_wsplit(len(wsplit_jobs))
    join()
    if stop <= 2:
        def f_fin2(e):
            for s in SEMS:
                wait(e, "sync", s)
        go("sync", f_fin2)
        sc2.close()
        es.close()
        return nc
    sc2.close()

    def f_cc(e):
        for s in SEMS:
            wait(e, "gpsimd", s)
        with allow_ici_all_to_all():
            inc(e.collective_compute(
                "AllToAll", ALU.bypass, replica_groups=[list(range(8))],
                ins=[a2a_in.ap().opt()], outs=[a2a_out.ap().opt()]), "cc")
    go("gpsimd", f_cc)

    y_tm = sb("y_tm", [128, 16, D], F32)
    s34 = contextlib.ExitStack()
    of_hi = s34.enter_context(nc.sbuf_tensor("of_hi", [128, 8, T], BF16))
    of_lo = s34.enter_context(nc.sbuf_tensor("of_lo", [128, 8, T], BF16))
    with contextlib.ExitStack() as s3:
        of_f = s3.enter_context(nc.sbuf_tensor("of_f", [128, 8, T], F32))

        def f_as(e):
            wait(e, "sync", "cc")
            for r in range(8):
                for q in range(4):
                    ii = e.dma_start(
                        out=of_f[:, 4 * (r % 2) + q, ts(r // 2, 512)],
                        in_=a2a_out.ap()[r, ts(q, 128), :])
                    inc(ii, "dsync", 16)
        go("sync", f_as)
        dv = mk.led["dsync"]

        tap("of_f", lambda: of_f[:])
        for k in range(8):
            def f_sp_a(e, k=k, dv=dv):
                wait(e, "scalar", "dsync", dv)
                inc(e.copy(of_hi[:, k, :], of_f[:, k, :]), "act")
            go("scalar", f_sp_a)
            av = mk.led["act"]

            def f_sp_v(e, k=k, av=av):
                wait(e, "vector", "act", av)
                inc(e.tensor_tensor(of_lo[:, k, :], of_f[:, k, :], of_hi[:, k, :],
                                    ALU.subtract), "vec")
            go("vector", f_sp_v)
        join()

    with contextlib.ExitStack() as s4:
        sbl = lambda n, sh, dt: s4.enter_context(nc.sbuf_tensor(n, sh, dt))
        fch = sbl("fch", [128, 8, D], BF16)
        fcl = sbl("fcl", [128, 8, D], BF16)
        bfc_t = sbl("bfc_t", [128, D], F32)
        brx_t = sbl("brx_t", [128, D], F32)
        xres_t = [sbl(f"xres_t{i}", [128, D], F32) for i in range(2)]

        def f_fl(e):
            wait(e, "sync", "dpool")
            inc(e.dma_start(out=fch[:],
                            in_=fc_hi_d.rearrange("(c p) d -> p c d", p=128)),
                "dsync", 16)
            inc(e.dma_start(out=fcl[:],
                            in_=fc_lo_d.rearrange("(c p) d -> p c d", p=128)),
                "dsync", 16)
            inc(e.dma_start(out=bfc_t[:], in_=bfc_rep[:]), "dsync", 16)
        go("sync", f_fl)
        fdv = mk.led["dsync"]

        hist4 = {}
        for tt in range(16):
            dxs = "dx0" if tt % 2 == 0 else "dx1"

            def f_xl(e, tt=tt, dxs=dxs):
                if tt >= 2:
                    wait(e, "sync", "vec", hist4[tt - 2])
                else:
                    wait(e, "sync", "vec")
                inc(e.dma_start(out=xres_t[tt % 2][:], in_=xres[ts(tt, 128), :]),
                    dxs, 16)
            go("sync", f_xl)
            dv = mk.led[dxs]

            def f_fm(e, tt=tt):
                wait(e, "tensor", "dsync", fdv)
                wait(e, "tensor", "vec")
                for half in range(2):
                    ii = mm3(e, P[2 * (tt % 2) + half][:, 0:512],
                             lambda k, w, tt=tt: (of_hi if w == "hi" else of_lo)[:, k, ts(tt, 128)],
                             lambda k, w, half=half: (fch if w == "hi" else fcl)[:, k, ts(half, 512)],
                             range(8))
                inc(ii, "pe")
            go("tensor", f_fm)
            pv = mk.led["pe"]

            def f_fe(e, tt=tt, pv=pv, dv=dv, dxs=dxs):
                wait(e, "vector", "pe", pv)
                wait(e, "vector", dxs, dv)
                e.tensor_tensor(brx_t[:], bfc_t[:], xres_t[tt % 2][:], ALU.add)
                e.drain()
                for half in range(2):
                    ii = e.tensor_tensor(y_tm[:, tt, ts(half, 512)],
                                         P[2 * (tt % 2) + half][:, 0:512],
                                         brx_t[:, ts(half, 512)], ALU.add)
                inc(ii, "vec")
            go("vector", f_fe)
            hist4[tt] = mk.led["vec"]
        tap("y_fc", lambda: y_tm[:])
        join()
    s34.close()
    join()

    comb = sb("comb", [128, 16, NE], F32)
    gwh = sb("gwh", [128, 8, NE], BF16)
    gwl = sb("gwl", [128, 8, NE], BF16)
    with contextlib.ExitStack() as sg:
        gwf = sg.enter_context(nc.sbuf_tensor("gwf", [128, 8, NE], F32))

        def f_gl(e):
            inc(e.dma_start(out=gwf[:],
                            in_=gate_wT.rearrange("(c p) d -> p c d", p=128)),
                "dsync", 16)
        go("sync", f_gl)

        def f_gc(e):
            wait(e, "scalar", "dsync")
            inc(e.copy(gwh[:], gwf[:]), "act")
        go("scalar", f_gc)

        def f_gv(e):
            wait(e, "vector", "act")
            inc(e.tensor_tensor(gwl[:], gwf[:], gwh[:], ALU.subtract), "vec")
        go("vector", f_gv)
        join()

    be_t = sb("be_t", [128, NE], F32)
    go("sync", lambda e: inc(e.dma_start(out=be_t[:], in_=be_rep[:]), "dsync", 16))

    for half in range(2):
        with contextlib.ExitStack() as s5:
            sbl = lambda n, sh, dt, half=half: s5.enter_context(
                nc.sbuf_tensor(f"{n}_{half}", sh, dt))
            xn2T_hi = sbl("xn2T_hi", [128, 8, 1024], BF16)
            xn2T_lo = sbl("xn2T_lo", [128, 8, 1024], BF16)
            x2f = sbl("x2f", [128, D], F32)
            x2h = sbl("x2h", [128, D], BF16)
            x2l = sbl("x2l", [128, D], BF16)
            st5 = sbl("st5", [128, 8], F32)
            sco = sbl("sco", [128, NE], F32)
            r_t2 = sbl("r_t2", [128, 4], F32)
            r_tmp = sbl("r_tmp", [128, NE], F32)
            r_msk = sbl("r_msk", [128, NE], F32)
            r_cnt = sbl("r_cnt", [128, NE], F32)
            r_eq = sbl("r_eq", [128, NE], F32)
            r_a = sbl("r_a", [128, NE], F32)
            r_b = sbl("r_b", [128, NE], F32)
            rbuf = sbl("rbuf", [128, NE, NE], F32)
            rbuf2 = sbl("rbuf2", [128, NE, NE], F32)

            for ttl in range(8):
                tt = half * 8 + ttl

                def f_l2(e, tt=tt):
                    wait(e, "vector", "act")
                    inc(e.reduce_sum(st5[:, 0:1], y_tm[:, tt, :], axis=AX.X,
                                     negate=True), "vec")
                go("vector", f_l2)
                vv = mk.led["vec"]

                def f_l2a(e, tt=tt, vv=vv):
                    wait(e, "scalar", "vec", vv)
                    inc(e.activation(x2f[:], y_tm[:, tt, :], AF.Square, bias=0.0,
                                     scale=1.0, accum_out=st5[:, 1:2]), "act")
                go("scalar", f_l2a)

                def f_l2v(e, tt=tt):
                    wait(e, "vector", "act")
                    e.drain()
                    e.tensor_scalar_mul(st5[:, 2:3], st5[:, 0:1], 1.0 / D)
                    e.drain()
                    e.tensor_tensor(st5[:, 3:4], st5[:, 2:3], st5[:, 2:3],
                                    ALU.mult)
                    e.tensor_scalar_mul(st5[:, 4:5], st5[:, 1:2], 1.0 / D)
                    e.drain()
                    e.tensor_tensor(st5[:, 0:1], st5[:, 4:5], st5[:, 3:4],
                                    ALU.subtract)
                    e.drain()
                    inc(e.tensor_scalar_add(st5[:, 0:1], st5[:, 0:1], 1e-5),
                        "vec")
                go("vector", f_l2v)

                def f_l2r(e, tt=tt):
                    wait(e, "scalar", "vec")
                    inc(e.activation(st5[:, 5:6], st5[:, 0:1], AF.Sqrt,
                                     bias=0.0, scale=1.0), "act")
                go("scalar", f_l2r)

                def f_l2b(e, tt=tt):
                    wait(e, "vector", "act")
                    e.reciprocal(st5[:, 1:2], st5[:, 5:6])
                    e.drain()
                    e.tensor_tensor(st5[:, 6:7], st5[:, 2:3], st5[:, 1:2],
                                    ALU.mult)
                    inc(e.drain(), "vec")
                go("vector", f_l2b)

                def f_l2x(e, tt=tt):
                    wait(e, "vector", "vec")
                    inc(e.tensor_scalar(x2f[:], y_tm[:, tt, :], st5[:, 1:2],
                                        st5[:, 6:7], ALU.mult, ALU.add), "vec")
                go("vector", f_l2x)

                def f_l2h(e, tt=tt):
                    wait(e, "scalar", "vec")
                    wait(e, "scalar", "pe")
                    inc(e.copy(x2h[:], x2f[:]), "act")
                go("scalar", f_l2h)
                av = mk.led["act"]

                def f_l2l(e, tt=tt, av=av):
                    wait(e, "vector", "act", av)
                    wait(e, "vector", "pe")
                    e.drain()
                    inc(e.tensor_tensor(x2l[:], x2f[:], x2h[:], ALU.subtract),
                        "vec")
                go("vector", f_l2l)
                vv = mk.led["vec"]

                for k in range(8):
                    def f_l2t(e, k=k):
                        wait(e, "tensor", "vec")
                        wait(e, "tensor", "act")
                        e.transpose(Pb[k % 2][:, 0:128], x2h[:, ts(k, 128)],
                                    identb[:])
                        inc(e.transpose(Pb[k % 2][:, 128:256],
                                        x2l[:, ts(k, 128)], identb[:]), "pe")
                    go("tensor", f_l2t)
                    pv = mk.led["pe"]

                    def f_l2tc(e, k=k, ttl=ttl, pv=pv):
                        wait(e, "vector", "pe", pv)
                        e.tensor_copy(xn2T_hi[:, k, ts(ttl, 128)],
                                      Pb[k % 2][:, 0:128])
                        inc(e.tensor_copy(xn2T_lo[:, k, ts(ttl, 128)],
                                          Pb[k % 2][:, 128:256]), "vec")
                    go("vector", f_l2tc)
                vv = mk.led["vec"]

                def f_rl(e, ttl=ttl, vv=vv):
                    wait(e, "tensor", "vec", vv)
                    wait(e, "tensor", "act")
                    ii = mm3(e, P[4][:, 0:NE],
                             lambda k, w, ttl=ttl: (xn2T_hi if w == "hi" else xn2T_lo)[:, k, ts(ttl, 128)],
                             lambda k, w: (gwh if w == "hi" else gwl)[:, k, :],
                             range(8))
                    inc(ii, "pe")
                go("tensor", f_rl)
                pv = mk.led["pe"]

                def f_rs(e, tt=tt, pv=pv):
                    wait(e, "scalar", "pe", pv)
                    inc(e.activation(sco[:], P[4][:, 0:NE], AF.Sigmoid,
                                     bias=0.0, scale=1.0), "act")
                go("scalar", f_rs)

                def f_rt(e, tt=tt):
                    wait(e, "vector", "act")
                    wait(e, "vector", "dsync")
                    e.drain()
                    e.tensor_tensor(r_tmp[:], sco[:], be_t[:], ALU.add)
                    e.drain()
                    g = r_tmp[:].rearrange("p (g i) -> p g i", i=4)
                    pairs = [(a_, b_) for a_ in range(4) for b_ in range(a_ + 1, 4)]
                    for pi, (a_, b_) in enumerate(pairs):
                        e.tensor_tensor(rbuf[:, pi, 0:4], g[:, :, a_],
                                        g[:, :, b_], ALU.add)
                    e.drain()
                    e.tensor_reduce(r_a[:, 0:4],
                                    rbuf[:, 0:6, 0:4].rearrange("p j g -> p g j"),
                                    axis=AX.X, op=ALU.max)
                    e.drain()
                    e.memset(rbuf[:], 0.0)
                    e.drain()
                    for j in range(4):
                        e.tensor_scalar(rbuf[:, j, 0:4], r_a[:, 0:4],
                                        r_a[:, j:j + 1], None, ALU.is_ge)
                        if j < 3:
                            e.tensor_scalar(rbuf[:, 4 + j, j + 1:4],
                                            r_a[:, j + 1:4], r_a[:, j:j + 1],
                                            None, ALU.is_equal)
                    e.drain()
                    e.tensor_reduce(r_b[:, 0:4],
                                    rbuf[:, 0:4, 0:4].rearrange("p j g -> p g j"),
                                    axis=AX.X, op=ALU.add)
                    e.tensor_reduce(r_b[:, 4:8],
                                    rbuf[:, 4:7, 0:4].rearrange("p j g -> p g j"),
                                    axis=AX.X, op=ALU.add)
                    e.drain()
                    e.tensor_tensor(r_a[:, 4:8], r_b[:, 0:4], r_b[:, 4:8],
                                    ALU.subtract)
                    e.drain()
                    e.tensor_scalar(r_b[:, 0:4], r_a[:, 4:8], 3.0, None,
                                    ALU.is_ge)
                    e.drain()
                    for i_ in range(4):
                        e.tensor_copy(r_a[:].rearrange("p (g i) -> p g i", i=4)[:, :, i_],
                                      r_b[:, 0:4])
                    e.drain()
                    e.tensor_tensor(r_msk[:], r_tmp[:], r_a[:], ALU.mult)
                    e.drain()
                    e.memset(rbuf[:], 0.0)
                    e.memset(rbuf2[:], 0.0)
                    e.drain()
                    for j in range(NE):
                        e.tensor_scalar(rbuf[:, j, :], r_msk[:],
                                        r_msk[:, j:j + 1], None, ALU.is_ge)
                        if j < NE - 1:
                            e.tensor_scalar(rbuf2[:, j, j + 1:], r_msk[:, j + 1:],
                                            r_msk[:, j:j + 1], None, ALU.is_equal)
                    e.drain()
                    e.tensor_reduce(r_a[:], rbuf[:].rearrange("p j g -> p g j"),
                                    axis=AX.X, op=ALU.add)
                    e.tensor_reduce(r_b[:], rbuf2[:].rearrange("p j g -> p g j"),
                                    axis=AX.X, op=ALU.add)
                    e.drain()
                    e.tensor_tensor(r_cnt[:], r_a[:], r_b[:], ALU.subtract)
                    e.drain()
                    e.tensor_scalar(r_eq[:], r_cnt[:], 13.0, None, ALU.is_ge)
                    e.drain()
                    e.tensor_tensor(r_a[:], sco[:], r_eq[:], ALU.mult)
                    e.drain()
                    e.reduce_sum(st5[:, 7:8], r_a[:], axis=AX.X)
                    e.drain()
                    e.tensor_scalar_add(st5[:, 7:8], st5[:, 7:8], 1e-20)
                    e.drain()
                    e.reciprocal(st5[:, 3:4], st5[:, 7:8])
                    e.drain()
                    inc(e.tensor_scalar(comb[:, tt, :], r_a[:], st5[:, 3:4],
                                        2.5, ALU.mult, ALU.mult), "vec")
                go("vector", f_rt)

            uph = [sbl(f"uph{i}", [128, 8, MI], BF16) for i in range(2)]
            upl = [sbl(f"upl{i}", [128, 8, MI], BF16) for i in range(2)]
            dnh = [sbl(f"dnh{i}", [128, 4, D], BF16) for i in range(2)]
            dnl = [sbl(f"dnl{i}", [128, 4, D], BF16) for i in range(2)]
            ytmp = x2f
            hrel = sbl("hrel", [128, 4, 512], F32)
            hpro = sbl("hpro", [128, 4, 512], F32)
            hh = [sbl(f"hh{i}", [128, 4, 512], BF16) for i in range(2)]
            hlo = [sbl(f"hlo{i}", [128, 4, 512], BF16) for i in range(2)]
            uhist = {}
            HALF = half
            steps = [(u, tc) for u in range(NU) for tc in range(2)]
            st_hist = {}

            def emit_load(u):
                bi = u % 2
                dus = "du0" if bi == 0 else "du1"

                def f_uw(e, u=u, bi=bi, dus=dus):
                    if u >= 2:
                        wait(e, "sync", "pe", uhist[u - 2])
                    else:
                        wait(e, "sync", "pe")
                    wait(e, "sync", "dpool" if u < 2 else "dpool")
                    inc(e.dma_start(out=uph[bi][:],
                                    in_=up_hi_d[u].rearrange("(c p) m -> p c m", p=128)),
                        dus, 16)
                    inc(e.dma_start(out=upl[bi][:],
                                    in_=up_lo_d[u].rearrange("(c p) m -> p c m", p=128)),
                        dus, 16)
                    inc(e.dma_start(out=dnh[bi][:],
                                    in_=dn_hi_d[u].rearrange("(c p) d -> p c d", p=128)),
                        dus, 16)
                    inc(e.dma_start(out=dnl[bi][:],
                                    in_=dn_lo_d[u].rearrange("(c p) d -> p c d", p=128)),
                        dus, 16)
                go("sync", f_uw)
                return mk.led[dus]

            udvs = {}
            udvs[0] = emit_load(0)
            udvs[1] = emit_load(1)

            def emit_up(s):
                u, tc = steps[s]
                bi = u % 2
                dus = "du0" if bi == 0 else "du1"
                prev_relu = st_hist.get(s - 1, {}).get("relu", 0)

                def f_up(e, u=u, bi=bi, tc=tc, s=s, dus=dus, prev_relu=prev_relu):
                    wait(e, "tensor", dus, udvs[u])
                    if s == 0:
                        wait(e, "tensor", "act")
                        wait(e, "tensor", "vec")
                    else:
                        wait(e, "tensor", "act", prev_relu)
                    for it in range(4):
                        ii = mm3(e, P[it][:, 0:512],
                                 lambda k, w, bi=bi, it=it: (uph[bi] if w == "hi" else upl[bi])[:, k, ts(it, 128)],
                                 lambda k, w, tc=tc: (xn2T_hi if w == "hi" else xn2T_lo)[:, k, ts(tc, 512)],
                                 range(8))
                    inc(ii, "pe")
                go("tensor", f_up)
                st_hist.setdefault(s, {})["up"] = mk.led["pe"]

            def emit_hchain(s):
                u, tc = steps[s]
                sp = s % 2
                prev_prod = st_hist.get(s - 1, {}).get("prod", 0)
                upv = st_hist[s]["up"]

                def f_hr(e, upv=upv, prev_prod=prev_prod):
                    wait(e, "scalar", "pe", upv)
                    wait(e, "scalar", "vec", prev_prod)
                    for it in range(4):
                        ii = e.activation(hrel[:, it, :], P[it][:, 0:512],
                                          AF.Relu, bias=0.0, scale=1.0)
                    inc(ii, "act")
                go("scalar", f_hr)
                st_hist[s]["relu"] = mk.led["act"]
                av = mk.led["act"]
                prev_hh = st_hist.get(s - 1, {}).get("hhdma", 0)
                prev_hlo = st_hist.get(s - 1, {}).get("hlo", 0)

                def f_hp(e, av=av, prev_hh=prev_hh, prev_hlo=prev_hlo):
                    wait(e, "vector", "act", av)
                    wait(e, "vector", "dpool", prev_hh)
                    wait(e, "vector", "pool", prev_hlo)
                    inc(e.scalar_tensor_tensor(hpro[:], hrel[:], 1.0,
                                               hrel[:], ALU.mult, ALU.mult),
                        "vec")
                go("vector", f_hp)
                st_hist[s]["prod"] = mk.led["vec"]
                vv = mk.led["vec"]
                prev_dn = st_hist.get(s - 2, {}).get("dn", 0)

                def f_hc(e, sp=sp, vv=vv, prev_dn=prev_dn):
                    wait(e, "gpsimd", "vec", vv)
                    wait(e, "gpsimd", "pe", prev_dn)
                    inc(e.dma_start(out=hh[sp][:], in_=hpro[:]), "dpool", 16)
                go("gpsimd", f_hc)
                st_hist[s]["hhdma"] = mk.led["dpool"]
                dpv = mk.led["dpool"]

                def f_hl(e, sp=sp, dpv=dpv, prev_dn=prev_dn):
                    wait(e, "gpsimd", "dpool", dpv)
                    inc(e.tensor_tensor(hlo[sp][:], hpro[:], hh[sp][:],
                                        ALU.subtract), "pool")
                go("gpsimd", f_hl)
                st_hist[s]["hlo"] = mk.led["pool"]

            def emit_down(s):
                u, tc = steps[s]
                bi = u % 2
                sp = s % 2
                half = HALF
                plv = st_hist[s]["hlo"]
                dpv = st_hist[s]["hhdma"]
                for tsub in range(4):
                    gt = half * 8 + tc * 4 + tsub

                    def f_dn(e, u=u, bi=bi, tsub=tsub, sp=sp, plv=plv, dpv=dpv):
                        wait(e, "tensor", "pool", plv)
                        wait(e, "tensor", "dpool", dpv)
                        wait(e, "tensor", "vec")
                        for h2 in range(2):
                            ii = mm3(e, P[4 + h2][:, 0:512],
                                     lambda k, w, tsub=tsub, sp=sp: (hh[sp] if w == "hi" else hlo[sp])[:, k, ts(tsub, 128)],
                                     lambda k, w, bi=bi, h2=h2: (dnh[bi] if w == "hi" else dnl[bi])[:, k, ts(h2, 512)],
                                     range(4))
                        inc(ii, "pe")
                    go("tensor", f_dn)
                    pv = mk.led["pe"]

                    def f_ya(e, u=u, gt=gt, pv=pv):
                        wait(e, "vector", "pe", pv)
                        for h2 in range(2):
                            ps = P[4 + h2][:, 0:512]
                            if u < NE:
                                e.scalar_tensor_tensor(
                                    ytmp[:, ts(h2, 512)], ps,
                                    comb[:, gt, u:u + 1],
                                    y_tm[:, gt, ts(h2, 512)], ALU.mult,
                                    ALU.add)
                            else:
                                e.tensor_tensor(
                                    ytmp[:, ts(h2, 512)], ps,
                                    y_tm[:, gt, ts(h2, 512)], ALU.add)
                        e.drain()
                        inc(e.tensor_copy(y_tm[:, gt, :], ytmp[:]), "vec")
                    go("vector", f_ya)
                st_hist[s]["dn"] = mk.led["pe"]
                uhist[u] = mk.led["pe"]

            for s in range(len(steps)):
                u, tc = steps[s]
                emit_up(s)
                emit_hchain(s)
                if s >= 1:
                    emit_down(s - 1)
                if tc == 0 and u >= 1 and u + 1 < NU:
                    udvs[u + 1] = emit_load(u + 1)
            emit_down(len(steps) - 1)
            join()

    def f_out(e):
        for s in SEMS:
            wait(e, "sync", s)
        inc(e.dma_start(out=yout.rearrange("(c p) d -> p c d", p=128),
                        in_=y_tm[:]), "dsync", 16)
        wait(e, "sync", "dsync")
    go("sync", f_out)

    es.close()
    return nc


_NC_CACHE = [None]


def stage_inputs(L0_attn_x, L0_attn_w_qkv, L0_attn_w_conv, L0_attn_w_fc,
                 L0_attn_b_fc, L0_ffn_gate_w, L0_ffn_bias_e, L0_ffn_up_w,
                 L0_ffn_down_w, L0_ffn_shared_up, L0_ffn_shared_down):
    x = np.ascontiguousarray(np.asarray(L0_attn_x, dtype=np.float32))
    w_qkv = np.asarray(L0_attn_w_qkv, dtype=np.float32)
    w_conv = np.asarray(L0_attn_w_conv, dtype=np.float32)[:, 0, :, 0]
    w_fc = np.ascontiguousarray(np.asarray(L0_attn_w_fc, dtype=np.float32))
    b_fc = np.asarray(L0_attn_b_fc, dtype=np.float32)
    gate_w = np.asarray(L0_ffn_gate_w, dtype=np.float32)
    bias_e = np.asarray(L0_ffn_bias_e, dtype=np.float32)
    up_w = np.asarray(L0_ffn_up_w, dtype=np.float32)
    down_w = np.asarray(L0_ffn_down_w, dtype=np.float32)
    shared_up = np.asarray(L0_ffn_shared_up, dtype=np.float32)
    shared_down = np.asarray(L0_ffn_shared_down, dtype=np.float32)

    import ml_dtypes
    identf = np.eye(128, dtype=np.float32)
    identb = np.eye(128, dtype=np.float32).astype(ml_dtypes.bfloat16)
    bfc_rep = np.ascontiguousarray(np.broadcast_to(b_fc, (128, D)))
    be_rep = np.ascontiguousarray(np.broadcast_to(bias_e, (128, NE)))
    gate_wT = np.ascontiguousarray(gate_w.T)
    up_all = np.empty((NU, D, MI), dtype=np.float32)
    dn_all = np.empty((NU, MI, D), dtype=np.float32)
    for e in range(NE):
        up_all[e] = up_w[e].T
        dn_all[e] = down_w[e].T
    for u in range(4):
        up_all[NE + u] = shared_up[ts(u, 512), :].T
        dn_all[NE + u] = shared_down[:, ts(u, 512)].T

    in_maps = []
    for c in range(8):
        b, hg = c // 2, c % 2
        wq = np.empty((HL, D, 3 * DH), dtype=np.float32)
        for hll in range(HL):
            h = 8 * hg + hll
            wq[hll, :, 0:64] = w_qkv[:, ts(h, 64)]
            wq[hll, :, 64:128] = w_qkv[:, 1024 + 64 * h:1024 + 64 * h + 64]
            wq[hll, :, 128:192] = w_qkv[:, 2048 + 64 * h:2048 + 64 * h + 64]
        wcol_c = np.ascontiguousarray(
            np.repeat(w_conv[ts(hg, 8), :], 64, axis=0))
        in_maps.append({
            "xT": np.ascontiguousarray(x[b].T),
            "xtm": np.ascontiguousarray(x[b]),
            "xres": np.ascontiguousarray(
                x[:, ts(c, 512), :].reshape(T, D)),
            "wqkv": wq,
            "wcol": wcol_c,
            "wfc": w_fc,
            "bfc_rep": bfc_rep,
            "gate_wT": gate_wT,
            "be_rep": be_rep,
            "up_wT": up_all,
            "down_wT": dn_all,
            "identf": identf,
            "identb": identb,
        })
    return in_maps


def kernel(**inputs):
    in_maps = stage_inputs(**inputs)
    if _NC_CACHE[0] is None:
        _NC_CACHE[0] = build_program()
    nc = _NC_CACHE[0]
    res = run_bass_kernel_spmd(nc, in_maps, core_ids=list(range(8)))
    out = np.zeros((BT, N, D), dtype=np.float32)
    for c in range(8):
        yc = np.asarray(res.results[c]["y"])
        for b in range(BT):
            out[b, ts(c, 512), :] = yc[ts(b, 512), :]
    return out


# version 10
# speedup vs baseline: 1.0754x; 1.0397x over previous
import contextlib
import numpy as np

import concourse.bass as bass
import concourse.mybir as mybir
from concourse.bass_utils import run_bass_kernel_spmd
from concourse.aot_env import allow_ici_all_to_all

F32 = mybir.dt.float32
BF16 = mybir.dt.bfloat16
AF = mybir.ActivationFunctionType
ALU = mybir.AluOpType
AX = mybir.AxisListType

D, N, BT, HL, DH, M, ITERS, KC = 1024, 4096, 4, 8, 64, 256, 6, 13
T, NE, NU, MI = 2048, 16, 20, 512
SCALE = 0.125

ENGS = ("tensor", "vector", "scalar", "gpsimd", "sync")
SEMS = ("pe", "vec", "act", "pool", "dsync", "dpool", "cc",
        "dx0", "dx1", "dw", "d3", "d4", "du0", "du1")


def ts(i, n):
    return slice(i * n, (i + 1) * n)


class MK:

    def __init__(self, nc, es, blk):
        self.nc, self.es, self.blk = nc, es, blk
        self.led = {s: 0 for s in SEMS}
        self.sem = {s: es.enter_context(nc.semaphore(s)) for s in SEMS}
        self.lw = {s: {} for s in SEMS}

    def go(self, eng, fn):
        getattr(self.blk, eng)(fn)

    def inc(self, instr, s, n=1):
        instr.then_inc(self.sem[s], n)
        self.led[s] += n
        return instr

    def wait(self, e, ename, s, val=None):
        v = self.led[s] if val is None else val
        if v <= 0 or self.lw[s].get(ename, -1) >= v:
            return
        e.wait_ge(self.sem[s], v)
        self.lw[s][ename] = v

    def join(self):
        snap = dict(self.led)
        for eng in ENGS:
            def f(e, eng=eng, snap=snap):
                for s in SEMS:
                    self.wait(e, eng, s, snap[s])
            self.go(eng, f)

    def sb(self, name, shape, dt):
        return self.es.enter_context(self.nc.sbuf_tensor(name, shape, dt))


def mm3(e, out_ap, lhs, rhs, ks, start=True, stop=True):
    seq = (("hi", "hi"), ("hi", "lo"), ("lo", "hi"))
    tot = len(seq) * len(ks)
    i = 0
    ii = None
    for lw, rw in seq:
        for k in ks:
            ii = e.matmul(out_ap, lhs(k, lw), rhs(k, rw),
                          start=(start and i == 0), stop=(stop and i == tot - 1))
            i += 1
    return ii


def mmf(e, outs, lhsT, rhs, start=True, stop=True):
    ii = None
    for ot in range(2):
        for k in range(2):
            ii = e.matmul(outs[ot], lhsT[:, k, ts(ot, 128)], rhs[:, k, :],
                          start=(start and k == 0), stop=(stop and k == 1))
    return ii


def build_program(dbg=(), stop=99):
    nc = bass.Bass("TRN2", target_bir_lowering=False, debug=False)
    es = contextlib.ExitStack()

    xT = nc.dram_tensor("xT", [D, N], F32, kind="ExternalInput").ap()
    xtm = nc.dram_tensor("xtm", [N, D], F32, kind="ExternalInput").ap()
    xres = nc.dram_tensor("xres", [T, D], F32, kind="ExternalInput").ap()
    wqkv = nc.dram_tensor("wqkv", [HL, D, 3 * DH], F32, kind="ExternalInput").ap()
    wcol = nc.dram_tensor("wcol", [8 * DH, KC], F32, kind="ExternalInput").ap()
    wfc = nc.dram_tensor("wfc", [D, D], F32, kind="ExternalInput").ap()
    bfc_rep = nc.dram_tensor("bfc_rep", [128, D], F32, kind="ExternalInput").ap()
    gate_wT = nc.dram_tensor("gate_wT", [D, NE], F32, kind="ExternalInput").ap()
    be_rep = nc.dram_tensor("be_rep", [128, NE], F32, kind="ExternalInput").ap()
    up_wT = nc.dram_tensor("up_wT", [NU, D, MI], F32, kind="ExternalInput").ap()
    down_wT = nc.dram_tensor("down_wT", [NU, MI, D], F32, kind="ExternalInput").ap()
    identf_d = nc.dram_tensor("identf", [128, 128], F32, kind="ExternalInput").ap()
    identb_d = nc.dram_tensor("identb", [128, 128], BF16, kind="ExternalInput").ap()
    yout = nc.dram_tensor("y", [T, D], F32, kind="ExternalOutput").ap()

    wq_hi_d = nc.dram_tensor("wq_hi_d", [HL, D, 3 * DH], BF16).ap()
    wq_lo_d = nc.dram_tensor("wq_lo_d", [HL, D, 3 * DH], BF16).ap()
    up_hi_d = nc.dram_tensor("up_hi_d", [NU, D, MI], BF16).ap()
    up_lo_d = nc.dram_tensor("up_lo_d", [NU, D, MI], BF16).ap()
    dn_hi_d = nc.dram_tensor("dn_hi_d", [NU, MI, D], BF16).ap()
    dn_lo_d = nc.dram_tensor("dn_lo_d", [NU, MI, D], BF16).ap()
    fc_hi_d = nc.dram_tensor("fc_hi_d", [D, D], BF16).ap()
    fc_lo_d = nc.dram_tensor("fc_lo_d", [D, D], BF16).ap()
    a2a_in = nc.dram_tensor("a2a_in", [8, 512, 512], F32)
    a2a_out = nc.dram_tensor("a2a_out", [8, 512, 512], F32)

    blk = es.enter_context(nc.Block())
    mk = MK(nc, es, blk)
    go, inc, wait, join, sb = mk.go, mk.inc, mk.wait, mk.join, mk.sb

    def tap(name, ap_fn):
        if name not in dbg:
            return
        ap = ap_fn() if callable(ap_fn) else ap_fn
        od = nc.dram_tensor("dbg_" + name, list(ap.shape),
                            ap.dtype, kind="ExternalOutput").ap()
        snap = dict(mk.led)

        def f(e):
            for s in SEMS:
                wait(e, "sync", s, snap[s])
            inc(e.dma_start(out=od, in_=ap), "dsync", 16)
        go("sync", f)

    identf = sb("identf_t", [128, 128], F32)
    identb = sb("identb_t", [128, 128], BF16)
    ones_bf = sb("ones_bf", [1, 128], BF16)
    ones_f = sb("ones_f", [1, 128], F32)
    onescol_f = sb("onescol_f", [128, 1], F32)
    onescol_bf = sb("onescol_bf", [128, 1], BF16)
    wcol_t = sb("wcol_t", [128, 4, KC], F32)
    sc2 = contextlib.ExitStack()
    sc = lambda n, sh, dt: sc2.enter_context(nc.sbuf_tensor(n, sh, dt))
    xnT_hi = sc("xnT_hi", [128, 8, N], BF16)
    xnT_lo = sc("xnT_lo", [128, 8, N], BF16)
    xm_hi = sc("xm_hi", [128, 8, M], BF16)
    xm_lo = sc("xm_lo", [128, 8, M], BF16)
    P = [es.enter_context(nc.psum_tensor(f"P{i}", [128, 512], F32)) for i in range(6)]
    Pb = [es.enter_context(nc.psum_tensor(f"Pb{i}", [128, 1024], BF16)) for i in range(2)]

    WSP = 1024

    def flat(ap_, nelem):
        names = "abcd"[:len(ap_.shape)]
        pat = f"{' '.join(names)} -> ({' '.join(names)})"
        return ap_.rearrange(pat).rearrange("(p f) -> p f", p=128), nelem // 128

    wsplit_jobs = []
    for (src, hid, lod, nel) in ((wfc, fc_hi_d, fc_lo_d, D * D),
                                 (up_wT, up_hi_d, up_lo_d, NU * D * MI),
                                 (down_wT, dn_hi_d, dn_lo_d, NU * MI * D)):
        fs, per = flat(src, nel)
        fh, _ = flat(hid, nel)
        fl, _ = flat(lod, nel)
        for c0 in range(0, per, WSP):
            c1 = min(c0 + WSP, per)
            wsplit_jobs.append((fs[:, c0:c1], fh[:, c0:c1], fl[:, c0:c1]))
    wsp_f = sc("wsp_f", [128, WSP], F32)
    wsp_h = sc("wsp_h", [128, WSP], BF16)
    wsp_l = sc("wsp_l", [128, WSP], BF16)
    wsp_i = [0]

    def emit_wsplit(kn):
        for _ in range(kn):
            if wsp_i[0] >= len(wsplit_jobs):
                return
            src, hid, lod = wsplit_jobs[wsp_i[0]]
            wsp_i[0] += 1

            def f_load(e, src=src):
                wait(e, "sync", "dpool")
                inc(e.dma_start(out=wsp_f[:], in_=src), "d3", 16)
            go("sync", f_load)

            def f_cast(e):
                wait(e, "gpsimd", "d3")
                wait(e, "gpsimd", "dpool")
                inc(e.dma_start(out=wsp_h[:], in_=wsp_f[:]), "dpool", 16)
            go("gpsimd", f_cast)

            def f_lo(e):
                wait(e, "vector", "dpool")
                inc(e.tensor_tensor(wsp_l[:], wsp_f[:], wsp_h[:], ALU.subtract),
                    "vec")
            go("vector", f_lo)

            def f_store(e, hid=hid, lod=lod):
                wait(e, "gpsimd", "vec")
                inc(e.dma_start(out=hid, in_=wsp_h[:]), "dpool", 16)
                inc(e.dma_start(out=lod, in_=wsp_l[:]), "dpool", 16)
            go("gpsimd", f_store)

    def p0_sync(e):
        inc(e.dma_start(out=identf[:], in_=identf_d[:]), "dsync", 16)
        inc(e.dma_start(out=identb[:], in_=identb_d[:]), "dsync", 16)
        inc(e.dma_start(out=wcol_t[:],
                        in_=wcol.rearrange("(c p) k -> p c k", p=128)),
            "dsync", 16)
    go("sync", p0_sync)

    def p0_pool(e):
        e.memset(ones_bf[:], 1.0)
        e.memset(ones_f[:], 1.0)
        e.memset(onescol_bf[:], 1.0)
        inc(e.memset(onescol_f[:], 1.0), "pool")
    go("gpsimd", p0_pool)

    wq_flat, _ = flat(wqkv, HL * D * 3 * DH)
    wqh_flat, _ = flat(wq_hi_d, HL * D * 3 * DH)
    wql_flat, _ = flat(wq_lo_d, HL * D * 3 * DH)
    with contextlib.ExitStack() as s0:
        wq_f = s0.enter_context(nc.sbuf_tensor("wq_f", [128, 6144], F32))
        wq_h = s0.enter_context(nc.sbuf_tensor("wq_h", [128, 6144], BF16))
        wq_l = s0.enter_context(nc.sbuf_tensor("wq_l", [128, 6144], BF16))
        for ih in range(2):
            sl = ts(ih, 6144)

            def p0_ld(e, sl=sl):
                wait(e, "sync", "dpool")
                inc(e.dma_start(out=wq_f[:], in_=wq_flat[:, sl]), "dsync", 16)
            go("sync", p0_ld)

            def p0_act(e):
                wait(e, "scalar", "dsync")
                inc(e.copy(wq_h[:], wq_f[:]), "act")
            go("scalar", p0_act)

            def p0_vec(e):
                wait(e, "vector", "act")
                inc(e.tensor_tensor(wq_l[:], wq_f[:], wq_h[:], ALU.subtract),
                    "vec")
            go("vector", p0_vec)

            def p0_pool2(e, sl=sl):
                wait(e, "gpsimd", "vec")
                inc(e.dma_start(out=wqh_flat[:, sl], in_=wq_h[:]), "dpool", 16)
                inc(e.dma_start(out=wql_flat[:, sl], in_=wq_l[:]), "dpool", 16)
            go("gpsimd", p0_pool2)

        def p0_vec2(e):
            wait(e, "vector", "dsync")
            inc(e.tensor_scalar_add(wcol_t[:, :, 6:7], wcol_t[:, :, 6:7], 1.0),
                "vec")
        go("vector", p0_vec2)
        join()

    with contextlib.ExitStack() as s1:
        sbl = lambda n, sh, dt: s1.enter_context(nc.sbuf_tensor(n, sh, dt))
        nst = sbl("nst", [128, 32], F32)
        sqst = sbl("sqst", [128, 32], F32)
        istd = sbl("istd", [128, 32], F32)
        bist = sbl("bist", [128, 32], F32)
        tst = sbl("tst", [128, 32], F32)
        sq_scr = sbl("sq_scr", [128, 1024], F32)
        xtm_t = [sbl(f"xtm_t{i}", [128, 1024], F32) for i in range(2)]
        x2f = sbl("x2f", [128, 1024], F32)
        x2h = sbl("x2h", [128, 1024], BF16)
        x2l = sbl("x2l", [128, 1024], BF16)
        hist = {}
        for i in range(32):
            dxs = "dx0" if i % 2 == 0 else "dx1"

            def f_load(e, i=i, dxs=dxs):
                if i >= 2:
                    wait(e, "sync", "act", hist[i - 2][0])
                    wait(e, "sync", "vec", hist[i - 2][1])
                inc(e.dma_start(out=xtm_t[i % 2][:], in_=xtm[ts(i, 128), :]),
                    dxs, 16)
            go("sync", f_load)
            dv = mk.led[dxs]

            def f_vec(e, i=i, dv=dv, dxs=dxs):
                wait(e, "vector", dxs, dv)
                inc(e.reduce_sum(nst[:, i:i + 1], xtm_t[i % 2][:], axis=AX.X,
                                 negate=True), "vec")
            go("vector", f_vec)

            def f_act(e, i=i, dv=dv, dxs=dxs):
                wait(e, "scalar", dxs, dv)
                wait(e, "scalar", "act")
                inc(e.activation(sq_scr[:], xtm_t[i % 2][:], AF.Square,
                                 bias=0.0, scale=1.0,
                                 accum_out=sqst[:, i:i + 1]), "act")
            go("scalar", f_act)
            hist[i] = (mk.led["act"], mk.led["vec"])

        varr = sbl("varr", [128, 32], F32)
        sdev = sbl("sdev", [128, 32], F32)

        def f_stats(e):
            wait(e, "vector", "act")
            e.drain()
            e.tensor_scalar_mul(nst[:], nst[:], 1.0 / D)
            e.drain()
            e.tensor_tensor(tst[:], nst[:], nst[:], ALU.mult)
            e.tensor_scalar_mul(sdev[:], sqst[:], 1.0 / D)
            e.drain()
            e.tensor_tensor(varr[:], sdev[:], tst[:], ALU.subtract)
            e.drain()
            inc(e.tensor_scalar_add(varr[:], varr[:], 1e-5), "vec")
        go("vector", f_stats)

        def f_sqrt(e):
            wait(e, "scalar", "vec")
            inc(e.activation(sdev[:], varr[:], AF.Sqrt, bias=0.0, scale=1.0),
                "act")
        go("scalar", f_sqrt)

        def f_recip(e):
            wait(e, "vector", "act")
            e.reciprocal(istd[:], sdev[:])
            e.drain()
            e.tensor_tensor(bist[:], nst[:], istd[:], ALU.mult)
            inc(e.drain(), "vec")
        go("vector", f_recip)
        tap("istd", lambda: istd[:])
        tap("bist", lambda: bist[:])

        hist2 = {}
        for i in range(32):
            dxs = "dx0" if i % 2 == 0 else "dx1"

            def f_load2(e, i=i, dxs=dxs):
                if i >= 2:
                    wait(e, "sync", "vec", hist2[i - 2])
                else:
                    wait(e, "sync", "vec")
                    wait(e, "sync", "act")
                inc(e.dma_start(out=xtm_t[i % 2][:], in_=xtm[ts(i, 128), :]),
                    dxs, 16)
            go("sync", f_load2)
            dv = mk.led[dxs]

            def f_ap(e, i=i, dv=dv, dxs=dxs):
                wait(e, "vector", dxs, dv)
                wait(e, "vector", "act")
                inc(e.tensor_scalar(x2f[:], xtm_t[i % 2][:], istd[:, i:i + 1],
                                    bist[:, i:i + 1], ALU.mult, ALU.add), "vec")
            go("vector", f_ap)
            vv = mk.led["vec"]
            hist2[i] = vv

            def f_hi(e, vv=vv):
                wait(e, "scalar", "vec", vv)
                wait(e, "scalar", "pe")
                inc(e.copy(x2h[:], x2f[:]), "act")
            go("scalar", f_hi)
            av = mk.led["act"]

            def f_lo(e, av=av):
                wait(e, "vector", "act", av)
                wait(e, "vector", "pe")
                e.drain()
                inc(e.tensor_tensor(x2l[:], x2f[:], x2h[:], ALU.subtract), "vec")
            go("vector", f_lo)

            for k in range(8):
                def f_tp(e, k=k):
                    wait(e, "tensor", "vec")
                    wait(e, "tensor", "act")
                    e.transpose(Pb[k % 2][:, 0:128], x2h[:, ts(k, 128)],
                                identb[:])
                    inc(e.transpose(Pb[k % 2][:, 128:256], x2l[:, ts(k, 128)],
                                    identb[:]), "pe")
                go("tensor", f_tp)
                pv = mk.led["pe"]

                def f_tpc(e, k=k, i=i, pv=pv):
                    wait(e, "vector", "pe", pv)
                    e.tensor_copy(xnT_hi[:, k, ts(i, 128)], Pb[k % 2][:, 0:128])
                    inc(e.tensor_copy(xnT_lo[:, k, ts(i, 128)],
                                      Pb[k % 2][:, 128:256]), "vec")
                go("vector", f_tpc)

        xm_f = sbl("xm_f", [128, 8, M], F32)
        xm_t = sbl("xm_t", [128, M], F32)
        xm_t2 = sbl("xm_t2", [128, M], F32)
        for c in range(8):
            def f_xm(e, c=c):
                wait(e, "vector", "vec")
                e.tensor_reduce(xm_t[:],
                                xnT_hi[:, c, :].rearrange("p (m l) -> p m l", l=16),
                                axis=AX.X, op=ALU.add)
                e.tensor_reduce(xm_t2[:],
                                xnT_lo[:, c, :].rearrange("p (m l) -> p m l", l=16),
                                axis=AX.X, op=ALU.add)
                e.drain()
                inc(e.tensor_tensor(xm_f[:, c, :], xm_t[:], xm_t2[:],
                                    ALU.add), "vec")
            go("vector", f_xm)

        def f_xm_a(e):
            wait(e, "scalar", "vec")
            inc(e.activation(xm_hi[:], xm_f[:], AF.Copy, bias=0.0,
                             scale=1.0 / 16), "act")
        go("scalar", f_xm_a)

        def f_xm_v(e):
            wait(e, "vector", "act")
            inc(e.scalar_tensor_tensor(xm_lo[:], xm_f[:], 1.0 / 16, xm_hi[:],
                                       ALU.mult, ALU.subtract), "vec")
        go("vector", f_xm_v)
        join()
        tap("xnT_hi", lambda: xnT_hi[:])
        tap("xnT_lo", lambda: xnT_lo[:])
        tap("xm_hi", lambda: xm_hi[:])
        tap("xm_lo", lambda: xm_lo[:])
    if stop <= 1:
        def f_fin1(e):
            for s in SEMS:
                wait(e, "sync", s)
        go("sync", f_fin1)
        sc2.close()
        es.close()
        return nc

    kqT_hi = sc("kqT_hi", [64, N], BF16)
    kqT_lo = sc("kqT_lo", [64, N], BF16)
    qlT_hi = sc("qlT_hi", [64, M], BF16)
    qlT_lo = sc("qlT_lo", [64, M], BF16)
    klT_hi = sc("klT_hi", [64, M], BF16)
    klT_lo = sc("klT_lo", [64, M], BF16)
    zw_hi = sc("zw_hi", [128, 2, DH + 1], BF16)
    zw_lo = sc("zw_lo", [128, 2, DH + 1], BF16)
    whe_hi = sc("whe_hi", [128, 8, 192], BF16)
    whe_lo = sc("whe_lo", [128, 8, 192], BF16)
    wcol_h = sc("wcol_h", [DH, KC], F32)
    NSPL = (len(wsplit_jobs) + HL - 1) // HL

    for hl in range(HL):
        def f_wl(e, hl=hl):
            wait(e, "sync", "pe")
            wait(e, "sync", "dpool")
            inc(e.dma_start(out=whe_hi[:],
                            in_=wq_hi_d[hl].rearrange("(c p) k -> p c k", p=128)),
                "dw", 16)
            inc(e.dma_start(out=whe_lo[:],
                            in_=wq_lo_d[hl].rearrange("(c p) k -> p c k", p=128)),
                "dw", 16)
        go("sync", f_wl)
        whe_dv = mk.led["dw"]

        def f_wc(e, hl=hl):
            wait(e, "vector", "vec")
            inc(e.tensor_copy(wcol_h[:], wcol_t[ts(hl % 2, 64)][:, hl // 2, :]),
                "vec")
        go("vector", f_wc)

        with contextlib.ExitStack() as sA:
            sba = lambda n, sh, dt, hl=hl: sA.enter_context(
                nc.sbuf_tensor(f"{n}_{hl}", sh, dt))
            vsc_hi = [sba(f"vsc_hi{i}", [64, 512], BF16) for i in range(2)]
            vsc_lo = [sba(f"vsc_lo{i}", [64, 512], BF16) for i in range(2)]
            v_tm = sba("v_tm", [128, 32, 132], BF16)
            e3f = [sba(f"e3f{i}", [128, M], F32) for i in range(2)]
            e3h = [sba(f"e3h{i}", [128, M], BF16) for i in range(2)]
            e3l = [sba(f"e3l{i}", [128, M], BF16) for i in range(2)]
            aT = sba("aT", [128, 2, M], F32)
            zA = sba("zA", [128, 2, M], F32)
            zTA = sba("zTA", [128, 2, M], F32)
            zB = sba("zB", [128, 2, M], F32)
            zTB = sba("zTB", [128, 2, M], F32)
            uu = sba("uu", [128, 2, M], F32)
            uuT = sba("uuT", [128, 2, M], F32)
            xw = sba("xw", [128, 2, M], F32)
            srow = sba("srow", [1, 520], F32)
            srow_b = sba("srow_b", [1, 512], BF16)
            stat2 = sba("stat2", [128, 8], F32)
            mnb2 = sba("mnb2", [128, 2], BF16)
            mn3 = sba("mn3", [128, 2], F32)
            mr3t = sba("mr3t", [128, 2, 8], F32)
            mrow3 = sba("mrow3", [1, M], BF16)
            w3s = sba("w3s", [128, 2, 132], F32)

            kvh = {}
            for tc in range(8):
                pr = kvh.get(tc - 2, {})

                def f_mm(e, tc=tc, pr=pr):
                    wait(e, "tensor", "dw", whe_dv)
                    if tc < 2:
                        wait(e, "tensor", "vec")
                        wait(e, "tensor", "act")
                    else:
                        wait(e, "tensor", "act", pr.get("eva", 0))
                        wait(e, "tensor", "vec", pr.get("evv", 0))
                    ii = mm3(e, P[tc % 2][:, 0:512],
                             lambda k, w: (whe_hi if w == "hi" else whe_lo)[:, k, 64:192],
                             lambda k, w: (xnT_hi if w == "hi" else xnT_lo)[:, k, ts(tc, 512)],
                             range(8))
                    inc(ii, "pe")
                go("tensor", f_mm)
                pv = mk.led["pe"]
                kvh.setdefault(tc, {})["mm"] = pv

                def f_ev_a(e, tc=tc, pv=pv, pr=pr):
                    wait(e, "scalar", "pe", pv)
                    e.copy(kqT_hi[:, ts(tc, 512)], P[tc % 2][0:64, 0:512])
                    inc(e.copy(vsc_hi[tc % 2][:], P[tc % 2][64:128, 0:512]),
                        "act")
                go("scalar", f_ev_a)
                av = mk.led["act"]
                kvh[tc]["eva"] = av

                def f_ev_v(e, tc=tc, pv=pv, av=av):
                    wait(e, "vector", "pe", pv)
                    wait(e, "vector", "act", av)
                    e.tensor_tensor(kqT_lo[:, ts(tc, 512)], P[tc % 2][0:64, 0:512],
                                    kqT_hi[:, ts(tc, 512)], ALU.subtract)
                    inc(e.tensor_tensor(vsc_lo[tc % 2][:], P[tc % 2][64:128, 0:512],
                                        vsc_hi[tc % 2][:], ALU.subtract), "vec")
                go("vector", f_ev_v)
                vv = mk.led["vec"]
                kvh[tc]["evv"] = vv

                def f_tp(e, tc=tc, vv=vv, pr=pr):
                    wait(e, "tensor", "vec", vv)
                    wait(e, "tensor", "vec", pr.get("tpc", 0))
                    for sub in range(4):
                        e.transpose(Pb[0][:, 256 * (tc % 2) + 64 * sub:
                                          256 * (tc % 2) + 64 * sub + 64],
                                    vsc_hi[tc % 2][:, ts(sub, 128)],
                                    identb[0:64, 0:64])
                    for sub in range(4):
                        ii = e.transpose(Pb[1][:, 256 * (tc % 2) + 64 * sub:
                                               256 * (tc % 2) + 64 * sub + 64],
                                         vsc_lo[tc % 2][:, ts(sub, 128)],
                                         identb[0:64, 0:64])
                    inc(ii, "pe")
                go("tensor", f_tp)
                pv2 = mk.led["pe"]

                def f_tpc(e, tc=tc, pv2=pv2):
                    wait(e, "vector", "pe", pv2)
                    for sub in range(4):
                        e.tensor_copy(v_tm[:, tc * 4 + sub, 0:64],
                                      Pb[0][:, 256 * (tc % 2) + 64 * sub:
                                            256 * (tc % 2) + 64 * sub + 64])
                    for sub in range(4):
                        ii = e.tensor_copy(v_tm[:, tc * 4 + sub, 64:128],
                                           Pb[1][:, 256 * (tc % 2) + 64 * sub:
                                                 256 * (tc % 2) + 64 * sub + 64])
                    inc(ii, "vec")
                go("vector", f_tpc)
                kvh[tc]["tpc"] = mk.led["vec"]
            go("gpsimd", lambda e: inc(e.memset(v_tm[:, :, 128:129], 1.0), "pool"))
            onev = mk.led["pool"]
            if hl == 0:
                tap("kT_hi", lambda: kqT_hi[:])
                tap("kT_lo", lambda: kqT_lo[:])
                tap("v_tm", lambda: v_tm[:])

            def f_lm(e):
                wait(e, "tensor", "vec")
                mm3(e, P[4][0:64, 0:256],
                    lambda k, w: (whe_hi if w == "hi" else whe_lo)[:, k, 0:64],
                    lambda k, w: (xm_hi if w == "hi" else xm_lo)[:, k, :],
                    range(8))
                ii = mm3(e, P[4][0:64, 256:512],
                         lambda k, w: (whe_hi if w == "hi" else whe_lo)[:, k, 64:128],
                         lambda k, w: (xm_hi if w == "hi" else xm_lo)[:, k, :],
                         range(8))
                inc(ii, "pe")
            go("tensor", f_lm)
            pv = mk.led["pe"]

            def f_lm_a(e, pv=pv):
                wait(e, "scalar", "pe", pv)
                e.activation(qlT_hi[:], P[4][0:64, 0:256], AF.Copy, bias=0.0,
                             scale=SCALE)
                inc(e.copy(klT_hi[:], P[4][0:64, 256:512]), "act")
            go("scalar", f_lm_a)
            av = mk.led["act"]

            def f_lm_v(e, pv=pv, av=av):
                wait(e, "vector", "pe", pv)
                wait(e, "vector", "act", av)
                e.scalar_tensor_tensor(qlT_lo[:], P[4][0:64, 0:256], SCALE,
                                       qlT_hi[:], ALU.mult, ALU.subtract)
                inc(e.tensor_tensor(klT_lo[:], P[4][0:64, 256:512], klT_hi[:],
                                    ALU.subtract), "vec")
            go("vector", f_lm_v)
            if hl == 0:
                tap("qlT_hi", lambda: qlT_hi[:])
                tap("qlT_lo", lambda: qlT_lo[:])
                tap("klT_hi", lambda: klT_hi[:])

            def f_s2(e):
                wait(e, "tensor", "vec")
                for h2 in range(2):
                    ii = mm3(e, P[4][:, ts(h2, 256)],
                             lambda k, w, h2=h2: (qlT_hi if w == "hi" else qlT_lo)[:, ts(h2, 128)],
                             lambda k, w: (klT_hi if w == "hi" else klT_lo)[:],
                             (0,))
                inc(ii, "pe")
            go("tensor", f_s2)
            pv = mk.led["pe"]

            def f_s2v(e, pv=pv):
                wait(e, "vector", "pe", pv)
                e.reduce_max(stat2[:, 0:1], P[4][:, 0:256], axis=AX.X, negate=True)
                inc(e.reduce_max(stat2[:, 1:2], P[4][:, 256:512], axis=AX.X,
                                 negate=True), "vec")
            go("vector", f_s2v)

            def f_s2r(e):
                wait(e, "scalar", "vec")
                inc(e.copy(mnb2[:], stat2[:, 0:2]), "act")
            go("scalar", f_s2r)

            def f_s2r2(e):
                wait(e, "vector", "act")
                inc(e.tensor_copy(stat2[:, 0:2], mnb2[:]), "vec")
            go("vector", f_s2r2)

            def f_s2e(e):
                wait(e, "scalar", "vec")
                e.activation(e3f[0][:], P[4][:, 0:256], AF.Exp,
                             bias=stat2[:, 0:1], scale=1.0,
                             accum_out=stat2[:, 2:3])
                inc(e.activation(e3f[1][:], P[4][:, 256:512], AF.Exp,
                                 bias=stat2[:, 1:2], scale=1.0,
                                 accum_out=stat2[:, 3:4]), "act")
            go("scalar", f_s2e)

            def f_s2i(e):
                wait(e, "vector", "act")
                inc(e.reciprocal(stat2[:, 4:6], stat2[:, 2:4]), "vec")
            go("vector", f_s2i)

            def f_s2t(e):
                wait(e, "tensor", "vec")
                e.transpose(P[2][0:1, 0:128], stat2[:, 0:1], identf[:])
                e.transpose(P[2][0:1, 128:256], stat2[:, 1:2], identf[:])
                e.transpose(P[2][0:1, 256:384], stat2[:, 4:5], identf[:])
                inc(e.transpose(P[2][0:1, 384:512], stat2[:, 5:6], identf[:]),
                    "pe")
            go("tensor", f_s2t)
            pv = mk.led["pe"]

            def f_s2tc(e, pv=pv):
                wait(e, "vector", "pe", pv)
                inc(e.tensor_copy(srow[0:1, 0:512], P[2][0:1, 0:512]), "vec")
            go("vector", f_s2tc)

            def f_s2tb(e):
                wait(e, "scalar", "vec")
                inc(e.copy(srow_b[0:1, 0:256], srow[0:1, 0:256]), "act")
            go("scalar", f_s2tb)

            def f_s2t2(e):
                wait(e, "tensor", "act")
                for h2 in range(2):
                    mm3(e, P[5][:, ts(h2, 256)],
                        lambda k, w, h2=h2: (klT_hi if w == "hi" else klT_lo)[:, ts(h2, 128)],
                        lambda k, w: (qlT_hi if w == "hi" else qlT_lo)[:],
                        (0,), stop=False)
                    e.matmul(P[5][:, ts(h2, 256)], ones_bf[:],
                             srow_b[0:1, 0:256], start=False, stop=True)
                inc(e.matmul(P[3][:, 0:256], ones_f[:], srow[0:1, 256:512],
                             start=True, stop=True), "pe")
            go("tensor", f_s2t2)
            pv = mk.led["pe"]

            def f_s2t3(e, pv=pv):
                wait(e, "scalar", "pe", pv)
                e.activation(e3f[0][:], P[5][:, 0:256], AF.Exp, bias=0.0,
                             scale=1.0)
                inc(e.activation(e3f[1][:], P[5][:, 256:512], AF.Exp,
                                 bias=0.0, scale=1.0), "act")
            go("scalar", f_s2t3)

            def f_s2t4(e):
                wait(e, "vector", "act")
                e.tensor_copy(xw[:, 0, :], P[3][:, 0:256])
                e.drain()
                e.tensor_tensor(aT[:, 0, :], e3f[0][:], xw[:, 0, :], ALU.mult)
                inc(e.tensor_tensor(aT[:, 1, :], e3f[1][:], xw[:, 0, :],
                                    ALU.mult), "vec")
            go("vector", f_s2t4)

            def f_z0a(e):
                wait(e, "vector", "vec")
                e.reduce_sum(stat2[:, 0:1], aT[:, 0, :], axis=AX.X)
                inc(e.reduce_sum(stat2[:, 1:2], aT[:, 1, :], axis=AX.X), "vec")
            go("vector", f_z0a)

            def f_z0b(e):
                wait(e, "tensor", "vec")
                e.transpose(P[2][0:1, 0:128], stat2[:, 0:1], identf[:])
                e.transpose(P[2][0:1, 128:256], stat2[:, 1:2], identf[:])
                for k in range(2):
                    ii = e.matmul(P[2][0:1, 256:512], onescol_f[:], aT[:, k, :],
                                  start=(k == 0), stop=(k == 1))
                inc(ii, "pe")
            go("tensor", f_z0b)
            pv = mk.led["pe"]

            def f_z0c(e, pv=pv):
                wait(e, "vector", "pe", pv)
                e.tensor_copy(srow[0:1, 0:512], P[2][0:1, 0:512])
                e.drain()
                e.reduce_max(srow[0:1, 512:513], srow[0:1, 0:256], axis=AX.X)
                e.reduce_max(srow[0:1, 513:514], srow[0:1, 256:512], axis=AX.X)
                e.drain()
                e.tensor_tensor(srow[0:1, 514:515], srow[0:1, 512:513],
                                srow[0:1, 513:514], ALU.mult)
                e.drain()
                inc(e.reciprocal(srow[0:1, 515:516], srow[0:1, 514:515]), "vec")
            go("vector", f_z0c)

            def f_z0d(e):
                wait(e, "tensor", "vec")
                inc(e.matmul(P[2][:, 0:1], ones_f[:], srow[0:1, 515:516],
                             start=True, stop=True), "pe")
            go("tensor", f_z0d)
            pv = mk.led["pe"]

            def f_z0e(e, pv=pv):
                wait(e, "vector", "pe", pv)
                e.tensor_copy(stat2[:, 6:7], P[2][:, 0:1])
                e.drain()
                e.tensor_scalar(zA[:, 0, :], aT[:, 0, :], stat2[:, 6:7], None,
                                ALU.mult)
                inc(e.tensor_scalar(zA[:, 1, :], aT[:, 1, :], stat2[:, 6:7],
                                    None, ALU.mult), "vec")
            go("vector", f_z0e)

            def f_z0f(e):
                wait(e, "tensor", "vec")
                e.transpose(P[2][:, 0:128], zA[:, 0, 0:128], identf[:])
                e.transpose(P[2][:, 128:256], zA[:, 1, 0:128], identf[:])
                e.transpose(P[3][:, 0:128], zA[:, 0, 128:256], identf[:])
                inc(e.transpose(P[3][:, 128:256], zA[:, 1, 128:256], identf[:]),
                    "pe")
            go("tensor", f_z0f)
            pv = mk.led["pe"]

            def f_z0g(e, pv=pv):
                wait(e, "vector", "pe", pv)
                e.tensor_copy(zTA[:, 0, 0:128], P[2][:, 0:128])
                e.tensor_copy(zTA[:, 0, 128:256], P[2][:, 128:256])
                e.tensor_copy(zTA[:, 1, 0:128], P[3][:, 0:128])
                inc(e.tensor_copy(zTA[:, 1, 128:256], P[3][:, 128:256]), "vec")
            go("vector", f_z0g)
            if hl == 0:
                tap("z0", lambda: zA[:])
                tap("zT0", lambda: zTA[:])

            zc, zTc, zn, zTn = zA, zTA, zB, zTB
            for it in range(ITERS):
                def f_u(e, zc=zc):
                    wait(e, "tensor", "vec")
                    mmf(e, (P[4][:, 0:256], P[4][:, 256:512]), aT, zc)
                    inc(mmf(e, (P[5][:, 0:256], P[5][:, 256:512]), zc, aT), "pe")
                go("tensor", f_u)
                pv = mk.led["pe"]

                def f_ue(e, pv=pv):
                    wait(e, "scalar", "pe", pv)
                    e.copy(uu[:, 0, :], P[4][:, 0:256])
                    e.copy(uu[:, 1, :], P[4][:, 256:512])
                    e.copy(uuT[:, 0, :], P[5][:, 0:256])
                    inc(e.copy(uuT[:, 1, :], P[5][:, 256:512]), "act")
                go("scalar", f_ue)

                def f_p(e):
                    wait(e, "tensor", "act")
                    inc(mmf(e, (P[4][:, 0:256], P[4][:, 256:512]), uuT, uu),
                        "pe")
                go("tensor", f_p)
                pv = mk.led["pe"]

                def f_x1(e, pv=pv):
                    wait(e, "vector", "pe", pv)
                    e.scalar_tensor_tensor(xw[:, 0, :], uu[:, 0, :], 7.0,
                                           P[4][:, 0:256], ALU.mult, ALU.subtract)
                    inc(e.scalar_tensor_tensor(xw[:, 1, :], uu[:, 1, :], 7.0,
                                               P[4][:, 256:512], ALU.mult,
                                               ALU.subtract), "vec")
                go("vector", f_x1)

                def f_r(e):
                    wait(e, "tensor", "vec")
                    inc(mmf(e, (P[4][:, 0:256], P[4][:, 256:512]), uuT, xw),
                        "pe")
                go("tensor", f_r)
                pv = mk.led["pe"]

                def f_x3(e, pv=pv):
                    wait(e, "vector", "pe", pv)
                    e.scalar_tensor_tensor(xw[:, 0, :], uu[:, 0, :], 15.0,
                                           P[4][:, 0:256], ALU.mult, ALU.subtract)
                    inc(e.scalar_tensor_tensor(xw[:, 1, :], uu[:, 1, :], 15.0,
                                               P[4][:, 256:512], ALU.mult,
                                               ALU.subtract), "vec")
                go("vector", f_x3)

                def f_s(e, zc=zc, zTc=zTc):
                    wait(e, "tensor", "vec")
                    mmf(e, (P[4][:, 0:256], P[4][:, 256:512]), zTc, xw)
                    inc(mmf(e, (P[5][:, 0:256], P[5][:, 256:512]), xw, zTc),
                        "pe")
                go("tensor", f_s)
                pv = mk.led["pe"]

                def f_z(e, zc=zc, zTc=zTc, zn=zn, zTn=zTn, pv=pv):
                    wait(e, "vector", "pe", pv)
                    e.scalar_tensor_tensor(zn[:, 0, :], zc[:, 0, :], 13.0,
                                           P[4][:, 0:256], ALU.mult, ALU.subtract)
                    e.scalar_tensor_tensor(zn[:, 1, :], zc[:, 1, :], 13.0,
                                           P[4][:, 256:512], ALU.mult,
                                           ALU.subtract)
                    e.scalar_tensor_tensor(zTn[:, 0, :], zTc[:, 0, :], 13.0,
                                           P[5][:, 0:256], ALU.mult, ALU.subtract)
                    e.scalar_tensor_tensor(zTn[:, 1, :], zTc[:, 1, :], 13.0,
                                           P[5][:, 256:512], ALU.mult,
                                           ALU.subtract)
                    e.drain()
                    e.tensor_scalar_mul(zn[:, 0, :], zn[:, 0, :], 0.25)
                    e.tensor_scalar_mul(zn[:, 1, :], zn[:, 1, :], 0.25)
                    e.tensor_scalar_mul(zTn[:, 0, :], zTn[:, 0, :], 0.25)
                    inc(e.tensor_scalar_mul(zTn[:, 1, :], zTn[:, 1, :], 0.25),
                        "vec")
                go("vector", f_z)
                if hl == 0 and it == 0:
                    tap("u0", lambda: uu[:])
                    tap("uT0", lambda: uuT[:])
                    tap("x3_0", lambda: xw[:])
                    tap("z1", lambda: zn[:])
                    tap("zT1", lambda: zTn[:])
                zc, zn = zn, zc
                zTc, zTn = zTn, zTc
            zT_fin = zTc
            if hl == 0:
                tap("aT", lambda: aT[:])
                tap("zT_fin", lambda: zT_fin[:])
                tap("zA0", lambda: zA[:])
                tap("srow", lambda: srow[:])
                tap("stat2", lambda: stat2[:])
                tap("e2s", lambda: e2s[:])

            for tc in range(8):
                def f_s3(e, tc=tc):
                    wait(e, "tensor", "vec")
                    for h2 in range(2):
                        ii = mm3(e, P[h2][:, 0:512],
                                 lambda k, w, h2=h2: (qlT_hi if w == "hi" else qlT_lo)[:, ts(h2, 128)],
                                 lambda k, w: (kqT_hi if w == "hi" else kqT_lo)[:, ts(tc, 512)],
                                 (0,))
                    inc(ii, "pe")
                go("tensor", f_s3)
                pv = mk.led["pe"]

                def f_s3v(e, tc=tc, pv=pv):
                    wait(e, "vector", "pe", pv)
                    e.reduce_max(mr3t[:, 0, tc:tc + 1], P[0][:, 0:512], axis=AX.X)
                    inc(e.reduce_max(mr3t[:, 1, tc:tc + 1], P[1][:, 0:512],
                                     axis=AX.X), "vec")
                go("vector", f_s3v)

            mnb3 = sba("mnb3", [128, 2], BF16)

            def f_s3m(e):
                wait(e, "vector", "vec")
                e.tensor_reduce(mn3[:], mr3t[:], axis=AX.X, op=ALU.max,
                                negate=True)
                inc(e.drain(), "vec")
            go("vector", f_s3m)

            def f_s3m2(e):
                wait(e, "scalar", "vec")
                inc(e.copy(mnb3[:], mn3[:]), "act")
            go("scalar", f_s3m2)

            def f_s3m3(e):
                wait(e, "vector", "act")
                inc(e.tensor_copy(mn3[:], mnb3[:]), "vec")
            go("vector", f_s3m3)

            def f_s3t(e):
                wait(e, "tensor", "vec")
                wait(e, "tensor", "act")
                e.transpose(Pb[0][0:1, 0:128], mnb3[:, 0:1], identb[:])
                inc(e.transpose(Pb[0][0:1, 128:256], mnb3[:, 1:2], identb[:]),
                    "pe")
            go("tensor", f_s3t)
            pv = mk.led["pe"]

            def f_s3tc(e, pv=pv):
                wait(e, "vector", "pe", pv)
                inc(e.tensor_copy(mrow3[0:1, 0:256], Pb[0][0:1, 0:256]), "vec")
            go("vector", f_s3tc)

            s3h = {}
            for sc in range(32):
                sp = sc % 2
                pr = s3h.get(sc - 2, {})

                def f_v1(e, sc=sc, sp=sp, pr=pr):
                    if sc < 2:
                        wait(e, "tensor", "vec")
                        wait(e, "tensor", "act")
                        wait(e, "tensor", "pool", onev)
                    else:
                        wait(e, "tensor", "act", pr.get("exp", 0))
                    mm3(e, (P[2] if sp == 0 else P[5])[:, 0:256],
                        lambda k, w, sc=sc: (kqT_hi if w == "hi" else kqT_lo)[:, ts(sc, 128)],
                        lambda k, w: (qlT_hi if w == "hi" else qlT_lo)[:],
                        (0,), stop=False)
                    inc(e.matmul((P[2] if sp == 0 else P[5])[:, 0:256], ones_bf[:],
                                 mrow3[0:1, :], start=False, stop=True), "pe")
                go("tensor", f_v1)
                pv = mk.led["pe"]
                s3h.setdefault(sc, {})["mm"] = pv

                def f_v2(e, sc=sc, sp=sp, pv=pv, pr=pr):
                    wait(e, "scalar", "pe", pv)
                    wait(e, "scalar", "vec", pr.get("cast", 0))
                    inc(e.activation(e3f[sp][:], (P[2] if sp == 0 else P[5])[:, 0:256],
                                     AF.Exp, bias=0.0, scale=1.0), "act")
                go("scalar", f_v2)
                av = mk.led["act"]
                s3h[sc]["exp"] = av

                def f_v2b(e, sc=sc, sp=sp, av=av, pr=pr):
                    wait(e, "vector", "act", av)
                    wait(e, "vector", "pe", pr.get("w3", 0))
                    inc(e.tensor_copy(e3h[sp][:], e3f[sp][:]), "vec")
                go("vector", f_v2b)
                vv = mk.led["vec"]
                s3h[sc]["cast"] = vv

                def f_v3(e, sc=sc, sp=sp, vv=vv, pr=pr):
                    wait(e, "gpsimd", "vec", vv)
                    wait(e, "gpsimd", "pe", pr.get("w3", 0))
                    inc(e.tensor_tensor(e3l[sp][:], e3f[sp][:], e3h[sp][:],
                                        ALU.subtract), "pool")
                go("gpsimd", f_v3)
                plv = mk.led["pool"]
                s3h[sc]["sub"] = plv

                def f_v4(e, sc=sc, sp=sp, vv=vv, plv=plv):
                    wait(e, "tensor", "vec", vv)
                    wait(e, "tensor", "pool", plv)
                    for h2 in range(2):
                        e.matmul(P[3 + h2][:, 0:129], e3h[sp][:, ts(h2, 128)],
                                 v_tm[:, sc, 0:129], start=(sc == 0), stop=False)
                        ii = e.matmul(P[3 + h2][:, 0:129],
                                      e3l[sp][:, ts(h2, 128)], v_tm[:, sc, 0:129],
                                      start=False, stop=(sc == 31))
                    inc(ii, "pe")
                go("tensor", f_v4)
                s3h[sc]["w3"] = mk.led["pe"]

            def f_w3(e):
                wait(e, "vector", "pe")
                e.tensor_copy(w3s[:, 0, 0:129], P[3][:, 0:129])
                e.tensor_copy(w3s[:, 1, 0:129], P[4][:, 0:129])
                e.drain()
                e.tensor_copy(stat2[:, 2:4],
                              w3s[:, :, 128:129].rearrange("p c o -> p (c o)"))
                e.drain()
                e.reciprocal(stat2[:, 0:2], stat2[:, 2:4])
                e.tensor_tensor(xw[:, 0, 0:64], w3s[:, 0, 0:64], w3s[:, 0, 64:128],
                                ALU.add)
                e.tensor_tensor(xw[:, 1, 0:64], w3s[:, 1, 0:64], w3s[:, 1, 64:128],
                                ALU.add)
                e.drain()
                e.tensor_scalar(xw[:, 0, 0:64], xw[:, 0, 0:64], stat2[:, 0:1], None,
                                ALU.mult)
                inc(e.tensor_scalar(xw[:, 1, 0:64], xw[:, 1, 0:64], stat2[:, 1:2],
                                    None, ALU.mult), "vec")
            go("vector", f_w3)

            def f_zw(e, zT_fin=zT_fin):
                wait(e, "tensor", "vec")
                for ot in range(2):
                    for k in range(2):
                        ii = e.matmul(P[4][:, ts(ot, 64)],
                                      zT_fin[:, k, ts(ot, 128)], xw[:, k, 0:64],
                                      start=(k == 0), stop=(k == 1))
                inc(ii, "pe")
            go("tensor", f_zw)
            pv = mk.led["pe"]

            def f_zwe(e, pv=pv):
                wait(e, "scalar", "pe", pv)
                e.copy(zw_hi[:, 0, 0:DH], P[4][:, 0:64])
                inc(e.copy(zw_hi[:, 1, 0:DH], P[4][:, 64:128]), "act")
            go("scalar", f_zwe)
            av = mk.led["act"]

            def f_zwl(e, av=av):
                wait(e, "vector", "act", av)
                e.tensor_tensor(zw_lo[:, 0, 0:DH], P[4][:, 0:64],
                                zw_hi[:, 0, 0:DH], ALU.subtract)
                inc(e.tensor_tensor(zw_lo[:, 1, 0:DH], P[4][:, 64:128],
                                    zw_hi[:, 1, 0:DH], ALU.subtract), "vec")
            go("vector", f_zwl)

            def f_zwo(e):
                e.memset(zw_hi[:, :, DH:DH + 1], 1.0)
                inc(e.memset(zw_lo[:, :, DH:DH + 1], 0.0), "pool")
            go("gpsimd", f_zwo)
            if hl == 0:
                tap("mrow3", lambda: mrow3[:])
                tap("w3v", lambda: xw[:, :, 0:64])
                tap("zw_hi", lambda: zw_hi[:])
                tap("zw_lo", lambda: zw_lo[:])

            emit_wsplit(NSPL // 2)

        with contextlib.ExitStack() as sB:
            sbb = lambda n, sh, dt, hl=hl: sB.enter_context(
                nc.sbuf_tensor(f"{n}_{hl}", sh, dt))
            outc = sbb("outc", [64, N + 12], F32)
            acc = [sbb(f"acc{i}", [64, 512], F32) for i in range(2)]
            e1f = sbb("e1f", [128, 2, 512], F32)
            e1h = sbb("e1h", [128, 2, 512], BF16)
            e1l = sbb("e1l", [128, 2, 512], BF16)
            mn1 = sbb("mn1", [128, 32], F32)
            mn1b = sbb("mn1b", [128, 32], BF16)
            rowm = [sbb(f"rowm{i}", [1, 512], BF16) for i in range(2)]
            rowi = sbb("rowi", [1, 512], F32)
            irs = sbb("irs", [64, 512], F32)

            join()

            for tc in range(8):
                def f_qm(e, tc=tc):
                    wait(e, "tensor", "vec")
                    wait(e, "tensor", "act")
                    ii = mm3(e, P[tc % 2][0:64, 0:512],
                             lambda k, w: (whe_hi if w == "hi" else whe_lo)[:, k, 0:64],
                             lambda k, w: (xnT_hi if w == "hi" else xnT_lo)[:, k, ts(tc, 512)],
                             range(8))
                    inc(ii, "pe")
                go("tensor", f_qm)
                pv = mk.led["pe"]

                def f_qa(e, tc=tc, pv=pv):
                    wait(e, "scalar", "pe", pv)
                    inc(e.activation(kqT_hi[:, ts(tc, 512)], P[tc % 2][0:64, 0:512],
                                     AF.Copy, bias=0.0, scale=SCALE), "act")
                go("scalar", f_qa)
                av = mk.led["act"]

                def f_qv(e, tc=tc, pv=pv, av=av):
                    wait(e, "vector", "pe", pv)
                    wait(e, "vector", "act", av)
                    inc(e.scalar_tensor_tensor(kqT_lo[:, ts(tc, 512)],
                                               P[tc % 2][0:64, 0:512], SCALE,
                                               kqT_hi[:, ts(tc, 512)], ALU.mult,
                                               ALU.subtract), "vec")
                go("vector", f_qv)

            if hl == 0:
                tap("qT_hi", lambda: kqT_hi[:])
                tap("qT_lo", lambda: kqT_lo[:])
            s1h = {}
            for it4 in range(8):
                bks = (P[2], P[5]) if it4 % 2 == 0 else (P[0], P[1])
                pr = s1h.get(it4 - 2, {})

                def f_1s(e, it4=it4, bks=bks, pr=pr):
                    if it4 < 2:
                        wait(e, "tensor", "vec")
                        wait(e, "tensor", "act")
                    else:
                        wait(e, "tensor", "vec", pr.get("red", 0))
                    for sub in range(4):
                        ii = mm3(e, bks[sub // 2][:, ts(sub % 2, 256)],
                                 lambda k, w, it4=it4, sub=sub: (kqT_hi if w == "hi" else kqT_lo)[:, ts(4 * it4 + sub, 128)],
                                 lambda k, w: (klT_hi if w == "hi" else klT_lo)[:],
                                 (0,))
                    inc(ii, "pe")
                go("tensor", f_1s)
                pv = mk.led["pe"]

                def f_1sv(e, it4=it4, bks=bks, pv=pv):
                    wait(e, "vector", "pe", pv)
                    for sub in range(4):
                        ii = e.reduce_max(mn1[:, 4 * it4 + sub:4 * it4 + sub + 1],
                                          bks[sub // 2][:, ts(sub % 2, 256)],
                                          axis=AX.X, negate=True)
                    inc(ii, "vec")
                go("vector", f_1sv)
                s1h.setdefault(it4, {})["red"] = mk.led["vec"]

            def f_1sb(e):
                wait(e, "scalar", "vec")
                inc(e.copy(mn1b[:], mn1[:]), "act")
            go("scalar", f_1sb)

            def f_gz(e):
                wait(e, "vector", "vec")
                e.memset(outc[:, 0:6], 0.0)
                inc(e.memset(outc[:, N + 6:N + 12], 0.0), "vec")
            go("vector", f_gz)

            vch = {}

            def tail1(tc):
                pv_o1 = vch[tc]["cast"]
                plv = vch[tc]["sub"]
                pro5 = vch.get(tc - 1, {}).get("o5", 0)

                def f_o4a(e, tc=tc, pv_o1=pv_o1, plv=plv, pro5=pro5):
                    wait(e, "tensor", "vec", pv_o1)
                    wait(e, "tensor", "pool", plv)
                    wait(e, "tensor", "vec", pro5)
                    i = 0
                    for h2 in range(2):
                        for (zz, ee) in ((zw_hi, e1h), (zw_hi, e1l), (zw_lo, e1h)):
                            ii = e.matmul(P[4][0:65, 0:512], zz[:, h2, :],
                                          ee[:, h2, :], start=(i == 0),
                                          stop=(i == 5))
                            i += 1
                    inc(ii, "pe")
                go("tensor", f_o4a)
                pv = mk.led["pe"]
                vch[tc]["o4a"] = pv

                def f_o4b(e, tc=tc, pv=pv):
                    wait(e, "vector", "pe", pv)
                    inc(e.reciprocal(rowi[0:1, 0:512], P[4][64:65, 0:512]), "vec")
                go("vector", f_o4b)
                vv = mk.led["vec"]

                def f_o4c(e, tc=tc, vv=vv):
                    wait(e, "tensor", "vec", vv)
                    inc(e.matmul(P[3][0:64, 0:512], ones_f[0:1, 0:64],
                                 rowi[0:1, 0:512], start=True, stop=True), "pe")
                go("tensor", f_o4c)
                pv2 = mk.led["pe"]

                def f_o5(e, tc=tc, pv2=pv2):
                    wait(e, "vector", "pe", pv2)
                    e.tensor_copy(irs[:], P[3][0:64, 0:512])
                    e.drain()
                    inc(e.tensor_tensor(outc[:, 6 + tc * 512:6 + (tc + 1) * 512],
                                        P[4][0:64, 0:512], irs[:], ALU.mult),
                        "vec")
                go("vector", f_o5)
                vch[tc]["o5"] = mk.led["vec"]

            for tc in range(8):
                sp = tc % 2
                bks = (P[0], P[1]) if sp == 0 else (P[2], P[5])
                pr = vch.get(tc - 2, {})

                def f_rowt(e, tc=tc, sp=sp, pr=pr):
                    wait(e, "tensor", "act")
                    wait(e, "tensor", "vec", pr.get("rowc", 0))
                    for r4 in range(4):
                        ii = e.transpose(Pb[0][0:1, 512 * sp + 128 * r4:
                                               512 * sp + 128 * r4 + 128],
                                         mn1b[:, 4 * tc + r4:4 * tc + r4 + 1],
                                         identb[:])
                    inc(ii, "pe")
                go("tensor", f_rowt)
                pv = mk.led["pe"]

                def f_rowc(e, tc=tc, sp=sp, pv=pv):
                    wait(e, "vector", "pe", pv)
                    inc(e.tensor_copy(rowm[sp][0:1, 0:512],
                                      Pb[0][0:1, ts(sp, 512)]), "vec")
                go("vector", f_rowc)
                vv = mk.led["vec"]
                vch.setdefault(tc, {})["rowc"] = vv

                def f_o1(e, tc=tc, sp=sp, bks=bks, vv=vv, pr=pr):
                    wait(e, "tensor", "vec", vv)
                    wait(e, "tensor", "act", pr.get("exp", 0))
                    for h2 in range(2):
                        mm3(e, bks[h2][:, 0:512],
                            lambda k, w, h2=h2: (klT_hi if w == "hi" else klT_lo)[:, ts(h2, 128)],
                            lambda k, w, tc=tc: (kqT_hi if w == "hi" else kqT_lo)[:, ts(tc, 512)],
                            (0,), stop=False)
                        ii = e.matmul(bks[h2][:, 0:512], ones_bf[:],
                                      rowm[sp][0:1, 0:512], start=False,
                                      stop=True)
                    inc(ii, "pe")
                go("tensor", f_o1)
                pv = mk.led["pe"]

                def f_o2(e, tc=tc, bks=bks, pv=pv, pr=pr):
                    wait(e, "scalar", "pe", pv)
                    wait(e, "scalar", "vec",
                         vch.get(tc - 1, {}).get("cast", 0))
                    e.activation(e1f[:, 0, :], bks[0][:, 0:512], AF.Exp,
                                 bias=0.0, scale=1.0)
                    inc(e.activation(e1f[:, 1, :], bks[1][:, 0:512], AF.Exp,
                                     bias=0.0, scale=1.0), "act")
                go("scalar", f_o2)
                av = mk.led["act"]
                vch[tc]["exp"] = av

                def f_o2b(e, tc=tc, av=av):
                    wait(e, "vector", "act", av)
                    wait(e, "vector", "pe",
                         vch.get(tc - 1, {}).get("o4a", 0))
                    inc(e.tensor_copy(e1h[:], e1f[:]), "vec")
                go("vector", f_o2b)
                vv = mk.led["vec"]
                vch[tc]["cast"] = vv

                def f_o3(e, tc=tc, vv=vv):
                    wait(e, "gpsimd", "vec", vv)
                    wait(e, "gpsimd", "pe",
                         vch.get(tc - 1, {}).get("o4a", 0))
                    inc(e.tensor_tensor(e1l[:], e1f[:], e1h[:], ALU.subtract),
                        "pool")
                go("gpsimd", f_o3)
                vch[tc]["sub"] = mk.led["pool"]

                if tc >= 1:
                    tail1(tc - 1)
            tail1(7)

            for cc_ in range(8):
                def f_cv(e, cc_=cc_, hl=hl):
                    wait(e, "vector", "vec")
                    wait(e, "vector", "d4")
                    a, b = acc[0], acc[1]
                    e.tensor_scalar(a[:], outc[:, cc_ * 512 + 0:cc_ * 512 + 512],
                                    wcol_h[:, 0:1], None, ALU.mult)
                    for j in range(1, KC):
                        src, dst = (a, b) if j % 2 == 1 else (b, a)
                        e.drain()
                        ii = e.scalar_tensor_tensor(
                            dst[:], outc[:, cc_ * 512 + j:cc_ * 512 + j + 512],
                            wcol_h[:, j:j + 1], src[:], ALU.mult, ALU.add)
                    inc(ii, "vec")
                go("vector", f_cv)
                vv = mk.led["vec"]

                def f_st(e, cc_=cc_, hl=hl, vv=vv):
                    wait(e, "sync", "vec", vv)
                    inc(e.dma_start(out=a2a_in.ap()[cc_, ts(hl, 64), :],
                                    in_=acc[0][:]), "d4", 16)
                go("sync", f_st)

            if hl == 0:
                tap("mn1", lambda: mn1[:])
                tap("outc", lambda: outc[:])
            emit_wsplit(NSPL - NSPL // 2)
            join()

    emit_wsplit(len(wsplit_jobs))
    join()
    if stop <= 2:
        def f_fin2(e):
            for s in SEMS:
                wait(e, "sync", s)
        go("sync", f_fin2)
        sc2.close()
        es.close()
        return nc
    sc2.close()

    def f_cc(e):
        for s in SEMS:
            wait(e, "gpsimd", s)
        with allow_ici_all_to_all():
            inc(e.collective_compute(
                "AllToAll", ALU.bypass, replica_groups=[list(range(8))],
                ins=[a2a_in.ap().opt()], outs=[a2a_out.ap().opt()]), "cc")
    go("gpsimd", f_cc)

    y_tm = sb("y_tm", [128, 16, D], F32)
    s34 = contextlib.ExitStack()
    of_hi = s34.enter_context(nc.sbuf_tensor("of_hi", [128, 8, T], BF16))
    of_lo = s34.enter_context(nc.sbuf_tensor("of_lo", [128, 8, T], BF16))
    with contextlib.ExitStack() as s3:
        of_f = s3.enter_context(nc.sbuf_tensor("of_f", [128, 8, T], F32))

        def f_as(e):
            wait(e, "sync", "cc")
            for r in range(8):
                for q in range(4):
                    ii = e.dma_start(
                        out=of_f[:, 4 * (r % 2) + q, ts(r // 2, 512)],
                        in_=a2a_out.ap()[r, ts(q, 128), :])
                    inc(ii, "dsync", 16)
        go("sync", f_as)
        dv = mk.led["dsync"]

        tap("of_f", lambda: of_f[:])
        for k in range(8):
            def f_sp_a(e, k=k, dv=dv):
                wait(e, "scalar", "dsync", dv)
                inc(e.copy(of_hi[:, k, :], of_f[:, k, :]), "act")
            go("scalar", f_sp_a)
            av = mk.led["act"]

            def f_sp_v(e, k=k, av=av):
                wait(e, "vector", "act", av)
                inc(e.tensor_tensor(of_lo[:, k, :], of_f[:, k, :], of_hi[:, k, :],
                                    ALU.subtract), "vec")
            go("vector", f_sp_v)
        join()

    with contextlib.ExitStack() as s4:
        sbl = lambda n, sh, dt: s4.enter_context(nc.sbuf_tensor(n, sh, dt))
        fch = sbl("fch", [128, 8, D], BF16)
        fcl = sbl("fcl", [128, 8, D], BF16)
        bfc_t = sbl("bfc_t", [128, D], F32)
        brx_t = sbl("brx_t", [128, D], F32)
        xres_t = [sbl(f"xres_t{i}", [128, D], F32) for i in range(2)]

        def f_fl(e):
            wait(e, "sync", "dpool")
            inc(e.dma_start(out=fch[:],
                            in_=fc_hi_d.rearrange("(c p) d -> p c d", p=128)),
                "dsync", 16)
            inc(e.dma_start(out=fcl[:],
                            in_=fc_lo_d.rearrange("(c p) d -> p c d", p=128)),
                "dsync", 16)
            inc(e.dma_start(out=bfc_t[:], in_=bfc_rep[:]), "dsync", 16)
        go("sync", f_fl)
        fdv = mk.led["dsync"]

        hist4 = {}
        for tt in range(16):
            dxs = "dx0" if tt % 2 == 0 else "dx1"

            def f_xl(e, tt=tt, dxs=dxs):
                if tt >= 2:
                    wait(e, "sync", "vec", hist4[tt - 2])
                else:
                    wait(e, "sync", "vec")
                inc(e.dma_start(out=xres_t[tt % 2][:], in_=xres[ts(tt, 128), :]),
                    dxs, 16)
            go("sync", f_xl)
            dv = mk.led[dxs]

            def f_fm(e, tt=tt):
                wait(e, "tensor", "dsync", fdv)
                wait(e, "tensor", "vec")
                for half in range(2):
                    ii = mm3(e, P[2 * (tt % 2) + half][:, 0:512],
                             lambda k, w, tt=tt: (of_hi if w == "hi" else of_lo)[:, k, ts(tt, 128)],
                             lambda k, w, half=half: (fch if w == "hi" else fcl)[:, k, ts(half, 512)],
                             range(8))
                inc(ii, "pe")
            go("tensor", f_fm)
            pv = mk.led["pe"]

            def f_fe(e, tt=tt, pv=pv, dv=dv, dxs=dxs):
                wait(e, "vector", "pe", pv)
                wait(e, "vector", dxs, dv)
                e.tensor_tensor(brx_t[:], bfc_t[:], xres_t[tt % 2][:], ALU.add)
                e.drain()
                for half in range(2):
                    ii = e.tensor_tensor(y_tm[:, tt, ts(half, 512)],
                                         P[2 * (tt % 2) + half][:, 0:512],
                                         brx_t[:, ts(half, 512)], ALU.add)
                inc(ii, "vec")
            go("vector", f_fe)
            hist4[tt] = mk.led["vec"]
        tap("y_fc", lambda: y_tm[:])
        join()
    s34.close()
    join()

    comb = sb("comb", [128, 16, NE], F32)
    gwh = sb("gwh", [128, 8, NE], BF16)
    gwl = sb("gwl", [128, 8, NE], BF16)
    with contextlib.ExitStack() as sg:
        gwf = sg.enter_context(nc.sbuf_tensor("gwf", [128, 8, NE], F32))

        def f_gl(e):
            inc(e.dma_start(out=gwf[:],
                            in_=gate_wT.rearrange("(c p) d -> p c d", p=128)),
                "dsync", 16)
        go("sync", f_gl)

        def f_gc(e):
            wait(e, "scalar", "dsync")
            inc(e.copy(gwh[:], gwf[:]), "act")
        go("scalar", f_gc)

        def f_gv(e):
            wait(e, "vector", "act")
            inc(e.tensor_tensor(gwl[:], gwf[:], gwh[:], ALU.subtract), "vec")
        go("vector", f_gv)
        join()

    be_t = sb("be_t", [128, NE], F32)
    go("sync", lambda e: inc(e.dma_start(out=be_t[:], in_=be_rep[:]), "dsync", 16))

    for half in range(2):
        with contextlib.ExitStack() as s5:
            sbl = lambda n, sh, dt, half=half: s5.enter_context(
                nc.sbuf_tensor(f"{n}_{half}", sh, dt))
            xn2T_hi = sbl("xn2T_hi", [128, 8, 1024], BF16)
            xn2T_lo = sbl("xn2T_lo", [128, 8, 1024], BF16)
            x2f = sbl("x2f", [128, D], F32)
            x2h = sbl("x2h", [128, D], BF16)
            x2l = sbl("x2l", [128, D], BF16)
            st5 = sbl("st5", [128, 8], F32)
            sco = sbl("sco", [128, NE], F32)
            r_t2 = sbl("r_t2", [128, 4], F32)
            r_tmp = sbl("r_tmp", [128, NE], F32)
            r_msk = sbl("r_msk", [128, NE], F32)
            r_cnt = sbl("r_cnt", [128, NE], F32)
            r_eq = sbl("r_eq", [128, NE], F32)
            r_a = sbl("r_a", [128, NE], F32)
            r_b = sbl("r_b", [128, NE], F32)
            rbuf = sbl("rbuf", [128, NE, NE], F32)
            rbuf2 = sbl("rbuf2", [128, NE, NE], F32)

            for ttl in range(8):
                tt = half * 8 + ttl

                def f_l2(e, tt=tt):
                    wait(e, "vector", "act")
                    inc(e.reduce_sum(st5[:, 0:1], y_tm[:, tt, :], axis=AX.X,
                                     negate=True), "vec")
                go("vector", f_l2)
                vv = mk.led["vec"]

                def f_l2a(e, tt=tt, vv=vv):
                    wait(e, "scalar", "vec", vv)
                    inc(e.activation(x2f[:], y_tm[:, tt, :], AF.Square, bias=0.0,
                                     scale=1.0, accum_out=st5[:, 1:2]), "act")
                go("scalar", f_l2a)

                def f_l2v(e, tt=tt):
                    wait(e, "vector", "act")
                    e.drain()
                    e.tensor_scalar_mul(st5[:, 2:3], st5[:, 0:1], 1.0 / D)
                    e.drain()
                    e.tensor_tensor(st5[:, 3:4], st5[:, 2:3], st5[:, 2:3],
                                    ALU.mult)
                    e.tensor_scalar_mul(st5[:, 4:5], st5[:, 1:2], 1.0 / D)
                    e.drain()
                    e.tensor_tensor(st5[:, 0:1], st5[:, 4:5], st5[:, 3:4],
                                    ALU.subtract)
                    e.drain()
                    inc(e.tensor_scalar_add(st5[:, 0:1], st5[:, 0:1], 1e-5),
                        "vec")
                go("vector", f_l2v)

                def f_l2r(e, tt=tt):
                    wait(e, "scalar", "vec")
                    inc(e.activation(st5[:, 5:6], st5[:, 0:1], AF.Sqrt,
                                     bias=0.0, scale=1.0), "act")
                go("scalar", f_l2r)

                def f_l2b(e, tt=tt):
                    wait(e, "vector", "act")
                    e.reciprocal(st5[:, 1:2], st5[:, 5:6])
                    e.drain()
                    e.tensor_tensor(st5[:, 6:7], st5[:, 2:3], st5[:, 1:2],
                                    ALU.mult)
                    inc(e.drain(), "vec")
                go("vector", f_l2b)

                def f_l2x(e, tt=tt):
                    wait(e, "vector", "vec")
                    inc(e.tensor_scalar(x2f[:], y_tm[:, tt, :], st5[:, 1:2],
                                        st5[:, 6:7], ALU.mult, ALU.add), "vec")
                go("vector", f_l2x)

                def f_l2h(e, tt=tt):
                    wait(e, "scalar", "vec")
                    wait(e, "scalar", "pe")
                    inc(e.copy(x2h[:], x2f[:]), "act")
                go("scalar", f_l2h)
                av = mk.led["act"]

                def f_l2l(e, tt=tt, av=av):
                    wait(e, "vector", "act", av)
                    wait(e, "vector", "pe")
                    e.drain()
                    inc(e.tensor_tensor(x2l[:], x2f[:], x2h[:], ALU.subtract),
                        "vec")
                go("vector", f_l2l)
                vv = mk.led["vec"]

                for k in range(8):
                    def f_l2t(e, k=k):
                        wait(e, "tensor", "vec")
                        wait(e, "tensor", "act")
                        e.transpose(Pb[k % 2][:, 0:128], x2h[:, ts(k, 128)],
                                    identb[:])
                        inc(e.transpose(Pb[k % 2][:, 128:256],
                                        x2l[:, ts(k, 128)], identb[:]), "pe")
                    go("tensor", f_l2t)
                    pv = mk.led["pe"]

                    def f_l2tc(e, k=k, ttl=ttl, pv=pv):
                        wait(e, "vector", "pe", pv)
                        e.tensor_copy(xn2T_hi[:, k, ts(ttl, 128)],
                                      Pb[k % 2][:, 0:128])
                        inc(e.tensor_copy(xn2T_lo[:, k, ts(ttl, 128)],
                                          Pb[k % 2][:, 128:256]), "vec")
                    go("vector", f_l2tc)
                vv = mk.led["vec"]

                def f_rl(e, ttl=ttl, vv=vv):
                    wait(e, "tensor", "vec", vv)
                    wait(e, "tensor", "act")
                    ii = mm3(e, P[4][:, 0:NE],
                             lambda k, w, ttl=ttl: (xn2T_hi if w == "hi" else xn2T_lo)[:, k, ts(ttl, 128)],
                             lambda k, w: (gwh if w == "hi" else gwl)[:, k, :],
                             range(8))
                    inc(ii, "pe")
                go("tensor", f_rl)
                pv = mk.led["pe"]

                def f_rs(e, tt=tt, pv=pv):
                    wait(e, "scalar", "pe", pv)
                    inc(e.activation(sco[:], P[4][:, 0:NE], AF.Sigmoid,
                                     bias=0.0, scale=1.0), "act")
                go("scalar", f_rs)

                def f_rt(e, tt=tt):
                    wait(e, "vector", "act")
                    wait(e, "vector", "dsync")
                    e.drain()
                    e.tensor_tensor(r_tmp[:], sco[:], be_t[:], ALU.add)
                    e.drain()
                    g = r_tmp[:].rearrange("p (g i) -> p g i", i=4)
                    pairs = [(a_, b_) for a_ in range(4) for b_ in range(a_ + 1, 4)]
                    for pi, (a_, b_) in enumerate(pairs):
                        e.tensor_tensor(rbuf[:, pi, 0:4], g[:, :, a_],
                                        g[:, :, b_], ALU.add)
                    e.drain()
                    e.tensor_reduce(r_a[:, 0:4],
                                    rbuf[:, 0:6, 0:4].rearrange("p j g -> p g j"),
                                    axis=AX.X, op=ALU.max)
                    e.drain()
                    e.memset(rbuf[:], 0.0)
                    e.drain()
                    for j in range(4):
                        e.tensor_scalar(rbuf[:, j, 0:4], r_a[:, 0:4],
                                        r_a[:, j:j + 1], None, ALU.is_ge)
                        if j < 3:
                            e.tensor_scalar(rbuf[:, 4 + j, j + 1:4],
                                            r_a[:, j + 1:4], r_a[:, j:j + 1],
                                            None, ALU.is_equal)
                    e.drain()
                    e.tensor_reduce(r_b[:, 0:4],
                                    rbuf[:, 0:4, 0:4].rearrange("p j g -> p g j"),
                                    axis=AX.X, op=ALU.add)
                    e.tensor_reduce(r_b[:, 4:8],
                                    rbuf[:, 4:7, 0:4].rearrange("p j g -> p g j"),
                                    axis=AX.X, op=ALU.add)
                    e.drain()
                    e.tensor_tensor(r_a[:, 4:8], r_b[:, 0:4], r_b[:, 4:8],
                                    ALU.subtract)
                    e.drain()
                    e.tensor_scalar(r_b[:, 0:4], r_a[:, 4:8], 3.0, None,
                                    ALU.is_ge)
                    e.drain()
                    for i_ in range(4):
                        e.tensor_copy(r_a[:].rearrange("p (g i) -> p g i", i=4)[:, :, i_],
                                      r_b[:, 0:4])
                    e.drain()
                    e.tensor_tensor(r_msk[:], r_tmp[:], r_a[:], ALU.mult)
                    e.drain()
                    e.memset(rbuf[:], 0.0)
                    e.memset(rbuf2[:], 0.0)
                    e.drain()
                    for j in range(NE):
                        e.tensor_scalar(rbuf[:, j, :], r_msk[:],
                                        r_msk[:, j:j + 1], None, ALU.is_ge)
                        if j < NE - 1:
                            e.tensor_scalar(rbuf2[:, j, j + 1:], r_msk[:, j + 1:],
                                            r_msk[:, j:j + 1], None, ALU.is_equal)
                    e.drain()
                    e.tensor_reduce(r_a[:], rbuf[:].rearrange("p j g -> p g j"),
                                    axis=AX.X, op=ALU.add)
                    e.tensor_reduce(r_b[:], rbuf2[:].rearrange("p j g -> p g j"),
                                    axis=AX.X, op=ALU.add)
                    e.drain()
                    e.tensor_tensor(r_cnt[:], r_a[:], r_b[:], ALU.subtract)
                    e.drain()
                    e.tensor_scalar(r_eq[:], r_cnt[:], 13.0, None, ALU.is_ge)
                    e.drain()
                    e.tensor_tensor(r_a[:], sco[:], r_eq[:], ALU.mult)
                    e.drain()
                    e.reduce_sum(st5[:, 7:8], r_a[:], axis=AX.X)
                    e.drain()
                    e.tensor_scalar_add(st5[:, 7:8], st5[:, 7:8], 1e-20)
                    e.drain()
                    e.reciprocal(st5[:, 3:4], st5[:, 7:8])
                    e.drain()
                    inc(e.tensor_scalar(comb[:, tt, :], r_a[:], st5[:, 3:4],
                                        2.5, ALU.mult, ALU.mult), "vec")
                go("vector", f_rt)

            uph = [sbl(f"uph{i}", [128, 8, MI], BF16) for i in range(2)]
            upl = [sbl(f"upl{i}", [128, 8, MI], BF16) for i in range(2)]
            dnh = [sbl(f"dnh{i}", [128, 4, D], BF16) for i in range(2)]
            dnl = [sbl(f"dnl{i}", [128, 4, D], BF16) for i in range(2)]
            ytmp = x2f
            hrel = sbl("hrel", [128, 4, 512], F32)
            hpro = sbl("hpro", [128, 4, 512], F32)
            hh = [sbl(f"hh{i}", [128, 4, 512], BF16) for i in range(2)]
            hlo = [sbl(f"hlo{i}", [128, 4, 512], BF16) for i in range(2)]
            uhist = {}
            HALF = half
            steps = [(u, tc) for u in range(NU) for tc in range(2)]
            st_hist = {}

            def emit_load(u):
                bi = u % 2
                dus = "du0" if bi == 0 else "du1"

                def f_uw(e, u=u, bi=bi, dus=dus):
                    if u >= 2:
                        wait(e, "sync", "pe", uhist[u - 2])
                    else:
                        wait(e, "sync", "pe")
                    wait(e, "sync", "dpool" if u < 2 else "dpool")
                    inc(e.dma_start(out=uph[bi][:],
                                    in_=up_hi_d[u].rearrange("(c p) m -> p c m", p=128)),
                        dus, 16)
                    inc(e.dma_start(out=upl[bi][:],
                                    in_=up_lo_d[u].rearrange("(c p) m -> p c m", p=128)),
                        dus, 16)
                    inc(e.dma_start(out=dnh[bi][:],
                                    in_=dn_hi_d[u].rearrange("(c p) d -> p c d", p=128)),
                        dus, 16)
                    inc(e.dma_start(out=dnl[bi][:],
                                    in_=dn_lo_d[u].rearrange("(c p) d -> p c d", p=128)),
                        dus, 16)
                go("sync", f_uw)
                return mk.led[dus]

            udvs = {}
            udvs[0] = emit_load(0)
            udvs[1] = emit_load(1)

            def emit_up(s):
                u, tc = steps[s]
                bi = u % 2
                dus = "du0" if bi == 0 else "du1"
                prev_relu = st_hist.get(s - 1, {}).get("relu", 0)

                def f_up(e, u=u, bi=bi, tc=tc, s=s, dus=dus, prev_relu=prev_relu):
                    wait(e, "tensor", dus, udvs[u])
                    if s == 0:
                        wait(e, "tensor", "act")
                        wait(e, "tensor", "vec")
                    else:
                        wait(e, "tensor", "act", prev_relu)
                    for it in range(4):
                        ii = mm3(e, P[it][:, 0:512],
                                 lambda k, w, bi=bi, it=it: (uph[bi] if w == "hi" else upl[bi])[:, k, ts(it, 128)],
                                 lambda k, w, tc=tc: (xn2T_hi if w == "hi" else xn2T_lo)[:, k, ts(tc, 512)],
                                 range(8))
                    inc(ii, "pe")
                go("tensor", f_up)
                st_hist.setdefault(s, {})["up"] = mk.led["pe"]

            def emit_hchain(s):
                u, tc = steps[s]
                sp = s % 2
                prev_prod = st_hist.get(s - 1, {}).get("prod", 0)
                upv = st_hist[s]["up"]

                def f_hr(e, upv=upv, prev_prod=prev_prod):
                    wait(e, "scalar", "pe", upv)
                    wait(e, "scalar", "vec", prev_prod)
                    for it in range(4):
                        ii = e.activation(hrel[:, it, :], P[it][:, 0:512],
                                          AF.Relu, bias=0.0, scale=1.0)
                    inc(ii, "act")
                go("scalar", f_hr)
                st_hist[s]["relu"] = mk.led["act"]
                av = mk.led["act"]
                prev_hh = st_hist.get(s - 1, {}).get("hhdma", 0)
                prev_hlo = st_hist.get(s - 1, {}).get("hlo", 0)

                def f_hp(e, av=av, prev_hh=prev_hh, prev_hlo=prev_hlo):
                    wait(e, "vector", "act", av)
                    wait(e, "vector", "dpool", prev_hh)
                    wait(e, "vector", "pool", prev_hlo)
                    inc(e.scalar_tensor_tensor(hpro[:], hrel[:], 1.0,
                                               hrel[:], ALU.mult, ALU.mult),
                        "vec")
                go("vector", f_hp)
                st_hist[s]["prod"] = mk.led["vec"]
                vv = mk.led["vec"]
                prev_dn = st_hist.get(s - 2, {}).get("dn", 0)

                def f_hc(e, sp=sp, vv=vv, prev_dn=prev_dn):
                    wait(e, "gpsimd", "vec", vv)
                    wait(e, "gpsimd", "pe", prev_dn)
                    inc(e.dma_start(out=hh[sp][:], in_=hpro[:]), "dpool", 16)
                go("gpsimd", f_hc)
                st_hist[s]["hhdma"] = mk.led["dpool"]
                dpv = mk.led["dpool"]

                def f_hl(e, sp=sp, dpv=dpv, prev_dn=prev_dn):
                    wait(e, "gpsimd", "dpool", dpv)
                    inc(e.tensor_tensor(hlo[sp][:], hpro[:], hh[sp][:],
                                        ALU.subtract), "pool")
                go("gpsimd", f_hl)
                st_hist[s]["hlo"] = mk.led["pool"]

            def emit_down(s):
                u, tc = steps[s]
                bi = u % 2
                sp = s % 2
                half = HALF
                plv = st_hist[s]["hlo"]
                dpv = st_hist[s]["hhdma"]
                for tsub in range(4):
                    gt = half * 8 + tc * 4 + tsub

                    def f_dn(e, u=u, bi=bi, tsub=tsub, sp=sp, plv=plv, dpv=dpv):
                        wait(e, "tensor", "pool", plv)
                        wait(e, "tensor", "dpool", dpv)
                        wait(e, "tensor", "vec")
                        for h2 in range(2):
                            ii = mm3(e, P[4 + h2][:, 0:512],
                                     lambda k, w, tsub=tsub, sp=sp: (hh[sp] if w == "hi" else hlo[sp])[:, k, ts(tsub, 128)],
                                     lambda k, w, bi=bi, h2=h2: (dnh[bi] if w == "hi" else dnl[bi])[:, k, ts(h2, 512)],
                                     range(4))
                        inc(ii, "pe")
                    go("tensor", f_dn)
                    pv = mk.led["pe"]

                    def f_ya(e, u=u, gt=gt, pv=pv):
                        wait(e, "vector", "pe", pv)
                        for h2 in range(2):
                            ps = P[4 + h2][:, 0:512]
                            if u < NE:
                                e.scalar_tensor_tensor(
                                    ytmp[:, ts(h2, 512)], ps,
                                    comb[:, gt, u:u + 1],
                                    y_tm[:, gt, ts(h2, 512)], ALU.mult,
                                    ALU.add)
                            else:
                                e.tensor_tensor(
                                    ytmp[:, ts(h2, 512)], ps,
                                    y_tm[:, gt, ts(h2, 512)], ALU.add)
                        e.drain()
                        inc(e.tensor_copy(y_tm[:, gt, :], ytmp[:]), "vec")
                    go("vector", f_ya)
                st_hist[s]["dn"] = mk.led["pe"]
                uhist[u] = mk.led["pe"]

            for s in range(len(steps)):
                u, tc = steps[s]
                emit_up(s)
                emit_hchain(s)
                if s >= 1:
                    emit_down(s - 1)
                if tc == 0 and u >= 1 and u + 1 < NU:
                    udvs[u + 1] = emit_load(u + 1)
            emit_down(len(steps) - 1)
            join()

    def f_out(e):
        for s in SEMS:
            wait(e, "sync", s)
        inc(e.dma_start(out=yout.rearrange("(c p) d -> p c d", p=128),
                        in_=y_tm[:]), "dsync", 16)
        wait(e, "sync", "dsync")
    go("sync", f_out)

    es.close()
    return nc


_NC_CACHE = [None]


def stage_inputs(L0_attn_x, L0_attn_w_qkv, L0_attn_w_conv, L0_attn_w_fc,
                 L0_attn_b_fc, L0_ffn_gate_w, L0_ffn_bias_e, L0_ffn_up_w,
                 L0_ffn_down_w, L0_ffn_shared_up, L0_ffn_shared_down):
    x = np.ascontiguousarray(np.asarray(L0_attn_x, dtype=np.float32))
    w_qkv = np.asarray(L0_attn_w_qkv, dtype=np.float32)
    w_conv = np.asarray(L0_attn_w_conv, dtype=np.float32)[:, 0, :, 0]
    w_fc = np.ascontiguousarray(np.asarray(L0_attn_w_fc, dtype=np.float32))
    b_fc = np.asarray(L0_attn_b_fc, dtype=np.float32)
    gate_w = np.asarray(L0_ffn_gate_w, dtype=np.float32)
    bias_e = np.asarray(L0_ffn_bias_e, dtype=np.float32)
    up_w = np.asarray(L0_ffn_up_w, dtype=np.float32)
    down_w = np.asarray(L0_ffn_down_w, dtype=np.float32)
    shared_up = np.asarray(L0_ffn_shared_up, dtype=np.float32)
    shared_down = np.asarray(L0_ffn_shared_down, dtype=np.float32)

    import ml_dtypes
    identf = np.eye(128, dtype=np.float32)
    identb = np.eye(128, dtype=np.float32).astype(ml_dtypes.bfloat16)
    bfc_rep = np.ascontiguousarray(np.broadcast_to(b_fc, (128, D)))
    be_rep = np.ascontiguousarray(np.broadcast_to(bias_e, (128, NE)))
    gate_wT = np.ascontiguousarray(gate_w.T)
    up_all = np.empty((NU, D, MI), dtype=np.float32)
    dn_all = np.empty((NU, MI, D), dtype=np.float32)
    for e in range(NE):
        up_all[e] = up_w[e].T
        dn_all[e] = down_w[e].T
    for u in range(4):
        up_all[NE + u] = shared_up[ts(u, 512), :].T
        dn_all[NE + u] = shared_down[:, ts(u, 512)].T

    in_maps = []
    for c in range(8):
        b, hg = c // 2, c % 2
        wq = np.empty((HL, D, 3 * DH), dtype=np.float32)
        for hll in range(HL):
            h = 8 * hg + hll
            wq[hll, :, 0:64] = w_qkv[:, ts(h, 64)]
            wq[hll, :, 64:128] = w_qkv[:, 1024 + 64 * h:1024 + 64 * h + 64]
            wq[hll, :, 128:192] = w_qkv[:, 2048 + 64 * h:2048 + 64 * h + 64]
        wcol_c = np.ascontiguousarray(
            np.repeat(w_conv[ts(hg, 8), :], 64, axis=0))
        in_maps.append({
            "xT": np.ascontiguousarray(x[b].T),
            "xtm": np.ascontiguousarray(x[b]),
            "xres": np.ascontiguousarray(
                x[:, ts(c, 512), :].reshape(T, D)),
            "wqkv": wq,
            "wcol": wcol_c,
            "wfc": w_fc,
            "bfc_rep": bfc_rep,
            "gate_wT": gate_wT,
            "be_rep": be_rep,
            "up_wT": up_all,
            "down_wT": dn_all,
            "identf": identf,
            "identb": identb,
        })
    return in_maps


def kernel(**inputs):
    in_maps = stage_inputs(**inputs)
    if _NC_CACHE[0] is None:
        _NC_CACHE[0] = build_program()
    nc = _NC_CACHE[0]
    res = run_bass_kernel_spmd(nc, in_maps, core_ids=list(range(8)))
    out = np.zeros((BT, N, D), dtype=np.float32)
    for c in range(8):
        yc = np.asarray(res.results[c]["y"])
        for b in range(BT):
            out[b, ts(c, 512), :] = yc[ts(b, 512), :]
    return out


# version 12 (final)
# speedup vs baseline: 1.0767x; 1.0012x over previous
import contextlib
import numpy as np

import concourse.bass as bass
import concourse.mybir as mybir
from concourse.bass_utils import run_bass_kernel_spmd
from concourse.aot_env import allow_ici_all_to_all

F32 = mybir.dt.float32
BF16 = mybir.dt.bfloat16
AF = mybir.ActivationFunctionType
ALU = mybir.AluOpType
AX = mybir.AxisListType

D, N, BT, HL, DH, M, ITERS, KC = 1024, 4096, 4, 8, 64, 256, 6, 13
T, NE, NU, MI = 2048, 16, 20, 512
SCALE = 0.125

ENGS = ("tensor", "vector", "scalar", "gpsimd", "sync")
SEMS = ("pe", "vec", "act", "pool", "dsync", "dpool", "cc",
        "dx0", "dx1", "dw", "d3", "d4", "du0", "du1")


def ts(i, n):
    return slice(i * n, (i + 1) * n)


class MK:

    def __init__(self, nc, es, blk):
        self.nc, self.es, self.blk = nc, es, blk
        self.led = {s: 0 for s in SEMS}
        self.sem = {s: es.enter_context(nc.semaphore(s)) for s in SEMS}
        self.lw = {s: {} for s in SEMS}

    def go(self, eng, fn):
        getattr(self.blk, eng)(fn)

    def inc(self, instr, s, n=1):
        instr.then_inc(self.sem[s], n)
        self.led[s] += n
        return instr

    def wait(self, e, ename, s, val=None):
        v = self.led[s] if val is None else val
        if v <= 0 or self.lw[s].get(ename, -1) >= v:
            return
        e.wait_ge(self.sem[s], v)
        self.lw[s][ename] = v

    def join(self):
        snap = dict(self.led)
        for eng in ENGS:
            def f(e, eng=eng, snap=snap):
                for s in SEMS:
                    self.wait(e, eng, s, snap[s])
            self.go(eng, f)

    def sb(self, name, shape, dt):
        return self.es.enter_context(self.nc.sbuf_tensor(name, shape, dt))


def mm3(e, out_ap, lhs, rhs, ks, start=True, stop=True):
    seq = (("hi", "hi"), ("hi", "lo"), ("lo", "hi"))
    tot = len(seq) * len(ks)
    i = 0
    ii = None
    for lw, rw in seq:
        for k in ks:
            ii = e.matmul(out_ap, lhs(k, lw), rhs(k, rw),
                          start=(start and i == 0), stop=(stop and i == tot - 1))
            i += 1
    return ii


def mmf(e, outs, lhsT, rhs, start=True, stop=True):
    ii = None
    for ot in range(2):
        for k in range(2):
            ii = e.matmul(outs[ot], lhsT[:, k, ts(ot, 128)], rhs[:, k, :],
                          start=(start and k == 0), stop=(stop and k == 1))
    return ii


def build_program(dbg=(), stop=99):
    nc = bass.Bass("TRN2", target_bir_lowering=False, debug=False)
    es = contextlib.ExitStack()

    xT = nc.dram_tensor("xT", [D, N], F32, kind="ExternalInput").ap()
    xtm = nc.dram_tensor("xtm", [N, D], F32, kind="ExternalInput").ap()
    xres = nc.dram_tensor("xres", [T, D], F32, kind="ExternalInput").ap()
    wqkv = nc.dram_tensor("wqkv", [HL, D, 3 * DH], F32, kind="ExternalInput").ap()
    wcol = nc.dram_tensor("wcol", [8 * DH, KC], F32, kind="ExternalInput").ap()
    wfc = nc.dram_tensor("wfc", [D, D], F32, kind="ExternalInput").ap()
    bfc_rep = nc.dram_tensor("bfc_rep", [128, D], F32, kind="ExternalInput").ap()
    gate_wT = nc.dram_tensor("gate_wT", [D, NE], F32, kind="ExternalInput").ap()
    be_rep = nc.dram_tensor("be_rep", [128, NE], F32, kind="ExternalInput").ap()
    up_wT = nc.dram_tensor("up_wT", [NU, D, MI], F32, kind="ExternalInput").ap()
    down_wT = nc.dram_tensor("down_wT", [NU, MI, D], F32, kind="ExternalInput").ap()
    identf_d = nc.dram_tensor("identf", [128, 128], F32, kind="ExternalInput").ap()
    identb_d = nc.dram_tensor("identb", [128, 128], BF16, kind="ExternalInput").ap()
    yout = nc.dram_tensor("y", [T, D], F32, kind="ExternalOutput").ap()

    wq_hi_d = nc.dram_tensor("wq_hi_d", [HL, D, 3 * DH], BF16).ap()
    wq_lo_d = nc.dram_tensor("wq_lo_d", [HL, D, 3 * DH], BF16).ap()
    up_hi_d = nc.dram_tensor("up_hi_d", [NU, D, MI], BF16).ap()
    up_lo_d = nc.dram_tensor("up_lo_d", [NU, D, MI], BF16).ap()
    dn_hi_d = nc.dram_tensor("dn_hi_d", [NU, MI, D], BF16).ap()
    dn_lo_d = nc.dram_tensor("dn_lo_d", [NU, MI, D], BF16).ap()
    fc_hi_d = nc.dram_tensor("fc_hi_d", [D, D], BF16).ap()
    fc_lo_d = nc.dram_tensor("fc_lo_d", [D, D], BF16).ap()
    a2a_in = nc.dram_tensor("a2a_in", [8, 512, 512], F32)
    a2a_out = nc.dram_tensor("a2a_out", [8, 512, 512], F32)

    blk = es.enter_context(nc.Block())
    mk = MK(nc, es, blk)
    go, inc, wait, join, sb = mk.go, mk.inc, mk.wait, mk.join, mk.sb

    def tap(name, ap_fn):
        if name not in dbg:
            return
        ap = ap_fn() if callable(ap_fn) else ap_fn
        od = nc.dram_tensor("dbg_" + name, list(ap.shape),
                            ap.dtype, kind="ExternalOutput").ap()
        snap = dict(mk.led)

        def f(e):
            for s in SEMS:
                wait(e, "sync", s, snap[s])
            inc(e.dma_start(out=od, in_=ap), "dsync", 16)
        go("sync", f)

    identf = sb("identf_t", [128, 128], F32)
    identb = sb("identb_t", [128, 128], BF16)
    ones_bf = sb("ones_bf", [1, 128], BF16)
    ones_f = sb("ones_f", [1, 128], F32)
    onescol_f = sb("onescol_f", [128, 1], F32)
    onescol_bf = sb("onescol_bf", [128, 1], BF16)
    wcol_t = sb("wcol_t", [128, 4, KC], F32)
    sc2 = contextlib.ExitStack()
    sc = lambda n, sh, dt: sc2.enter_context(nc.sbuf_tensor(n, sh, dt))
    xnT_hi = sc("xnT_hi", [128, 8, N], BF16)
    xnT_lo = sc("xnT_lo", [128, 8, N], BF16)
    xm_hi = sc("xm_hi", [128, 8, M], BF16)
    xm_lo = sc("xm_lo", [128, 8, M], BF16)
    P = [es.enter_context(nc.psum_tensor(f"P{i}", [128, 512], F32)) for i in range(6)]
    Pb = [es.enter_context(nc.psum_tensor(f"Pb{i}", [128, 1024], BF16)) for i in range(2)]

    WSP = 1024

    def flat(ap_, nelem):
        names = "abcd"[:len(ap_.shape)]
        pat = f"{' '.join(names)} -> ({' '.join(names)})"
        return ap_.rearrange(pat).rearrange("(p f) -> p f", p=128), nelem // 128

    wsplit_jobs = []
    for (src, hid, lod, nel) in ((wfc, fc_hi_d, fc_lo_d, D * D),
                                 (up_wT, up_hi_d, up_lo_d, NU * D * MI),
                                 (down_wT, dn_hi_d, dn_lo_d, NU * MI * D)):
        fs, per = flat(src, nel)
        fh, _ = flat(hid, nel)
        fl, _ = flat(lod, nel)
        for c0 in range(0, per, WSP):
            c1 = min(c0 + WSP, per)
            wsplit_jobs.append((fs[:, c0:c1], fh[:, c0:c1], fl[:, c0:c1]))
    wsp_f = sc("wsp_f", [128, WSP], F32)
    wsp_h = sc("wsp_h", [128, WSP], BF16)
    wsp_l = sc("wsp_l", [128, WSP], BF16)
    wsp_i = [0]

    def emit_wsplit(kn):
        for _ in range(kn):
            if wsp_i[0] >= len(wsplit_jobs):
                return
            src, hid, lod = wsplit_jobs[wsp_i[0]]
            wsp_i[0] += 1

            def f_load(e, src=src):
                wait(e, "sync", "dpool")
                wait(e, "sync", "vec")
                inc(e.dma_start(out=wsp_f[:], in_=src), "d3", 16)
            go("sync", f_load)

            def f_cast(e):
                wait(e, "gpsimd", "d3")
                wait(e, "gpsimd", "dpool")
                wait(e, "gpsimd", "vec")
                inc(e.dma_start(out=wsp_h[:], in_=wsp_f[:]), "dpool", 16)
            go("gpsimd", f_cast)

            def f_lo(e):
                wait(e, "vector", "dpool")
                inc(e.tensor_tensor(wsp_l[:], wsp_f[:], wsp_h[:], ALU.subtract),
                    "vec")
            go("vector", f_lo)

            def f_store(e, hid=hid, lod=lod):
                wait(e, "gpsimd", "vec")
                inc(e.dma_start(out=hid, in_=wsp_h[:]), "dpool", 16)
                inc(e.dma_start(out=lod, in_=wsp_l[:]), "dpool", 16)
            go("gpsimd", f_store)

    def p0_sync(e):
        inc(e.dma_start(out=identf[:], in_=identf_d[:]), "dsync", 16)
        inc(e.dma_start(out=identb[:], in_=identb_d[:]), "dsync", 16)
        inc(e.dma_start(out=wcol_t[:],
                        in_=wcol.rearrange("(c p) k -> p c k", p=128)),
            "dsync", 16)
    go("sync", p0_sync)

    def p0_pool(e):
        e.memset(ones_bf[:], 1.0)
        e.memset(ones_f[:], 1.0)
        e.memset(onescol_bf[:], 1.0)
        inc(e.memset(onescol_f[:], 1.0), "pool")
    go("gpsimd", p0_pool)

    wq_flat, _ = flat(wqkv, HL * D * 3 * DH)
    wqh_flat, _ = flat(wq_hi_d, HL * D * 3 * DH)
    wql_flat, _ = flat(wq_lo_d, HL * D * 3 * DH)
    with contextlib.ExitStack() as s0:
        wq_f = s0.enter_context(nc.sbuf_tensor("wq_f", [128, 6144], F32))
        wq_h = s0.enter_context(nc.sbuf_tensor("wq_h", [128, 6144], BF16))
        wq_l = s0.enter_context(nc.sbuf_tensor("wq_l", [128, 6144], BF16))
        for ih in range(2):
            sl = ts(ih, 6144)

            def p0_ld(e, sl=sl):
                wait(e, "sync", "dpool")
                inc(e.dma_start(out=wq_f[:], in_=wq_flat[:, sl]), "dsync", 16)
            go("sync", p0_ld)

            def p0_act(e):
                wait(e, "scalar", "dsync")
                inc(e.copy(wq_h[:], wq_f[:]), "act")
            go("scalar", p0_act)

            def p0_vec(e):
                wait(e, "vector", "act")
                inc(e.tensor_tensor(wq_l[:], wq_f[:], wq_h[:], ALU.subtract),
                    "vec")
            go("vector", p0_vec)

            def p0_pool2(e, sl=sl):
                wait(e, "gpsimd", "vec")
                inc(e.dma_start(out=wqh_flat[:, sl], in_=wq_h[:]), "dpool", 16)
                inc(e.dma_start(out=wql_flat[:, sl], in_=wq_l[:]), "dpool", 16)
            go("gpsimd", p0_pool2)

        def p0_vec2(e):
            wait(e, "vector", "dsync")
            inc(e.tensor_scalar_add(wcol_t[:, :, 6:7], wcol_t[:, :, 6:7], 1.0),
                "vec")
        go("vector", p0_vec2)
        join()

    with contextlib.ExitStack() as s1:
        sbl = lambda n, sh, dt: s1.enter_context(nc.sbuf_tensor(n, sh, dt))
        nst = sbl("nst", [128, 32], F32)
        sqst = sbl("sqst", [128, 32], F32)
        istd = sbl("istd", [128, 32], F32)
        bist = sbl("bist", [128, 32], F32)
        tst = sbl("tst", [128, 32], F32)
        sq_scr = sbl("sq_scr", [128, 1024], F32)
        xtm_t = [sbl(f"xtm_t{i}", [128, 1024], F32) for i in range(2)]
        x2f = sbl("x2f", [128, 1024], F32)
        x2h = sbl("x2h", [128, 1024], BF16)
        x2l = sbl("x2l", [128, 1024], BF16)
        hist = {}
        for i in range(32):
            dxs = "dx0" if i % 2 == 0 else "dx1"

            def f_load(e, i=i, dxs=dxs):
                if i >= 2:
                    wait(e, "sync", "act", hist[i - 2][0])
                    wait(e, "sync", "vec", hist[i - 2][1])
                inc(e.dma_start(out=xtm_t[i % 2][:], in_=xtm[ts(i, 128), :]),
                    dxs, 16)
            go("sync", f_load)
            dv = mk.led[dxs]

            def f_vec(e, i=i, dv=dv, dxs=dxs):
                wait(e, "vector", dxs, dv)
                inc(e.reduce_sum(nst[:, i:i + 1], xtm_t[i % 2][:], axis=AX.X,
                                 negate=True), "vec")
            go("vector", f_vec)

            def f_act(e, i=i, dv=dv, dxs=dxs):
                wait(e, "scalar", dxs, dv)
                wait(e, "scalar", "act")
                inc(e.activation(sq_scr[:], xtm_t[i % 2][:], AF.Square,
                                 bias=0.0, scale=1.0,
                                 accum_out=sqst[:, i:i + 1]), "act")
            go("scalar", f_act)
            hist[i] = (mk.led["act"], mk.led["vec"])

        varr = sbl("varr", [128, 32], F32)
        sdev = sbl("sdev", [128, 32], F32)

        def f_stats(e):
            wait(e, "vector", "act")
            e.drain()
            e.tensor_scalar_mul(nst[:], nst[:], 1.0 / D)
            e.drain()
            e.tensor_tensor(tst[:], nst[:], nst[:], ALU.mult)
            e.tensor_scalar_mul(sdev[:], sqst[:], 1.0 / D)
            e.drain()
            e.tensor_tensor(varr[:], sdev[:], tst[:], ALU.subtract)
            e.drain()
            inc(e.tensor_scalar_add(varr[:], varr[:], 1e-5), "vec")
        go("vector", f_stats)

        def f_sqrt(e):
            wait(e, "scalar", "vec")
            inc(e.activation(sdev[:], varr[:], AF.Sqrt, bias=0.0, scale=1.0),
                "act")
        go("scalar", f_sqrt)

        def f_recip(e):
            wait(e, "vector", "act")
            e.reciprocal(istd[:], sdev[:])
            e.drain()
            e.tensor_tensor(bist[:], nst[:], istd[:], ALU.mult)
            inc(e.drain(), "vec")
        go("vector", f_recip)
        tap("istd", lambda: istd[:])
        tap("bist", lambda: bist[:])

        hist2 = {}
        for i in range(32):
            dxs = "dx0" if i % 2 == 0 else "dx1"

            def f_load2(e, i=i, dxs=dxs):
                if i >= 2:
                    wait(e, "sync", "vec", hist2[i - 2])
                else:
                    wait(e, "sync", "vec")
                    wait(e, "sync", "act")
                inc(e.dma_start(out=xtm_t[i % 2][:], in_=xtm[ts(i, 128), :]),
                    dxs, 16)
            go("sync", f_load2)
            dv = mk.led[dxs]

            def f_ap(e, i=i, dv=dv, dxs=dxs):
                wait(e, "vector", dxs, dv)
                wait(e, "vector", "act")
                inc(e.tensor_scalar(x2f[:], xtm_t[i % 2][:], istd[:, i:i + 1],
                                    bist[:, i:i + 1], ALU.mult, ALU.add), "vec")
            go("vector", f_ap)
            vv = mk.led["vec"]
            hist2[i] = vv

            def f_hi(e, vv=vv):
                wait(e, "scalar", "vec", vv)
                wait(e, "scalar", "pe")
                inc(e.copy(x2h[:], x2f[:]), "act")
            go("scalar", f_hi)
            av = mk.led["act"]

            def f_lo(e, av=av):
                wait(e, "vector", "act", av)
                wait(e, "vector", "pe")
                e.drain()
                inc(e.tensor_tensor(x2l[:], x2f[:], x2h[:], ALU.subtract), "vec")
            go("vector", f_lo)

            for k in range(8):
                def f_tp(e, k=k):
                    wait(e, "tensor", "vec")
                    wait(e, "tensor", "act")
                    e.transpose(Pb[k % 2][:, 0:128], x2h[:, ts(k, 128)],
                                identb[:])
                    inc(e.transpose(Pb[k % 2][:, 128:256], x2l[:, ts(k, 128)],
                                    identb[:]), "pe")
                go("tensor", f_tp)
                pv = mk.led["pe"]

                def f_tpc(e, k=k, i=i, pv=pv):
                    wait(e, "vector", "pe", pv)
                    e.tensor_copy(xnT_hi[:, k, ts(i, 128)], Pb[k % 2][:, 0:128])
                    inc(e.tensor_copy(xnT_lo[:, k, ts(i, 128)],
                                      Pb[k % 2][:, 128:256]), "vec")
                go("vector", f_tpc)

        xm_f = sbl("xm_f", [128, 8, M], F32)
        xm_t = sbl("xm_t", [128, M], F32)
        xm_t2 = sbl("xm_t2", [128, M], F32)
        for c in range(8):
            def f_xm(e, c=c):
                wait(e, "vector", "vec")
                e.tensor_reduce(xm_t[:],
                                xnT_hi[:, c, :].rearrange("p (m l) -> p m l", l=16),
                                axis=AX.X, op=ALU.add)
                e.tensor_reduce(xm_t2[:],
                                xnT_lo[:, c, :].rearrange("p (m l) -> p m l", l=16),
                                axis=AX.X, op=ALU.add)
                e.drain()
                inc(e.tensor_tensor(xm_f[:, c, :], xm_t[:], xm_t2[:],
                                    ALU.add), "vec")
            go("vector", f_xm)

        def f_xm_a(e):
            wait(e, "scalar", "vec")
            inc(e.activation(xm_hi[:], xm_f[:], AF.Copy, bias=0.0,
                             scale=1.0 / 16), "act")
        go("scalar", f_xm_a)

        def f_xm_v(e):
            wait(e, "vector", "act")
            inc(e.scalar_tensor_tensor(xm_lo[:], xm_f[:], 1.0 / 16, xm_hi[:],
                                       ALU.mult, ALU.subtract), "vec")
        go("vector", f_xm_v)
        join()
        tap("xnT_hi", lambda: xnT_hi[:])
        tap("xnT_lo", lambda: xnT_lo[:])
        tap("xm_hi", lambda: xm_hi[:])
        tap("xm_lo", lambda: xm_lo[:])
    if stop <= 1:
        def f_fin1(e):
            for s in SEMS:
                wait(e, "sync", s)
        go("sync", f_fin1)
        sc2.close()
        es.close()
        return nc

    kqT_hi = sc("kqT_hi", [64, N], BF16)
    kqT_lo = sc("kqT_lo", [64, N], BF16)
    qlT_hi = sc("qlT_hi", [64, M], BF16)
    qlT_lo = sc("qlT_lo", [64, M], BF16)
    klT_hi = sc("klT_hi", [64, M], BF16)
    klT_lo = sc("klT_lo", [64, M], BF16)
    zw_hi = sc("zw_hi", [128, 2, DH + 1], BF16)
    zw_lo = sc("zw_lo", [128, 2, DH + 1], BF16)
    whe_hi = sc("whe_hi", [128, 8, 192], BF16)
    whe_lo = sc("whe_lo", [128, 8, 192], BF16)
    wcol_h = sc("wcol_h", [DH, KC], F32)
    NSPL = (len(wsplit_jobs) + HL - 1) // HL

    for hl in range(HL):
        def f_wl(e, hl=hl):
            wait(e, "sync", "pe")
            wait(e, "sync", "dpool")
            inc(e.dma_start(out=whe_hi[:],
                            in_=wq_hi_d[hl].rearrange("(c p) k -> p c k", p=128)),
                "dw", 16)
            inc(e.dma_start(out=whe_lo[:],
                            in_=wq_lo_d[hl].rearrange("(c p) k -> p c k", p=128)),
                "dw", 16)
        go("sync", f_wl)
        whe_dv = mk.led["dw"]

        def f_wc(e, hl=hl):
            wait(e, "vector", "vec")
            inc(e.tensor_copy(wcol_h[:], wcol_t[ts(hl % 2, 64)][:, hl // 2, :]),
                "vec")
        go("vector", f_wc)

        with contextlib.ExitStack() as sA:
            sba = lambda n, sh, dt, hl=hl: sA.enter_context(
                nc.sbuf_tensor(f"{n}_{hl}", sh, dt))
            vsc_hi = [sba(f"vsc_hi{i}", [64, 512], BF16) for i in range(2)]
            vsc_lo = [sba(f"vsc_lo{i}", [64, 512], BF16) for i in range(2)]
            v_tm = sba("v_tm", [128, 32, 132], BF16)
            e3f = [sba(f"e3f{i}", [128, M], F32) for i in range(2)]
            e3h = [sba(f"e3h{i}", [128, M], BF16) for i in range(2)]
            e3l = [sba(f"e3l{i}", [128, M], BF16) for i in range(2)]
            aT = sba("aT", [128, 2, M], F32)
            zA = sba("zA", [128, 2, M], F32)
            zTA = sba("zTA", [128, 2, M], F32)
            zB = sba("zB", [128, 2, M], F32)
            zTB = sba("zTB", [128, 2, M], F32)
            uu = sba("uu", [128, 2, M], F32)
            uuT = sba("uuT", [128, 2, M], F32)
            xw = sba("xw", [128, 2, M], F32)
            srow = sba("srow", [1, 520], F32)
            srow_b = sba("srow_b", [1, 512], BF16)
            stat2 = sba("stat2", [128, 8], F32)
            mnb2 = sba("mnb2", [128, 2], BF16)
            mn3 = sba("mn3", [128, 2], F32)
            mr3t = sba("mr3t", [128, 2, 8], F32)
            mrow3 = sba("mrow3", [1, M], BF16)
            w3s = sba("w3s", [128, 2, 132], F32)

            kvh = {}
            for tc in range(8):
                pr = kvh.get(tc - 2, {})

                def f_mm(e, tc=tc, pr=pr):
                    wait(e, "tensor", "dw", whe_dv)
                    if tc < 2:
                        wait(e, "tensor", "vec")
                        wait(e, "tensor", "act")
                    else:
                        wait(e, "tensor", "act", pr.get("eva", 0))
                        wait(e, "tensor", "vec", pr.get("evv", 0))
                    ii = mm3(e, P[tc % 2][:, 0:512],
                             lambda k, w: (whe_hi if w == "hi" else whe_lo)[:, k, 64:192],
                             lambda k, w: (xnT_hi if w == "hi" else xnT_lo)[:, k, ts(tc, 512)],
                             range(8))
                    inc(ii, "pe")
                go("tensor", f_mm)
                pv = mk.led["pe"]
                kvh.setdefault(tc, {})["mm"] = pv

                def f_ev_a(e, tc=tc, pv=pv, pr=pr):
                    wait(e, "scalar", "pe", pv)
                    e.copy(kqT_hi[:, ts(tc, 512)], P[tc % 2][0:64, 0:512])
                    inc(e.copy(vsc_hi[tc % 2][:], P[tc % 2][64:128, 0:512]),
                        "act")
                go("scalar", f_ev_a)
                av = mk.led["act"]
                kvh[tc]["eva"] = av

                def f_ev_v(e, tc=tc, pv=pv, av=av):
                    wait(e, "vector", "pe", pv)
                    wait(e, "vector", "act", av)
                    e.tensor_tensor(kqT_lo[:, ts(tc, 512)], P[tc % 2][0:64, 0:512],
                                    kqT_hi[:, ts(tc, 512)], ALU.subtract)
                    inc(e.tensor_tensor(vsc_lo[tc % 2][:], P[tc % 2][64:128, 0:512],
                                        vsc_hi[tc % 2][:], ALU.subtract), "vec")
                go("vector", f_ev_v)
                vv = mk.led["vec"]
                kvh[tc]["evv"] = vv

                def f_tp(e, tc=tc, vv=vv, pr=pr):
                    wait(e, "tensor", "vec", vv)
                    wait(e, "tensor", "vec", pr.get("tpc", 0))
                    for sub in range(4):
                        e.transpose(Pb[0][:, 256 * (tc % 2) + 64 * sub:
                                          256 * (tc % 2) + 64 * sub + 64],
                                    vsc_hi[tc % 2][:, ts(sub, 128)],
                                    identb[0:64, 0:64])
                    for sub in range(4):
                        ii = e.transpose(Pb[1][:, 256 * (tc % 2) + 64 * sub:
                                               256 * (tc % 2) + 64 * sub + 64],
                                         vsc_lo[tc % 2][:, ts(sub, 128)],
                                         identb[0:64, 0:64])
                    inc(ii, "pe")
                go("tensor", f_tp)
                pv2 = mk.led["pe"]

                def f_tpc(e, tc=tc, pv2=pv2):
                    wait(e, "vector", "pe", pv2)
                    for sub in range(4):
                        e.tensor_copy(v_tm[:, tc * 4 + sub, 0:64],
                                      Pb[0][:, 256 * (tc % 2) + 64 * sub:
                                            256 * (tc % 2) + 64 * sub + 64])
                    for sub in range(4):
                        ii = e.tensor_copy(v_tm[:, tc * 4 + sub, 64:128],
                                           Pb[1][:, 256 * (tc % 2) + 64 * sub:
                                                 256 * (tc % 2) + 64 * sub + 64])
                    inc(ii, "vec")
                go("vector", f_tpc)
                kvh[tc]["tpc"] = mk.led["vec"]
            go("gpsimd", lambda e: inc(e.memset(v_tm[:, :, 128:129], 1.0), "pool"))
            onev = mk.led["pool"]
            if hl == 0:
                tap("kT_hi", lambda: kqT_hi[:])
                tap("kT_lo", lambda: kqT_lo[:])
                tap("v_tm", lambda: v_tm[:])

            def f_lm(e):
                wait(e, "tensor", "vec")
                mm3(e, P[4][0:64, 0:256],
                    lambda k, w: (whe_hi if w == "hi" else whe_lo)[:, k, 0:64],
                    lambda k, w: (xm_hi if w == "hi" else xm_lo)[:, k, :],
                    range(8))
                ii = mm3(e, P[4][0:64, 256:512],
                         lambda k, w: (whe_hi if w == "hi" else whe_lo)[:, k, 64:128],
                         lambda k, w: (xm_hi if w == "hi" else xm_lo)[:, k, :],
                         range(8))
                inc(ii, "pe")
            go("tensor", f_lm)
            pv = mk.led["pe"]

            def f_lm_a(e, pv=pv):
                wait(e, "scalar", "pe", pv)
                e.activation(qlT_hi[:], P[4][0:64, 0:256], AF.Copy, bias=0.0,
                             scale=SCALE)
                inc(e.copy(klT_hi[:], P[4][0:64, 256:512]), "act")
            go("scalar", f_lm_a)
            av = mk.led["act"]

            def f_lm_v(e, pv=pv, av=av):
                wait(e, "vector", "pe", pv)
                wait(e, "vector", "act", av)
                e.scalar_tensor_tensor(qlT_lo[:], P[4][0:64, 0:256], SCALE,
                                       qlT_hi[:], ALU.mult, ALU.subtract)
                inc(e.tensor_tensor(klT_lo[:], P[4][0:64, 256:512], klT_hi[:],
                                    ALU.subtract), "vec")
            go("vector", f_lm_v)
            if hl == 0:
                tap("qlT_hi", lambda: qlT_hi[:])
                tap("qlT_lo", lambda: qlT_lo[:])
                tap("klT_hi", lambda: klT_hi[:])

            def f_s2(e):
                wait(e, "tensor", "vec")
                for h2 in range(2):
                    ii = mm3(e, P[4][:, ts(h2, 256)],
                             lambda k, w, h2=h2: (qlT_hi if w == "hi" else qlT_lo)[:, ts(h2, 128)],
                             lambda k, w: (klT_hi if w == "hi" else klT_lo)[:],
                             (0,))
                inc(ii, "pe")
            go("tensor", f_s2)
            pv = mk.led["pe"]

            def f_s2v(e, pv=pv):
                wait(e, "vector", "pe", pv)
                e.reduce_max(stat2[:, 0:1], P[4][:, 0:256], axis=AX.X, negate=True)
                inc(e.reduce_max(stat2[:, 1:2], P[4][:, 256:512], axis=AX.X,
                                 negate=True), "vec")
            go("vector", f_s2v)

            def f_s2r(e):
                wait(e, "scalar", "vec")
                inc(e.copy(mnb2[:], stat2[:, 0:2]), "act")
            go("scalar", f_s2r)

            def f_s2r2(e):
                wait(e, "vector", "act")
                inc(e.tensor_copy(stat2[:, 0:2], mnb2[:]), "vec")
            go("vector", f_s2r2)

            def f_s2e(e):
                wait(e, "scalar", "vec")
                e.activation(e3f[0][:], P[4][:, 0:256], AF.Exp,
                             bias=stat2[:, 0:1], scale=1.0,
                             accum_out=stat2[:, 2:3])
                inc(e.activation(e3f[1][:], P[4][:, 256:512], AF.Exp,
                                 bias=stat2[:, 1:2], scale=1.0,
                                 accum_out=stat2[:, 3:4]), "act")
            go("scalar", f_s2e)

            def f_s2i(e):
                wait(e, "vector", "act")
                inc(e.reciprocal(stat2[:, 4:6], stat2[:, 2:4]), "vec")
            go("vector", f_s2i)

            def f_s2t(e):
                wait(e, "tensor", "vec")
                e.transpose(P[2][0:1, 0:128], stat2[:, 0:1], identf[:])
                e.transpose(P[2][0:1, 128:256], stat2[:, 1:2], identf[:])
                e.transpose(P[2][0:1, 256:384], stat2[:, 4:5], identf[:])
                inc(e.transpose(P[2][0:1, 384:512], stat2[:, 5:6], identf[:]),
                    "pe")
            go("tensor", f_s2t)
            pv = mk.led["pe"]

            def f_s2tc(e, pv=pv):
                wait(e, "vector", "pe", pv)
                inc(e.tensor_copy(srow[0:1, 0:512], P[2][0:1, 0:512]), "vec")
            go("vector", f_s2tc)

            def f_s2tb(e):
                wait(e, "scalar", "vec")
                inc(e.copy(srow_b[0:1, 0:256], srow[0:1, 0:256]), "act")
            go("scalar", f_s2tb)

            def f_s2t2(e):
                wait(e, "tensor", "act")
                for h2 in range(2):
                    mm3(e, P[5][:, ts(h2, 256)],
                        lambda k, w, h2=h2: (klT_hi if w == "hi" else klT_lo)[:, ts(h2, 128)],
                        lambda k, w: (qlT_hi if w == "hi" else qlT_lo)[:],
                        (0,), stop=False)
                    e.matmul(P[5][:, ts(h2, 256)], ones_bf[:],
                             srow_b[0:1, 0:256], start=False, stop=True)
                inc(e.matmul(P[3][:, 0:256], ones_f[:], srow[0:1, 256:512],
                             start=True, stop=True), "pe")
            go("tensor", f_s2t2)
            pv = mk.led["pe"]

            def f_s2t3(e, pv=pv):
                wait(e, "scalar", "pe", pv)
                e.activation(e3f[0][:], P[5][:, 0:256], AF.Exp, bias=0.0,
                             scale=1.0)
                inc(e.activation(e3f[1][:], P[5][:, 256:512], AF.Exp,
                                 bias=0.0, scale=1.0), "act")
            go("scalar", f_s2t3)

            def f_s2t4(e):
                wait(e, "vector", "act")
                e.tensor_copy(xw[:, 0, :], P[3][:, 0:256])
                e.drain()
                e.tensor_tensor(aT[:, 0, :], e3f[0][:], xw[:, 0, :], ALU.mult)
                inc(e.tensor_tensor(aT[:, 1, :], e3f[1][:], xw[:, 0, :],
                                    ALU.mult), "vec")
            go("vector", f_s2t4)

            def f_z0a(e):
                wait(e, "vector", "vec")
                e.reduce_sum(stat2[:, 0:1], aT[:, 0, :], axis=AX.X)
                inc(e.reduce_sum(stat2[:, 1:2], aT[:, 1, :], axis=AX.X), "vec")
            go("vector", f_z0a)

            def f_z0b(e):
                wait(e, "tensor", "vec")
                e.transpose(P[2][0:1, 0:128], stat2[:, 0:1], identf[:])
                e.transpose(P[2][0:1, 128:256], stat2[:, 1:2], identf[:])
                for k in range(2):
                    ii = e.matmul(P[2][0:1, 256:512], onescol_f[:], aT[:, k, :],
                                  start=(k == 0), stop=(k == 1))
                inc(ii, "pe")
            go("tensor", f_z0b)
            pv = mk.led["pe"]

            def f_z0c(e, pv=pv):
                wait(e, "vector", "pe", pv)
                e.tensor_copy(srow[0:1, 0:512], P[2][0:1, 0:512])
                e.drain()
                e.reduce_max(srow[0:1, 512:513], srow[0:1, 0:256], axis=AX.X)
                e.reduce_max(srow[0:1, 513:514], srow[0:1, 256:512], axis=AX.X)
                e.drain()
                e.tensor_tensor(srow[0:1, 514:515], srow[0:1, 512:513],
                                srow[0:1, 513:514], ALU.mult)
                e.drain()
                inc(e.reciprocal(srow[0:1, 515:516], srow[0:1, 514:515]), "vec")
            go("vector", f_z0c)

            def f_z0d(e):
                wait(e, "tensor", "vec")
                inc(e.matmul(P[2][:, 0:1], ones_f[:], srow[0:1, 515:516],
                             start=True, stop=True), "pe")
            go("tensor", f_z0d)
            pv = mk.led["pe"]

            def f_z0e(e, pv=pv):
                wait(e, "vector", "pe", pv)
                e.tensor_copy(stat2[:, 6:7], P[2][:, 0:1])
                e.drain()
                e.tensor_scalar(zA[:, 0, :], aT[:, 0, :], stat2[:, 6:7], None,
                                ALU.mult)
                inc(e.tensor_scalar(zA[:, 1, :], aT[:, 1, :], stat2[:, 6:7],
                                    None, ALU.mult), "vec")
            go("vector", f_z0e)

            def f_z0f(e):
                wait(e, "tensor", "vec")
                e.transpose(P[2][:, 0:128], zA[:, 0, 0:128], identf[:])
                e.transpose(P[2][:, 128:256], zA[:, 1, 0:128], identf[:])
                e.transpose(P[3][:, 0:128], zA[:, 0, 128:256], identf[:])
                inc(e.transpose(P[3][:, 128:256], zA[:, 1, 128:256], identf[:]),
                    "pe")
            go("tensor", f_z0f)
            pv = mk.led["pe"]

            def f_z0g(e, pv=pv):
                wait(e, "vector", "pe", pv)
                e.tensor_copy(zTA[:, 0, 0:128], P[2][:, 0:128])
                e.tensor_copy(zTA[:, 0, 128:256], P[2][:, 128:256])
                e.tensor_copy(zTA[:, 1, 0:128], P[3][:, 0:128])
                inc(e.tensor_copy(zTA[:, 1, 128:256], P[3][:, 128:256]), "vec")
            go("vector", f_z0g)
            if hl == 0:
                tap("z0", lambda: zA[:])
                tap("zT0", lambda: zTA[:])

            zc, zTc, zn, zTn = zA, zTA, zB, zTB
            for it in range(ITERS):
                def f_u(e, zc=zc):
                    wait(e, "tensor", "vec")
                    mmf(e, (P[4][:, 0:256], P[4][:, 256:512]), aT, zc)
                    inc(mmf(e, (P[5][:, 0:256], P[5][:, 256:512]), zc, aT), "pe")
                go("tensor", f_u)
                pv = mk.led["pe"]

                def f_ue(e, pv=pv):
                    wait(e, "scalar", "pe", pv)
                    e.copy(uu[:, 0, :], P[4][:, 0:256])
                    e.copy(uu[:, 1, :], P[4][:, 256:512])
                    e.copy(uuT[:, 0, :], P[5][:, 0:256])
                    inc(e.copy(uuT[:, 1, :], P[5][:, 256:512]), "act")
                go("scalar", f_ue)

                def f_p(e):
                    wait(e, "tensor", "act")
                    inc(mmf(e, (P[4][:, 0:256], P[4][:, 256:512]), uuT, uu),
                        "pe")
                go("tensor", f_p)
                pv = mk.led["pe"]

                def f_x1(e, pv=pv):
                    wait(e, "vector", "pe", pv)
                    e.scalar_tensor_tensor(xw[:, 0, :], uu[:, 0, :], 7.0,
                                           P[4][:, 0:256], ALU.mult, ALU.subtract)
                    inc(e.scalar_tensor_tensor(xw[:, 1, :], uu[:, 1, :], 7.0,
                                               P[4][:, 256:512], ALU.mult,
                                               ALU.subtract), "vec")
                go("vector", f_x1)

                def f_r(e):
                    wait(e, "tensor", "vec")
                    inc(mmf(e, (P[4][:, 0:256], P[4][:, 256:512]), uuT, xw),
                        "pe")
                go("tensor", f_r)
                pv = mk.led["pe"]

                def f_x3(e, pv=pv):
                    wait(e, "vector", "pe", pv)
                    e.scalar_tensor_tensor(xw[:, 0, :], uu[:, 0, :], 15.0,
                                           P[4][:, 0:256], ALU.mult, ALU.subtract)
                    inc(e.scalar_tensor_tensor(xw[:, 1, :], uu[:, 1, :], 15.0,
                                               P[4][:, 256:512], ALU.mult,
                                               ALU.subtract), "vec")
                go("vector", f_x3)

                def f_s(e, zc=zc, zTc=zTc):
                    wait(e, "tensor", "vec")
                    mmf(e, (P[4][:, 0:256], P[4][:, 256:512]), zTc, xw)
                    inc(mmf(e, (P[5][:, 0:256], P[5][:, 256:512]), xw, zTc),
                        "pe")
                go("tensor", f_s)
                pv = mk.led["pe"]

                def f_z(e, zc=zc, zTc=zTc, zn=zn, zTn=zTn, pv=pv):
                    wait(e, "vector", "pe", pv)
                    e.scalar_tensor_tensor(zn[:, 0, :], zc[:, 0, :], 13.0,
                                           P[4][:, 0:256], ALU.mult, ALU.subtract)
                    e.scalar_tensor_tensor(zn[:, 1, :], zc[:, 1, :], 13.0,
                                           P[4][:, 256:512], ALU.mult,
                                           ALU.subtract)
                    e.scalar_tensor_tensor(zTn[:, 0, :], zTc[:, 0, :], 13.0,
                                           P[5][:, 0:256], ALU.mult, ALU.subtract)
                    e.scalar_tensor_tensor(zTn[:, 1, :], zTc[:, 1, :], 13.0,
                                           P[5][:, 256:512], ALU.mult,
                                           ALU.subtract)
                    e.drain()
                    e.tensor_scalar_mul(zn[:, 0, :], zn[:, 0, :], 0.25)
                    e.tensor_scalar_mul(zn[:, 1, :], zn[:, 1, :], 0.25)
                    e.tensor_scalar_mul(zTn[:, 0, :], zTn[:, 0, :], 0.25)
                    inc(e.tensor_scalar_mul(zTn[:, 1, :], zTn[:, 1, :], 0.25),
                        "vec")
                go("vector", f_z)
                if hl == 0 and it == 0:
                    tap("u0", lambda: uu[:])
                    tap("uT0", lambda: uuT[:])
                    tap("x3_0", lambda: xw[:])
                    tap("z1", lambda: zn[:])
                    tap("zT1", lambda: zTn[:])
                zc, zn = zn, zc
                zTc, zTn = zTn, zTc
            zT_fin = zTc
            if hl == 0:
                tap("aT", lambda: aT[:])
                tap("zT_fin", lambda: zT_fin[:])
                tap("zA0", lambda: zA[:])
                tap("srow", lambda: srow[:])
                tap("stat2", lambda: stat2[:])
                tap("e2s", lambda: e2s[:])

            for tc in range(8):
                def f_s3(e, tc=tc):
                    wait(e, "tensor", "vec")
                    for h2 in range(2):
                        ii = mm3(e, P[h2][:, 0:512],
                                 lambda k, w, h2=h2: (qlT_hi if w == "hi" else qlT_lo)[:, ts(h2, 128)],
                                 lambda k, w: (kqT_hi if w == "hi" else kqT_lo)[:, ts(tc, 512)],
                                 (0,))
                    inc(ii, "pe")
                go("tensor", f_s3)
                pv = mk.led["pe"]

                def f_s3v(e, tc=tc, pv=pv):
                    wait(e, "vector", "pe", pv)
                    e.reduce_max(mr3t[:, 0, tc:tc + 1], P[0][:, 0:512], axis=AX.X)
                    inc(e.reduce_max(mr3t[:, 1, tc:tc + 1], P[1][:, 0:512],
                                     axis=AX.X), "vec")
                go("vector", f_s3v)

            mnb3 = sba("mnb3", [128, 2], BF16)

            def f_s3m(e):
                wait(e, "vector", "vec")
                e.tensor_reduce(mn3[:], mr3t[:], axis=AX.X, op=ALU.max,
                                negate=True)
                inc(e.drain(), "vec")
            go("vector", f_s3m)

            def f_s3m2(e):
                wait(e, "scalar", "vec")
                inc(e.copy(mnb3[:], mn3[:]), "act")
            go("scalar", f_s3m2)

            def f_s3m3(e):
                wait(e, "vector", "act")
                inc(e.tensor_copy(mn3[:], mnb3[:]), "vec")
            go("vector", f_s3m3)

            def f_s3t(e):
                wait(e, "tensor", "vec")
                wait(e, "tensor", "act")
                e.transpose(Pb[0][0:1, 0:128], mnb3[:, 0:1], identb[:])
                inc(e.transpose(Pb[0][0:1, 128:256], mnb3[:, 1:2], identb[:]),
                    "pe")
            go("tensor", f_s3t)
            pv = mk.led["pe"]

            def f_s3tc(e, pv=pv):
                wait(e, "vector", "pe", pv)
                inc(e.tensor_copy(mrow3[0:1, 0:256], Pb[0][0:1, 0:256]), "vec")
            go("vector", f_s3tc)

            s3h = {}
            for sc in range(32):
                sp = sc % 2
                pr = s3h.get(sc - 2, {})

                def f_v1(e, sc=sc, sp=sp, pr=pr):
                    if sc < 2:
                        wait(e, "tensor", "vec")
                        wait(e, "tensor", "act")
                        wait(e, "tensor", "pool", onev)
                    else:
                        wait(e, "tensor", "act", pr.get("exp", 0))
                    mm3(e, (P[2] if sp == 0 else P[5])[:, 0:256],
                        lambda k, w, sc=sc: (kqT_hi if w == "hi" else kqT_lo)[:, ts(sc, 128)],
                        lambda k, w: (qlT_hi if w == "hi" else qlT_lo)[:],
                        (0,), stop=False)
                    inc(e.matmul((P[2] if sp == 0 else P[5])[:, 0:256], ones_bf[:],
                                 mrow3[0:1, :], start=False, stop=True), "pe")
                go("tensor", f_v1)
                pv = mk.led["pe"]
                s3h.setdefault(sc, {})["mm"] = pv

                def f_v2(e, sc=sc, sp=sp, pv=pv, pr=pr):
                    wait(e, "scalar", "pe", pv)
                    wait(e, "scalar", "vec", pr.get("cast", 0))
                    wait(e, "scalar", "pool", pr.get("sub", 0))
                    inc(e.activation(e3f[sp][:], (P[2] if sp == 0 else P[5])[:, 0:256],
                                     AF.Exp, bias=0.0, scale=1.0), "act")
                go("scalar", f_v2)
                av = mk.led["act"]
                s3h[sc]["exp"] = av

                def f_v2b(e, sc=sc, sp=sp, av=av, pr=pr):
                    wait(e, "vector", "act", av)
                    wait(e, "vector", "pe", pr.get("w3", 0))
                    inc(e.tensor_copy(e3h[sp][:], e3f[sp][:]), "vec")
                go("vector", f_v2b)
                vv = mk.led["vec"]
                s3h[sc]["cast"] = vv

                def f_v3(e, sc=sc, sp=sp, vv=vv, av=av, pr=pr):
                    wait(e, "gpsimd", "act", av)
                    wait(e, "gpsimd", "vec", vv)
                    wait(e, "gpsimd", "pe", pr.get("w3", 0))
                    inc(e.tensor_tensor(e3l[sp][:], e3f[sp][:], e3h[sp][:],
                                        ALU.subtract), "pool")
                go("gpsimd", f_v3)
                plv = mk.led["pool"]
                s3h[sc]["sub"] = plv

                def f_v4(e, sc=sc, sp=sp, vv=vv, plv=plv):
                    wait(e, "tensor", "vec", vv)
                    wait(e, "tensor", "pool", plv)
                    for h2 in range(2):
                        e.matmul(P[3 + h2][:, 0:129], e3h[sp][:, ts(h2, 128)],
                                 v_tm[:, sc, 0:129], start=(sc == 0), stop=False)
                        ii = e.matmul(P[3 + h2][:, 0:129],
                                      e3l[sp][:, ts(h2, 128)], v_tm[:, sc, 0:129],
                                      start=False, stop=(sc == 31))
                    inc(ii, "pe")
                go("tensor", f_v4)
                s3h[sc]["w3"] = mk.led["pe"]

            def f_w3(e):
                wait(e, "vector", "pe")
                e.tensor_copy(w3s[:, 0, 0:129], P[3][:, 0:129])
                e.tensor_copy(w3s[:, 1, 0:129], P[4][:, 0:129])
                e.drain()
                e.tensor_copy(stat2[:, 2:4],
                              w3s[:, :, 128:129].rearrange("p c o -> p (c o)"))
                e.drain()
                e.reciprocal(stat2[:, 0:2], stat2[:, 2:4])
                e.tensor_tensor(xw[:, 0, 0:64], w3s[:, 0, 0:64], w3s[:, 0, 64:128],
                                ALU.add)
                e.tensor_tensor(xw[:, 1, 0:64], w3s[:, 1, 0:64], w3s[:, 1, 64:128],
                                ALU.add)
                e.drain()
                e.tensor_scalar(xw[:, 0, 0:64], xw[:, 0, 0:64], stat2[:, 0:1], None,
                                ALU.mult)
                inc(e.tensor_scalar(xw[:, 1, 0:64], xw[:, 1, 0:64], stat2[:, 1:2],
                                    None, ALU.mult), "vec")
            go("vector", f_w3)

            def f_zw(e, zT_fin=zT_fin):
                wait(e, "tensor", "vec")
                for ot in range(2):
                    for k in range(2):
                        ii = e.matmul(P[4][:, ts(ot, 64)],
                                      zT_fin[:, k, ts(ot, 128)], xw[:, k, 0:64],
                                      start=(k == 0), stop=(k == 1))
                inc(ii, "pe")
            go("tensor", f_zw)
            pv = mk.led["pe"]

            def f_zwe(e, pv=pv):
                wait(e, "scalar", "pe", pv)
                e.copy(zw_hi[:, 0, 0:DH], P[4][:, 0:64])
                inc(e.copy(zw_hi[:, 1, 0:DH], P[4][:, 64:128]), "act")
            go("scalar", f_zwe)
            av = mk.led["act"]

            def f_zwl(e, av=av):
                wait(e, "vector", "act", av)
                e.tensor_tensor(zw_lo[:, 0, 0:DH], P[4][:, 0:64],
                                zw_hi[:, 0, 0:DH], ALU.subtract)
                inc(e.tensor_tensor(zw_lo[:, 1, 0:DH], P[4][:, 64:128],
                                    zw_hi[:, 1, 0:DH], ALU.subtract), "vec")
            go("vector", f_zwl)

            def f_zwo(e):
                e.memset(zw_hi[:, :, DH:DH + 1], 1.0)
                inc(e.memset(zw_lo[:, :, DH:DH + 1], 0.0), "pool")
            go("gpsimd", f_zwo)
            if hl == 0:
                tap("mrow3", lambda: mrow3[:])
                tap("w3v", lambda: xw[:, :, 0:64])
                tap("zw_hi", lambda: zw_hi[:])
                tap("zw_lo", lambda: zw_lo[:])

            emit_wsplit(NSPL // 2)

        with contextlib.ExitStack() as sB:
            sbb = lambda n, sh, dt, hl=hl: sB.enter_context(
                nc.sbuf_tensor(f"{n}_{hl}", sh, dt))
            outc = sbb("outc", [64, N + 12], F32)
            acc = [sbb(f"acc{i}", [64, 512], F32) for i in range(2)]
            e1f = sbb("e1f", [128, 2, 512], F32)
            e1h = sbb("e1h", [128, 2, 512], BF16)
            e1l = sbb("e1l", [128, 2, 512], BF16)
            mn1 = sbb("mn1", [128, 32], F32)
            mn1b = sbb("mn1b", [128, 32], BF16)
            rowm = [sbb(f"rowm{i}", [1, 512], BF16) for i in range(2)]
            rowi = sbb("rowi", [1, 512], F32)
            irs = sbb("irs", [64, 512], F32)

            join()

            for tc in range(8):
                def f_qm(e, tc=tc):
                    wait(e, "tensor", "vec")
                    wait(e, "tensor", "act")
                    ii = mm3(e, P[tc % 2][0:64, 0:512],
                             lambda k, w: (whe_hi if w == "hi" else whe_lo)[:, k, 0:64],
                             lambda k, w: (xnT_hi if w == "hi" else xnT_lo)[:, k, ts(tc, 512)],
                             range(8))
                    inc(ii, "pe")
                go("tensor", f_qm)
                pv = mk.led["pe"]

                def f_qa(e, tc=tc, pv=pv):
                    wait(e, "scalar", "pe", pv)
                    inc(e.activation(kqT_hi[:, ts(tc, 512)], P[tc % 2][0:64, 0:512],
                                     AF.Copy, bias=0.0, scale=SCALE), "act")
                go("scalar", f_qa)
                av = mk.led["act"]

                def f_qv(e, tc=tc, pv=pv, av=av):
                    wait(e, "vector", "pe", pv)
                    wait(e, "vector", "act", av)
                    inc(e.scalar_tensor_tensor(kqT_lo[:, ts(tc, 512)],
                                               P[tc % 2][0:64, 0:512], SCALE,
                                               kqT_hi[:, ts(tc, 512)], ALU.mult,
                                               ALU.subtract), "vec")
                go("vector", f_qv)

            if hl == 0:
                tap("qT_hi", lambda: kqT_hi[:])
                tap("qT_lo", lambda: kqT_lo[:])
            s1h = {}
            for it4 in range(8):
                bks = (P[2], P[5]) if it4 % 2 == 0 else (P[0], P[1])
                pr = s1h.get(it4 - 2, {})

                def f_1s(e, it4=it4, bks=bks, pr=pr):
                    if it4 < 2:
                        wait(e, "tensor", "vec")
                        wait(e, "tensor", "act")
                    else:
                        wait(e, "tensor", "vec", pr.get("red", 0))
                    for sub in range(4):
                        ii = mm3(e, bks[sub // 2][:, ts(sub % 2, 256)],
                                 lambda k, w, it4=it4, sub=sub: (kqT_hi if w == "hi" else kqT_lo)[:, ts(4 * it4 + sub, 128)],
                                 lambda k, w: (klT_hi if w == "hi" else klT_lo)[:],
                                 (0,))
                    inc(ii, "pe")
                go("tensor", f_1s)
                pv = mk.led["pe"]

                def f_1sv(e, it4=it4, bks=bks, pv=pv):
                    wait(e, "vector", "pe", pv)
                    for sub in range(4):
                        ii = e.reduce_max(mn1[:, 4 * it4 + sub:4 * it4 + sub + 1],
                                          bks[sub // 2][:, ts(sub % 2, 256)],
                                          axis=AX.X, negate=True)
                    inc(ii, "vec")
                go("vector", f_1sv)
                s1h.setdefault(it4, {})["red"] = mk.led["vec"]

            def f_1sb(e):
                wait(e, "scalar", "vec")
                inc(e.copy(mn1b[:], mn1[:]), "act")
            go("scalar", f_1sb)

            def f_gz(e):
                wait(e, "vector", "vec")
                e.memset(outc[:, 0:6], 0.0)
                inc(e.memset(outc[:, N + 6:N + 12], 0.0), "vec")
            go("vector", f_gz)

            vch = {}

            def tail1(tc):
                pv_o1 = vch[tc]["cast"]
                plv = vch[tc]["sub"]
                pro5 = vch.get(tc - 1, {}).get("o5", 0)

                def f_o4a(e, tc=tc, pv_o1=pv_o1, plv=plv, pro5=pro5):
                    wait(e, "tensor", "vec", pv_o1)
                    wait(e, "tensor", "pool", plv)
                    wait(e, "tensor", "vec", pro5)
                    i = 0
                    for h2 in range(2):
                        for (zz, ee) in ((zw_hi, e1h), (zw_hi, e1l), (zw_lo, e1h)):
                            ii = e.matmul(P[4][0:65, 0:512], zz[:, h2, :],
                                          ee[:, h2, :], start=(i == 0),
                                          stop=(i == 5))
                            i += 1
                    inc(ii, "pe")
                go("tensor", f_o4a)
                pv = mk.led["pe"]
                vch[tc]["o4a"] = pv

                def f_o4b(e, tc=tc, pv=pv):
                    wait(e, "vector", "pe", pv)
                    inc(e.reciprocal(rowi[0:1, 0:512], P[4][64:65, 0:512]), "vec")
                go("vector", f_o4b)
                vv = mk.led["vec"]

                def f_o4c(e, tc=tc, vv=vv):
                    wait(e, "tensor", "vec", vv)
                    inc(e.matmul(P[3][0:64, 0:512], ones_f[0:1, 0:64],
                                 rowi[0:1, 0:512], start=True, stop=True), "pe")
                go("tensor", f_o4c)
                pv2 = mk.led["pe"]

                def f_o5(e, tc=tc, pv2=pv2):
                    wait(e, "vector", "pe", pv2)
                    e.tensor_copy(irs[:], P[3][0:64, 0:512])
                    e.drain()
                    inc(e.tensor_tensor(outc[:, 6 + tc * 512:6 + (tc + 1) * 512],
                                        P[4][0:64, 0:512], irs[:], ALU.mult),
                        "vec")
                go("vector", f_o5)
                vch[tc]["o5"] = mk.led["vec"]

            for tc in range(8):
                sp = tc % 2
                bks = (P[0], P[1]) if sp == 0 else (P[2], P[5])
                pr = vch.get(tc - 2, {})

                def f_rowt(e, tc=tc, sp=sp, pr=pr):
                    wait(e, "tensor", "act")
                    wait(e, "tensor", "vec", pr.get("rowc", 0))
                    for r4 in range(4):
                        ii = e.transpose(Pb[0][0:1, 512 * sp + 128 * r4:
                                               512 * sp + 128 * r4 + 128],
                                         mn1b[:, 4 * tc + r4:4 * tc + r4 + 1],
                                         identb[:])
                    inc(ii, "pe")
                go("tensor", f_rowt)
                pv = mk.led["pe"]

                def f_rowc(e, tc=tc, sp=sp, pv=pv):
                    wait(e, "vector", "pe", pv)
                    inc(e.tensor_copy(rowm[sp][0:1, 0:512],
                                      Pb[0][0:1, ts(sp, 512)]), "vec")
                go("vector", f_rowc)
                vv = mk.led["vec"]
                vch.setdefault(tc, {})["rowc"] = vv

                def f_o1(e, tc=tc, sp=sp, bks=bks, vv=vv, pr=pr):
                    wait(e, "tensor", "vec", vv)
                    wait(e, "tensor", "act", pr.get("exp", 0))
                    for h2 in range(2):
                        mm3(e, bks[h2][:, 0:512],
                            lambda k, w, h2=h2: (klT_hi if w == "hi" else klT_lo)[:, ts(h2, 128)],
                            lambda k, w, tc=tc: (kqT_hi if w == "hi" else kqT_lo)[:, ts(tc, 512)],
                            (0,), stop=False)
                        ii = e.matmul(bks[h2][:, 0:512], ones_bf[:],
                                      rowm[sp][0:1, 0:512], start=False,
                                      stop=True)
                    inc(ii, "pe")
                go("tensor", f_o1)
                pv = mk.led["pe"]

                if tc >= 1:
                    tail1(tc - 1)
                def f_o2(e, tc=tc, bks=bks, pv=pv, pr=pr):
                    wait(e, "scalar", "pe", pv)
                    wait(e, "scalar", "vec",
                         vch.get(tc - 1, {}).get("cast", 0))
                    wait(e, "scalar", "pool",
                         vch.get(tc - 1, {}).get("sub", 0))
                    e.activation(e1f[:, 0, :], bks[0][:, 0:512], AF.Exp,
                                 bias=0.0, scale=1.0)
                    inc(e.activation(e1f[:, 1, :], bks[1][:, 0:512], AF.Exp,
                                     bias=0.0, scale=1.0), "act")
                go("scalar", f_o2)
                av = mk.led["act"]
                vch[tc]["exp"] = av

                def f_o2b(e, tc=tc, av=av):
                    wait(e, "vector", "act", av)
                    wait(e, "vector", "pe",
                         vch.get(tc - 1, {}).get("o4a", 0))
                    inc(e.tensor_copy(e1h[:], e1f[:]), "vec")
                go("vector", f_o2b)
                vv = mk.led["vec"]
                vch[tc]["cast"] = vv

                def f_o3(e, tc=tc, vv=vv, av=av):
                    wait(e, "gpsimd", "act", av)
                    wait(e, "gpsimd", "vec", vv)
                    wait(e, "gpsimd", "pe",
                         vch.get(tc - 1, {}).get("o4a", 0))
                    inc(e.tensor_tensor(e1l[:], e1f[:], e1h[:], ALU.subtract),
                        "pool")
                go("gpsimd", f_o3)
                vch[tc]["sub"] = mk.led["pool"]

            tail1(7)

            for cc_ in range(8):
                def f_cv(e, cc_=cc_, hl=hl):
                    wait(e, "vector", "vec")
                    wait(e, "vector", "d4")
                    a, b = acc[0], acc[1]
                    e.tensor_scalar(a[:], outc[:, cc_ * 512 + 0:cc_ * 512 + 512],
                                    wcol_h[:, 0:1], None, ALU.mult)
                    for j in range(1, KC):
                        src, dst = (a, b) if j % 2 == 1 else (b, a)
                        e.drain()
                        ii = e.scalar_tensor_tensor(
                            dst[:], outc[:, cc_ * 512 + j:cc_ * 512 + j + 512],
                            wcol_h[:, j:j + 1], src[:], ALU.mult, ALU.add)
                    inc(ii, "vec")
                go("vector", f_cv)
                vv = mk.led["vec"]

                def f_st(e, cc_=cc_, hl=hl, vv=vv):
                    wait(e, "sync", "vec", vv)
                    inc(e.dma_start(out=a2a_in.ap()[cc_, ts(hl, 64), :],
                                    in_=acc[0][:]), "d4", 16)
                go("sync", f_st)

            if hl == 0:
                tap("mn1", lambda: mn1[:])
                tap("outc", lambda: outc[:])
            emit_wsplit(NSPL - NSPL // 2)
            join()

    emit_wsplit(len(wsplit_jobs))
    join()
    if stop <= 2:
        def f_fin2(e):
            for s in SEMS:
                wait(e, "sync", s)
        go("sync", f_fin2)
        sc2.close()
        es.close()
        return nc
    sc2.close()

    def f_cc(e):
        for s in SEMS:
            wait(e, "gpsimd", s)
        with allow_ici_all_to_all():
            inc(e.collective_compute(
                "AllToAll", ALU.bypass, replica_groups=[list(range(8))],
                ins=[a2a_in.ap().opt()], outs=[a2a_out.ap().opt()]), "cc")
    go("gpsimd", f_cc)

    y_tm = sb("y_tm", [128, 16, D], F32)
    s34 = contextlib.ExitStack()
    of_hi = s34.enter_context(nc.sbuf_tensor("of_hi", [128, 8, T], BF16))
    of_lo = s34.enter_context(nc.sbuf_tensor("of_lo", [128, 8, T], BF16))
    with contextlib.ExitStack() as s3:
        of_f = s3.enter_context(nc.sbuf_tensor("of_f", [128, 8, T], F32))

        def f_as(e):
            wait(e, "sync", "cc")
            for r in range(8):
                for q in range(4):
                    ii = e.dma_start(
                        out=of_f[:, 4 * (r % 2) + q, ts(r // 2, 512)],
                        in_=a2a_out.ap()[r, ts(q, 128), :])
                    inc(ii, "dsync", 16)
        go("sync", f_as)
        dv = mk.led["dsync"]

        tap("of_f", lambda: of_f[:])
        for k in range(8):
            def f_sp_a(e, k=k, dv=dv):
                wait(e, "scalar", "dsync", dv)
                inc(e.copy(of_hi[:, k, :], of_f[:, k, :]), "act")
            go("scalar", f_sp_a)
            av = mk.led["act"]

            def f_sp_v(e, k=k, av=av):
                wait(e, "vector", "act", av)
                inc(e.tensor_tensor(of_lo[:, k, :], of_f[:, k, :], of_hi[:, k, :],
                                    ALU.subtract), "vec")
            go("vector", f_sp_v)
        join()

    with contextlib.ExitStack() as s4:
        sbl = lambda n, sh, dt: s4.enter_context(nc.sbuf_tensor(n, sh, dt))
        fch = sbl("fch", [128, 8, D], BF16)
        fcl = sbl("fcl", [128, 8, D], BF16)
        bfc_t = sbl("bfc_t", [128, D], F32)
        brx_t = sbl("brx_t", [128, D], F32)
        xres_t = [sbl(f"xres_t{i}", [128, D], F32) for i in range(2)]

        def f_fl(e):
            wait(e, "sync", "dpool")
            inc(e.dma_start(out=fch[:],
                            in_=fc_hi_d.rearrange("(c p) d -> p c d", p=128)),
                "dsync", 16)
            inc(e.dma_start(out=fcl[:],
                            in_=fc_lo_d.rearrange("(c p) d -> p c d", p=128)),
                "dsync", 16)
            inc(e.dma_start(out=bfc_t[:], in_=bfc_rep[:]), "dsync", 16)
        go("sync", f_fl)
        fdv = mk.led["dsync"]

        hist4 = {}
        for tt in range(16):
            dxs = "dx0" if tt % 2 == 0 else "dx1"

            def f_xl(e, tt=tt, dxs=dxs):
                if tt >= 2:
                    wait(e, "sync", "vec", hist4[tt - 2])
                else:
                    wait(e, "sync", "vec")
                inc(e.dma_start(out=xres_t[tt % 2][:], in_=xres[ts(tt, 128), :]),
                    dxs, 16)
            go("sync", f_xl)
            dv = mk.led[dxs]

            def f_fm(e, tt=tt):
                wait(e, "tensor", "dsync", fdv)
                wait(e, "tensor", "vec")
                for half in range(2):
                    ii = mm3(e, P[2 * (tt % 2) + half][:, 0:512],
                             lambda k, w, tt=tt: (of_hi if w == "hi" else of_lo)[:, k, ts(tt, 128)],
                             lambda k, w, half=half: (fch if w == "hi" else fcl)[:, k, ts(half, 512)],
                             range(8))
                inc(ii, "pe")
            go("tensor", f_fm)
            pv = mk.led["pe"]

            def f_fe(e, tt=tt, pv=pv, dv=dv, dxs=dxs):
                wait(e, "vector", "pe", pv)
                wait(e, "vector", dxs, dv)
                e.tensor_tensor(brx_t[:], bfc_t[:], xres_t[tt % 2][:], ALU.add)
                e.drain()
                for half in range(2):
                    ii = e.tensor_tensor(y_tm[:, tt, ts(half, 512)],
                                         P[2 * (tt % 2) + half][:, 0:512],
                                         brx_t[:, ts(half, 512)], ALU.add)
                inc(ii, "vec")
            go("vector", f_fe)
            hist4[tt] = mk.led["vec"]
        tap("y_fc", lambda: y_tm[:])
        join()
    s34.close()
    join()

    comb = sb("comb", [128, 16, NE], F32)
    gwh = sb("gwh", [128, 8, NE], BF16)
    gwl = sb("gwl", [128, 8, NE], BF16)
    with contextlib.ExitStack() as sg:
        gwf = sg.enter_context(nc.sbuf_tensor("gwf", [128, 8, NE], F32))

        def f_gl(e):
            inc(e.dma_start(out=gwf[:],
                            in_=gate_wT.rearrange("(c p) d -> p c d", p=128)),
                "dsync", 16)
        go("sync", f_gl)

        def f_gc(e):
            wait(e, "scalar", "dsync")
            inc(e.copy(gwh[:], gwf[:]), "act")
        go("scalar", f_gc)

        def f_gv(e):
            wait(e, "vector", "act")
            inc(e.tensor_tensor(gwl[:], gwf[:], gwh[:], ALU.subtract), "vec")
        go("vector", f_gv)
        join()

    be_t = sb("be_t", [128, NE], F32)
    go("sync", lambda e: inc(e.dma_start(out=be_t[:], in_=be_rep[:]), "dsync", 16))

    for half in range(2):
        with contextlib.ExitStack() as s5:
            sbl = lambda n, sh, dt, half=half: s5.enter_context(
                nc.sbuf_tensor(f"{n}_{half}", sh, dt))
            xn2T_hi = sbl("xn2T_hi", [128, 8, 1024], BF16)
            xn2T_lo = sbl("xn2T_lo", [128, 8, 1024], BF16)
            x2f = sbl("x2f", [128, D], F32)
            x2h = sbl("x2h", [128, D], BF16)
            x2l = sbl("x2l", [128, D], BF16)
            st5 = sbl("st5", [128, 8], F32)
            sco = sbl("sco", [128, NE], F32)
            r_t2 = sbl("r_t2", [128, 4], F32)
            r_tmp = sbl("r_tmp", [128, NE], F32)
            r_msk = sbl("r_msk", [128, NE], F32)
            r_cnt = sbl("r_cnt", [128, NE], F32)
            r_eq = sbl("r_eq", [128, NE], F32)
            r_a = sbl("r_a", [128, NE], F32)
            r_b = sbl("r_b", [128, NE], F32)
            rbuf = sbl("rbuf", [128, NE, NE], F32)
            rbuf2 = sbl("rbuf2", [128, NE, NE], F32)

            for ttl in range(8):
                tt = half * 8 + ttl

                def f_l2(e, tt=tt):
                    wait(e, "vector", "act")
                    inc(e.reduce_sum(st5[:, 0:1], y_tm[:, tt, :], axis=AX.X,
                                     negate=True), "vec")
                go("vector", f_l2)
                vv = mk.led["vec"]

                def f_l2a(e, tt=tt, vv=vv):
                    wait(e, "scalar", "vec", vv)
                    inc(e.activation(x2f[:], y_tm[:, tt, :], AF.Square, bias=0.0,
                                     scale=1.0, accum_out=st5[:, 1:2]), "act")
                go("scalar", f_l2a)

                def f_l2v(e, tt=tt):
                    wait(e, "vector", "act")
                    e.drain()
                    e.tensor_scalar_mul(st5[:, 2:3], st5[:, 0:1], 1.0 / D)
                    e.drain()
                    e.tensor_tensor(st5[:, 3:4], st5[:, 2:3], st5[:, 2:3],
                                    ALU.mult)
                    e.tensor_scalar_mul(st5[:, 4:5], st5[:, 1:2], 1.0 / D)
                    e.drain()
                    e.tensor_tensor(st5[:, 0:1], st5[:, 4:5], st5[:, 3:4],
                                    ALU.subtract)
                    e.drain()
                    inc(e.tensor_scalar_add(st5[:, 0:1], st5[:, 0:1], 1e-5),
                        "vec")
                go("vector", f_l2v)

                def f_l2r(e, tt=tt):
                    wait(e, "scalar", "vec")
                    inc(e.activation(st5[:, 5:6], st5[:, 0:1], AF.Sqrt,
                                     bias=0.0, scale=1.0), "act")
                go("scalar", f_l2r)

                def f_l2b(e, tt=tt):
                    wait(e, "vector", "act")
                    e.reciprocal(st5[:, 1:2], st5[:, 5:6])
                    e.drain()
                    e.tensor_tensor(st5[:, 6:7], st5[:, 2:3], st5[:, 1:2],
                                    ALU.mult)
                    inc(e.drain(), "vec")
                go("vector", f_l2b)

                def f_l2x(e, tt=tt):
                    wait(e, "vector", "vec")
                    inc(e.tensor_scalar(x2f[:], y_tm[:, tt, :], st5[:, 1:2],
                                        st5[:, 6:7], ALU.mult, ALU.add), "vec")
                go("vector", f_l2x)

                def f_l2h(e, tt=tt):
                    wait(e, "scalar", "vec")
                    wait(e, "scalar", "pe")
                    inc(e.copy(x2h[:], x2f[:]), "act")
                go("scalar", f_l2h)
                av = mk.led["act"]

                def f_l2l(e, tt=tt, av=av):
                    wait(e, "vector", "act", av)
                    wait(e, "vector", "pe")
                    e.drain()
                    inc(e.tensor_tensor(x2l[:], x2f[:], x2h[:], ALU.subtract),
                        "vec")
                go("vector", f_l2l)
                vv = mk.led["vec"]

                for k in range(8):
                    def f_l2t(e, k=k):
                        wait(e, "tensor", "vec")
                        wait(e, "tensor", "act")
                        e.transpose(Pb[k % 2][:, 0:128], x2h[:, ts(k, 128)],
                                    identb[:])
                        inc(e.transpose(Pb[k % 2][:, 128:256],
                                        x2l[:, ts(k, 128)], identb[:]), "pe")
                    go("tensor", f_l2t)
                    pv = mk.led["pe"]

                    def f_l2tc(e, k=k, ttl=ttl, pv=pv):
                        wait(e, "vector", "pe", pv)
                        e.tensor_copy(xn2T_hi[:, k, ts(ttl, 128)],
                                      Pb[k % 2][:, 0:128])
                        inc(e.tensor_copy(xn2T_lo[:, k, ts(ttl, 128)],
                                          Pb[k % 2][:, 128:256]), "vec")
                    go("vector", f_l2tc)
                vv = mk.led["vec"]

                def f_rl(e, ttl=ttl, vv=vv):
                    wait(e, "tensor", "vec", vv)
                    wait(e, "tensor", "act")
                    ii = mm3(e, P[4][:, 0:NE],
                             lambda k, w, ttl=ttl: (xn2T_hi if w == "hi" else xn2T_lo)[:, k, ts(ttl, 128)],
                             lambda k, w: (gwh if w == "hi" else gwl)[:, k, :],
                             range(8))
                    inc(ii, "pe")
                go("tensor", f_rl)
                pv = mk.led["pe"]

                def f_rs(e, tt=tt, pv=pv):
                    wait(e, "scalar", "pe", pv)
                    inc(e.activation(sco[:], P[4][:, 0:NE], AF.Sigmoid,
                                     bias=0.0, scale=1.0), "act")
                go("scalar", f_rs)

                def f_rt(e, tt=tt):
                    wait(e, "vector", "act")
                    wait(e, "vector", "dsync")
                    e.drain()
                    e.tensor_tensor(r_tmp[:], sco[:], be_t[:], ALU.add)
                    e.drain()
                    g = r_tmp[:].rearrange("p (g i) -> p g i", i=4)
                    pairs = [(a_, b_) for a_ in range(4) for b_ in range(a_ + 1, 4)]
                    for pi, (a_, b_) in enumerate(pairs):
                        e.tensor_tensor(rbuf[:, pi, 0:4], g[:, :, a_],
                                        g[:, :, b_], ALU.add)
                    e.drain()
                    e.tensor_reduce(r_a[:, 0:4],
                                    rbuf[:, 0:6, 0:4].rearrange("p j g -> p g j"),
                                    axis=AX.X, op=ALU.max)
                    e.drain()
                    e.memset(rbuf[:], 0.0)
                    e.drain()
                    for j in range(4):
                        e.tensor_scalar(rbuf[:, j, 0:4], r_a[:, 0:4],
                                        r_a[:, j:j + 1], None, ALU.is_ge)
                        if j < 3:
                            e.tensor_scalar(rbuf[:, 4 + j, j + 1:4],
                                            r_a[:, j + 1:4], r_a[:, j:j + 1],
                                            None, ALU.is_equal)
                    e.drain()
                    e.tensor_reduce(r_b[:, 0:4],
                                    rbuf[:, 0:4, 0:4].rearrange("p j g -> p g j"),
                                    axis=AX.X, op=ALU.add)
                    e.tensor_reduce(r_b[:, 4:8],
                                    rbuf[:, 4:7, 0:4].rearrange("p j g -> p g j"),
                                    axis=AX.X, op=ALU.add)
                    e.drain()
                    e.tensor_tensor(r_a[:, 4:8], r_b[:, 0:4], r_b[:, 4:8],
                                    ALU.subtract)
                    e.drain()
                    e.tensor_scalar(r_b[:, 0:4], r_a[:, 4:8], 3.0, None,
                                    ALU.is_ge)
                    e.drain()
                    for i_ in range(4):
                        e.tensor_copy(r_a[:].rearrange("p (g i) -> p g i", i=4)[:, :, i_],
                                      r_b[:, 0:4])
                    e.drain()
                    e.tensor_tensor(r_msk[:], r_tmp[:], r_a[:], ALU.mult)
                    e.drain()
                    e.memset(rbuf[:], 0.0)
                    e.memset(rbuf2[:], 0.0)
                    e.drain()
                    for j in range(NE):
                        e.tensor_scalar(rbuf[:, j, :], r_msk[:],
                                        r_msk[:, j:j + 1], None, ALU.is_ge)
                        if j < NE - 1:
                            e.tensor_scalar(rbuf2[:, j, j + 1:], r_msk[:, j + 1:],
                                            r_msk[:, j:j + 1], None, ALU.is_equal)
                    e.drain()
                    e.tensor_reduce(r_a[:], rbuf[:].rearrange("p j g -> p g j"),
                                    axis=AX.X, op=ALU.add)
                    e.tensor_reduce(r_b[:], rbuf2[:].rearrange("p j g -> p g j"),
                                    axis=AX.X, op=ALU.add)
                    e.drain()
                    e.tensor_tensor(r_cnt[:], r_a[:], r_b[:], ALU.subtract)
                    e.drain()
                    e.tensor_scalar(r_eq[:], r_cnt[:], 13.0, None, ALU.is_ge)
                    e.drain()
                    e.tensor_tensor(r_a[:], sco[:], r_eq[:], ALU.mult)
                    e.drain()
                    e.reduce_sum(st5[:, 7:8], r_a[:], axis=AX.X)
                    e.drain()
                    e.tensor_scalar_add(st5[:, 7:8], st5[:, 7:8], 1e-20)
                    e.drain()
                    e.reciprocal(st5[:, 3:4], st5[:, 7:8])
                    e.drain()
                    inc(e.tensor_scalar(comb[:, tt, :], r_a[:], st5[:, 3:4],
                                        2.5, ALU.mult, ALU.mult), "vec")
                go("vector", f_rt)

            uph = [sbl(f"uph{i}", [128, 8, MI], BF16) for i in range(2)]
            upl = [sbl(f"upl{i}", [128, 8, MI], BF16) for i in range(2)]
            dnh = [sbl(f"dnh{i}", [128, 4, D], BF16) for i in range(2)]
            dnl = [sbl(f"dnl{i}", [128, 4, D], BF16) for i in range(2)]
            ytmp = x2f
            hrel = sbl("hrel", [128, 4, 512], F32)
            hpro = sbl("hpro", [128, 4, 512], F32)
            hh = [sbl(f"hh{i}", [128, 4, 512], BF16) for i in range(2)]
            hlo = [sbl(f"hlo{i}", [128, 4, 512], BF16) for i in range(2)]
            uhist = {}
            HALF = half
            steps = [(u, tc) for u in range(NU) for tc in range(2)]
            st_hist = {}

            def emit_load(u):
                bi = u % 2
                dus = "du0" if bi == 0 else "du1"

                def f_uw(e, u=u, bi=bi, dus=dus):
                    if u >= 2:
                        wait(e, "sync", "pe", uhist[u - 2])
                    else:
                        wait(e, "sync", "pe")
                    wait(e, "sync", "dpool" if u < 2 else "dpool")
                    inc(e.dma_start(out=uph[bi][:],
                                    in_=up_hi_d[u].rearrange("(c p) m -> p c m", p=128)),
                        dus, 16)
                    inc(e.dma_start(out=upl[bi][:],
                                    in_=up_lo_d[u].rearrange("(c p) m -> p c m", p=128)),
                        dus, 16)
                    inc(e.dma_start(out=dnh[bi][:],
                                    in_=dn_hi_d[u].rearrange("(c p) d -> p c d", p=128)),
                        dus, 16)
                    inc(e.dma_start(out=dnl[bi][:],
                                    in_=dn_lo_d[u].rearrange("(c p) d -> p c d", p=128)),
                        dus, 16)
                go("sync", f_uw)
                return mk.led[dus]

            udvs = {}
            udvs[0] = emit_load(0)
            udvs[1] = emit_load(1)

            def emit_up(s):
                u, tc = steps[s]
                bi = u % 2
                dus = "du0" if bi == 0 else "du1"
                prev_relu = st_hist.get(s - 1, {}).get("relu", 0)

                def f_up(e, u=u, bi=bi, tc=tc, s=s, dus=dus, prev_relu=prev_relu):
                    wait(e, "tensor", dus, udvs[u])
                    if s == 0:
                        wait(e, "tensor", "act")
                        wait(e, "tensor", "vec")
                    else:
                        wait(e, "tensor", "act", prev_relu)
                    for it in range(4):
                        ii = mm3(e, P[it][:, 0:512],
                                 lambda k, w, bi=bi, it=it: (uph[bi] if w == "hi" else upl[bi])[:, k, ts(it, 128)],
                                 lambda k, w, tc=tc: (xn2T_hi if w == "hi" else xn2T_lo)[:, k, ts(tc, 512)],
                                 range(8))
                    inc(ii, "pe")
                go("tensor", f_up)
                st_hist.setdefault(s, {})["up"] = mk.led["pe"]

            def emit_hchain(s):
                u, tc = steps[s]
                sp = s % 2
                prev_prod = st_hist.get(s - 1, {}).get("prod", 0)
                upv = st_hist[s]["up"]

                def f_hr(e, upv=upv, prev_prod=prev_prod):
                    wait(e, "scalar", "pe", upv)
                    wait(e, "scalar", "vec", prev_prod)
                    for it in range(4):
                        ii = e.activation(hrel[:, it, :], P[it][:, 0:512],
                                          AF.Relu, bias=0.0, scale=1.0)
                    inc(ii, "act")
                go("scalar", f_hr)
                st_hist[s]["relu"] = mk.led["act"]
                av = mk.led["act"]
                prev_hh = st_hist.get(s - 1, {}).get("hhdma", 0)
                prev_hlo = st_hist.get(s - 1, {}).get("hlo", 0)

                def f_hp(e, av=av, prev_hh=prev_hh, prev_hlo=prev_hlo):
                    wait(e, "vector", "act", av)
                    wait(e, "vector", "dpool", prev_hh)
                    wait(e, "vector", "pool", prev_hlo)
                    inc(e.scalar_tensor_tensor(hpro[:], hrel[:], 1.0,
                                               hrel[:], ALU.mult, ALU.mult),
                        "vec")
                go("vector", f_hp)
                st_hist[s]["prod"] = mk.led["vec"]
                vv = mk.led["vec"]
                prev_dn = st_hist.get(s - 2, {}).get("dn", 0)

                def f_hc(e, sp=sp, vv=vv, prev_dn=prev_dn):
                    wait(e, "gpsimd", "vec", vv)
                    wait(e, "gpsimd", "pe", prev_dn)
                    inc(e.dma_start(out=hh[sp][:], in_=hpro[:]), "dpool", 16)
                go("gpsimd", f_hc)
                st_hist[s]["hhdma"] = mk.led["dpool"]
                dpv = mk.led["dpool"]

                def f_hl(e, sp=sp, dpv=dpv, prev_dn=prev_dn):
                    wait(e, "gpsimd", "dpool", dpv)
                    inc(e.tensor_tensor(hlo[sp][:], hpro[:], hh[sp][:],
                                        ALU.subtract), "pool")
                go("gpsimd", f_hl)
                st_hist[s]["hlo"] = mk.led["pool"]

            def emit_down(s):
                u, tc = steps[s]
                bi = u % 2
                sp = s % 2
                half = HALF
                plv = st_hist[s]["hlo"]
                dpv = st_hist[s]["hhdma"]
                for tsub in range(4):
                    gt = half * 8 + tc * 4 + tsub

                    def f_dn(e, u=u, bi=bi, tsub=tsub, sp=sp, plv=plv, dpv=dpv):
                        wait(e, "tensor", "pool", plv)
                        wait(e, "tensor", "dpool", dpv)
                        wait(e, "tensor", "vec")
                        for h2 in range(2):
                            ii = mm3(e, P[4 + h2][:, 0:512],
                                     lambda k, w, tsub=tsub, sp=sp: (hh[sp] if w == "hi" else hlo[sp])[:, k, ts(tsub, 128)],
                                     lambda k, w, bi=bi, h2=h2: (dnh[bi] if w == "hi" else dnl[bi])[:, k, ts(h2, 512)],
                                     range(4))
                        inc(ii, "pe")
                    go("tensor", f_dn)
                    pv = mk.led["pe"]

                    def f_ya(e, u=u, gt=gt, pv=pv):
                        wait(e, "vector", "pe", pv)
                        for h2 in range(2):
                            ps = P[4 + h2][:, 0:512]
                            if u < NE:
                                e.scalar_tensor_tensor(
                                    ytmp[:, ts(h2, 512)], ps,
                                    comb[:, gt, u:u + 1],
                                    y_tm[:, gt, ts(h2, 512)], ALU.mult,
                                    ALU.add)
                            else:
                                e.tensor_tensor(
                                    ytmp[:, ts(h2, 512)], ps,
                                    y_tm[:, gt, ts(h2, 512)], ALU.add)
                        e.drain()
                        inc(e.tensor_copy(y_tm[:, gt, :], ytmp[:]), "vec")
                    go("vector", f_ya)
                st_hist[s]["dn"] = mk.led["pe"]
                uhist[u] = mk.led["pe"]

            for s in range(len(steps)):
                u, tc = steps[s]
                emit_up(s)
                emit_hchain(s)
                if s >= 1:
                    emit_down(s - 1)
                if tc == 0 and u >= 1 and u + 1 < NU:
                    udvs[u + 1] = emit_load(u + 1)
            emit_down(len(steps) - 1)
            join()

    def f_out(e):
        for s in SEMS:
            wait(e, "sync", s)
        inc(e.dma_start(out=yout.rearrange("(c p) d -> p c d", p=128),
                        in_=y_tm[:]), "dsync", 16)
        wait(e, "sync", "dsync")
    go("sync", f_out)

    es.close()
    return nc


_NC_CACHE = [None]


def stage_inputs(L0_attn_x, L0_attn_w_qkv, L0_attn_w_conv, L0_attn_w_fc,
                 L0_attn_b_fc, L0_ffn_gate_w, L0_ffn_bias_e, L0_ffn_up_w,
                 L0_ffn_down_w, L0_ffn_shared_up, L0_ffn_shared_down):
    x = np.ascontiguousarray(np.asarray(L0_attn_x, dtype=np.float32))
    w_qkv = np.asarray(L0_attn_w_qkv, dtype=np.float32)
    w_conv = np.asarray(L0_attn_w_conv, dtype=np.float32)[:, 0, :, 0]
    w_fc = np.ascontiguousarray(np.asarray(L0_attn_w_fc, dtype=np.float32))
    b_fc = np.asarray(L0_attn_b_fc, dtype=np.float32)
    gate_w = np.asarray(L0_ffn_gate_w, dtype=np.float32)
    bias_e = np.asarray(L0_ffn_bias_e, dtype=np.float32)
    up_w = np.asarray(L0_ffn_up_w, dtype=np.float32)
    down_w = np.asarray(L0_ffn_down_w, dtype=np.float32)
    shared_up = np.asarray(L0_ffn_shared_up, dtype=np.float32)
    shared_down = np.asarray(L0_ffn_shared_down, dtype=np.float32)

    import ml_dtypes
    identf = np.eye(128, dtype=np.float32)
    identb = np.eye(128, dtype=np.float32).astype(ml_dtypes.bfloat16)
    bfc_rep = np.ascontiguousarray(np.broadcast_to(b_fc, (128, D)))
    be_rep = np.ascontiguousarray(np.broadcast_to(bias_e, (128, NE)))
    gate_wT = np.ascontiguousarray(gate_w.T)
    up_all = np.empty((NU, D, MI), dtype=np.float32)
    dn_all = np.empty((NU, MI, D), dtype=np.float32)
    for e in range(NE):
        up_all[e] = up_w[e].T
        dn_all[e] = down_w[e].T
    for u in range(4):
        up_all[NE + u] = shared_up[ts(u, 512), :].T
        dn_all[NE + u] = shared_down[:, ts(u, 512)].T

    in_maps = []
    for c in range(8):
        b, hg = c // 2, c % 2
        wq = np.empty((HL, D, 3 * DH), dtype=np.float32)
        for hll in range(HL):
            h = 8 * hg + hll
            wq[hll, :, 0:64] = w_qkv[:, ts(h, 64)]
            wq[hll, :, 64:128] = w_qkv[:, 1024 + 64 * h:1024 + 64 * h + 64]
            wq[hll, :, 128:192] = w_qkv[:, 2048 + 64 * h:2048 + 64 * h + 64]
        wcol_c = np.ascontiguousarray(
            np.repeat(w_conv[ts(hg, 8), :], 64, axis=0))
        in_maps.append({
            "xT": np.ascontiguousarray(x[b].T),
            "xtm": np.ascontiguousarray(x[b]),
            "xres": np.ascontiguousarray(
                x[:, ts(c, 512), :].reshape(T, D)),
            "wqkv": wq,
            "wcol": wcol_c,
            "wfc": w_fc,
            "bfc_rep": bfc_rep,
            "gate_wT": gate_wT,
            "be_rep": be_rep,
            "up_wT": up_all,
            "down_wT": dn_all,
            "identf": identf,
            "identb": identb,
        })
    return in_maps


def kernel(**inputs):
    in_maps = stage_inputs(**inputs)
    if _NC_CACHE[0] is None:
        _NC_CACHE[0] = build_program()
    nc = _NC_CACHE[0]
    res = run_bass_kernel_spmd(nc, in_maps, core_ids=list(range(8)))
    out = np.zeros((BT, N, D), dtype=np.float32)
    for c in range(8):
        yc = np.asarray(res.results[c]["y"])
        for b in range(BT):
            out[b, ts(c, 512), :] = yc[ts(b, 512), :]
    return out
